# version 14
# baseline (speedup 1.0000x reference)
import numpy as np
from contextlib import ExitStack
import concourse.bass as bass
import concourse.mybir as mybir
from concourse.alu_op_type import AluOpType as OP
from concourse.bass_utils import run_bass_kernel_spmd

F32, BF16, I32, U8 = mybir.dt.float32, mybir.dt.bfloat16, mybir.dt.int32, mybir.dt.uint8
AF = mybir.ActivationFunctionType
AX = mybir.AxisListType
ENGS = ("tensor", "vector", "scalar", "gpsimd", "sync")
ESEM = {"tensor": "Tn", "vector": "V", "scalar": "S", "gpsimd": "G", "sync": "SY"}

C = 1024
KC = C // 128
H = 4096
E = 4
I_FF = 2816
IC = I_FF // 128
NHEAD, HD = 16, 64
EXP60 = float(np.exp(60.0))
RND_C = 12582912.0
DBG = False
AR_BYTES = 178176

INPUT_SHAPES = [
    ("L0_attn_time_decay", [C]), ("L0_attn_time_first", [C]), ("L0_attn_time_mix", [C]),
    ("L0_attn_Wk", [C, C]), ("L0_attn_Wv", [C, C]), ("L0_attn_Wr", [C, C]), ("L0_attn_Wo", [C, C]),
    ("L0_ffn_w1", [E, C, H]), ("L0_ffn_b1", [E, H]), ("L0_ffn_w2", [E, H, C]), ("L0_ffn_b2", [E, C]),
    ("L0_ffn_wr", [C, E]), ("L0_ffn_br", [E]),
    ("L1_attn_qkv_w", [C, 3 * C]), ("L1_attn_qkv_b", [3 * C]),
    ("L1_attn_out_w", [C, C]), ("L1_attn_out_b", [C]),
    ("L1_attn_w1", [C, H]), ("L1_attn_b1", [H]), ("L1_attn_w2", [H, C]), ("L1_attn_b2", [C]),
    ("L1_attn_g1", [C]), ("L1_attn_be1", [C]), ("L1_attn_g2", [C]), ("L1_attn_be2", [C]),
    ("L1_ffn_w_gate", [I_FF, C]), ("L1_ffn_w_up", [I_FF, C]), ("L1_ffn_w_down", [C, I_FF]),
]


class Prog:

    def __init__(self, nc, stack):
        self.nc = nc
        self.stack = stack
        self.q = {e: [] for e in ENGS}
        self.sems = {}
        self.cnt = {}
        self.last_wait = {}
        self.w = {}
        self.r = {}

    def sem(self, name):
        h = self.stack.enter_context(self.nc.semaphore(name))
        self.sems[name] = h
        self.cnt[name] = 0

    def _wait(self, eng, tok):
        if tok is None:
            return
        s, v = tok
        if s.startswith("@"):
            s = s[1:]
            v = self.cnt[s]
        key = (eng, s)
        if self.last_wait.get(key, 0) >= v:
            return
        self.last_wait[key] = v
        h = self.sems[s]
        self.q[eng].append(lambda e, h=h, v=v: e.wait_ge(h, v))

    def emit(self, eng, build, reads=(), writes=(), dma=None,
             accum_key=None, accum_first=True, accum_last=True):
        for k in reads:
            self._wait(eng, self.w.get(k))
        wkeys = list(writes)
        if accum_key is not None and accum_first:
            wkeys.append(accum_key)
        for k in wkeys:
            self._wait(eng, self.w.get(k))
            for t in self.r.get(k, ()):
                self._wait(eng, t)
        if dma:
            sname = "D_" + dma
            if sname not in self.sems:
                self.sem(sname)
            n = 16
        else:
            sname = ESEM[eng]
            n = 1
        self.cnt[sname] += n
        tok = (("@" + sname) if dma else sname, self.cnt[sname])
        h = self.sems[sname]

        def f(e, build=build, h=h, n=n):
            build(e).then_inc(h, n)

        self.q[eng].append(f)
        for k in reads:
            self.r.setdefault(k, []).append(tok)
        reg = list(writes)
        if accum_key is not None and accum_last:
            reg.append(accum_key)
        for k in reg:
            self.w[k] = tok
            self.r[k] = []
        return tok

    def tok(self, eng):
        s = ESEM[eng]
        return (s, self.cnt[s])

    def setw(self, key, tok):
        self.w[key] = tok
        self.r[key] = []

    def barrier(self):
        toks = [((("@" + s) if s.startswith("D_") else s), self.cnt[s])
                for s in self.sems if self.cnt[s] > 0]
        for eng in ENGS:
            for t in toks:
                self._wait(eng, t)

    def finalize(self, block):
        for name in ENGS:
            fns = self.q[name]
            getattr(block, name)(lambda e, fns=fns: [f(e) for f in fns])


class Arena:

    def __init__(self, t):
        self.t = t
        self.off = 0

    def reset(self):
        self.off = 0

    def take(self, shape, dt):
        assert shape[0] == 128
        elems = int(np.prod(shape[1:]))
        size = {F32: 4, BF16: 2, I32: 4, U8: 1}[dt]
        nbytes = elems * size
        off = (self.off + 63) & ~63
        self.off = off + nbytes
        assert self.off <= AR_BYTES, ("arena overflow", self.off)
        v = self.t.ap()[:, off:off + nbytes].bitcast(dt)
        if len(shape) > 2:
            dims = " ".join("d%d" % i for i in range(1, len(shape)))
            kw = {("d%d" % i): shape[i] for i in range(1, len(shape))}
            v = v.rearrange("p (%s) -> p %s" % (dims, dims), **kw)
        return v


def ap_fm(dram, c_lo, c_hi, n_lo, n_hi):
    return dram.rearrange("(c p) n -> p c n", p=128)[:, c_lo:c_hi, n_lo:n_hi]


def build(T=4096, last_phase="p2c"):
    nc = bass.Bass("TRN2", target_bir_lowering=False)
    st = ExitStack()
    p = Prog(nc, st)

    di = {"x": nc.dram_tensor("x", [T, C], F32, kind="ExternalInput").ap()}
    for n, sh in INPUT_SHAPES:
        di[n] = nc.dram_tensor(n, sh, F32, kind="ExternalInput").ap()

    out_y = nc.dram_tensor("y", [T, C], F32, kind="ExternalOutput").ap()
    s0_h = nc.dram_tensor("s0h", [C, T], BF16, kind="Internal").ap()
    xn_h = nc.dram_tensor("xnh", [C, T], BF16, kind="Internal").ap()
    x1t_h = nc.dram_tensor("x1th", [T, C], F32, kind="Internal").ap()
    x1a_t_h = nc.dram_tensor("x1ath", [T, C], F32, kind="Internal").ap()
    x1a_f_h = nc.dram_tensor("x1afh", [C, T], F32, kind="Internal").ap()
    kvz_h = nc.dram_tensor("kvzh", [128, 8 * 130], F32, kind="Internal").ap()
    xq_h = nc.dram_tensor("xqh", [T, C], BF16, kind="Internal").ap()
    sp_h = nc.dram_tensor("sph", [C, T], F32, kind="Internal").ap()
    s1_h = nc.dram_tensor("s1h", [C, T], F32, kind="Internal").ap()
    wqg_h = nc.dram_tensor("wqgh", [I_FF, C], BF16, kind="Internal").ap()
    wqu_h = nc.dram_tensor("wquh", [I_FF, C], BF16, kind="Internal").ap()
    wqd_h = nc.dram_tensor("wqdh", [C, I_FF], BF16, kind="Internal").ap()

    for s in ("V", "S", "Tn", "G", "SY"):
        p.sem(s)

    sb = {}
    def sbuf(name, shape, dt):
        sb[name] = st.enter_context(nc.sbuf_tensor(name, shape, dt))
        return sb[name]

    ident_b = sbuf("ident_b", [128, 128], BF16)
    ident_f = sbuf("ident_f", [128, 128], F32)
    stg = sbuf("stg", [128, 128], F32)
    io32 = sbuf("io32", [128, 128], I32)
    ones_b = sbuf("ones_b", [128, 128], BF16)
    ones_f = sbuf("ones_f", [128, 128], F32)
    tm_c = sbuf("tm_c", [128, KC], F32)
    dec_c = sbuf("dec_c", [128, KC], F32)
    fw_c = sbuf("fw_c", [128, KC], F32)
    mvar1 = sbuf("mvar1", [128, 2, 32], F32)
    mvar2 = sbuf("mvar2", [128, 2, 32], F32)
    rr1 = sbuf("rr1", [128, 32], F32)
    nmrr1 = sbuf("nmrr1", [128, 32], F32)
    rstd1c = sbuf("rstd1c", [128, 32], F32)
    nmr1c = sbuf("nmr1c", [128, 32], F32)
    g1c = sbuf("g1c", [128, KC], F32)
    be1c = sbuf("be1c", [128, KC], F32)
    perm_f = sbuf("perm_f", [4, 3, 4], F32)
    prow = sbuf("prow", [4, 8], F32)
    ones4f = sbuf("ones4f", [4, 4], F32)
    br4 = sbuf("br4", [4, 1], F32)

    ar_t = sbuf("AR", [128, AR_BYTES], U8)
    A = Arena(ar_t)

    ps = [st.enter_context(nc.psum_tensor("ps%d" % i, [128, 512], F32)) for i in range(6)]
    ptA = st.enter_context(nc.psum_tensor("ptA", [128, 4, 128], BF16))
    ptB = st.enter_context(nc.psum_tensor("ptB", [128, 4, 128], BF16))

    tp_state = {"i": 0}

    def tp_batch2(batches, f32=False, fbanks=(4, 5)):
        for in_aps, dst_ap, rk, wk in batches:
            n = len(in_aps)
            assert n <= 4
            wid = in_aps[0].shape[0]
            if f32:
                b = fbanks[tp_state["i"] % 2]
                bank = ps[b]
                bank3 = bank.rearrange("p (a o) -> p a o", o=128)
                run = bank3[:, 0:n, 0:wid] if wid != 128 else bank[:, 0:128 * n]
                slots = [bank[:, 128 * j:128 * (j + 1)] for j in range(n)]
                bkey = "B%d" % b
                ikey, iap = "identf", ident_f[:, :]
            else:
                bno = tp_state["i"] % 2
                bankt = (ptA, ptB)[bno]
                run = (bankt[:, 0:n, 0:wid] if wid != 128
                       else bankt.rearrange("p a o -> p (a o)")[:, 0:128 * n])
                slots = [bankt[:, j, :] for j in range(n)]
                bkey = ("BTA", "BTB")[bno]
                ikey, iap = "ident", ident_b[:, :]
            tp_state["i"] += 1
            for j, in_ap in enumerate(in_aps):
                pp = in_ap.shape[0]
                ff = int(np.prod(in_ap.shape[1:]))
                s_ = slots[j][0:ff, 0:pp]
                iap2 = iap[0:pp, 0:pp]
                p.emit("tensor",
                       lambda e, in_ap=in_ap, s_=s_, iap2=iap2: e.transpose(s_, in_ap, iap2),
                       reads=list(rk) + [ikey], accum_key=bkey,
                       accum_first=(j == 0), accum_last=(j == n - 1))
            p.emit("vector",
                   lambda e, dst_ap=dst_ap, run=run: e.tensor_copy(dst_ap, run),
                   reads=[bkey], writes=wk)

    def tp_batch(items, f32=False, fbanks=(4, 5)):
        idx = 0
        while idx < len(items):
            grp = items[idx:idx + 4]
            if f32:
                b = fbanks[tp_state["i"] % 2]
                slots = [ps[b][:, 128 * j:128 * (j + 1)] for j in range(4)]
                bkey = "B%d" % b
                ikey, iap = "identf", ident_f[:, :]
            else:
                bno = tp_state["i"] % 2
                bank = (ptA, ptB)[bno]
                slots = [bank[:, j, :] for j in range(4)]
                bkey = ("BTA", "BTB")[bno]
                ikey, iap = "ident", ident_b[:, :]
            tp_state["i"] += 1
            for j, (in_ap, out_ap, rk, wk) in enumerate(grp):
                pp, ff = in_ap.shape[0], int(np.prod(in_ap.shape[1:]))
                s_ = slots[j][0:ff, 0:pp]
                iap2 = iap[0:pp, 0:pp]
                p.emit("tensor",
                       lambda e, in_ap=in_ap, s_=s_, iap2=iap2: e.transpose(s_, in_ap, iap2),
                       reads=list(rk) + [ikey], accum_key=bkey,
                       accum_first=(j == 0), accum_last=(j == len(grp) - 1))
            for j, (in_ap, out_ap, rk, wk) in enumerate(grp):
                pp, ff = in_ap.shape[0], int(np.prod(in_ap.shape[1:]))
                s_ = slots[j][0:ff, 0:pp]
                p.emit("vector",
                       lambda e, out_ap=out_ap, s_=s_: e.tensor_copy(out_ap, s_),
                       reads=[bkey], writes=wk)
            idx += 4

    def mm(eng_key, out_ap, lhsT_ap, rhs_ap, rkeys, first, last, isa_start=None):
        s_ = first if isa_start is None else isa_start
        p.emit("tensor", lambda e: e.matmul(out_ap, lhsT_ap, rhs_ap, start=s_, stop=last,
                                            skip_group_check=True),
               reads=rkeys, accum_key=eng_key, accum_first=first, accum_last=last)

    p.emit("gpsimd", lambda e: e.memset(ones_b[:, :], 1.0), writes=["ones"])
    p.emit("gpsimd", lambda e: e.memset(ones_f[:, :], 1.0), writes=["onesf"])
    p.emit("gpsimd", lambda e: e.memset(ones4f[:, :], 1.0), writes=["ones4"])
    for r in range(3):
        p.emit("gpsimd", lambda e, r=r: e.iota(io32[:, 0:4], pattern=[[-1, 4]], base=0,
                                               channel_multiplier=1), writes=["io32"])
        p.emit("vector", lambda e, r=r: e.tensor_scalar(
            out=prow[:, 0:4], in0=io32[0:4, 0:4], scalar1=float(r + 1), scalar2=None,
            op0=OP.subtract, op1=OP.bypass), reads=["io32"], writes=["ptmp"])
        p.emit("vector", lambda e, r=r: e.tensor_scalar(
            out=prow[:, 4:8], in0=prow[:, 0:4], scalar1=-4.0, scalar2=None,
            op0=OP.is_equal, op1=OP.bypass), reads=["ptmp"], writes=["p4"])
        p.emit("vector", lambda e, r=r: e.tensor_scalar(
            out=prow[:, 0:4], in0=prow[:, 0:4], scalar1=0.0, scalar2=None,
            op0=OP.is_equal, op1=OP.bypass), reads=["ptmp"], writes=["ptmp"])
        p.emit("vector", lambda e, r=r: e.tensor_tensor(
            out=perm_f[:, r, :], in0=prow[:, 0:4], in1=prow[:, 4:8], op=OP.add),
            reads=["ptmp", "p4"], writes=["perm"])
    p.emit("gpsimd", lambda e: e.iota(io32[:, :], pattern=[[-1, 128]], base=0,
                                      channel_multiplier=1), writes=["io32"])
    p.emit("vector", lambda e: e.tensor_scalar(out=ident_b[:, :], in0=io32[:, :],
                                               scalar1=0, scalar2=None,
                                               op0=OP.is_equal, op1=OP.bypass),
           reads=["io32"], writes=["ident"])
    p.emit("vector", lambda e: e.tensor_scalar(out=ident_f[:, :], in0=io32[:, :],
                                               scalar1=0, scalar2=None,
                                               op0=OP.is_equal, op1=OP.bypass),
           reads=["io32"], writes=["identf"])

    def chan_load(dstname, dst_ap, src_ap, n):
        q = n // 128
        p.emit("sync",
               lambda e: e.dma_start(out=stg[0:q, :], in_=src_ap.rearrange("(q j) -> q j", j=128)),
               writes=["stg"], dma="chan")
        p.emit("tensor", lambda e: e.transpose(ps[5][:, 0:q], stg[0:q, :], ident_f[0:q, 0:q]),
               reads=["stg", "identf"], accum_key="B5")
        p.emit("vector", lambda e: e.tensor_copy(dst_ap, ps[5][:, 0:q]),
               reads=["B5"], writes=[dstname])
    chan_load("tm_c", tm_c[:, :], di["L0_attn_time_mix"], C)
    chan_load("dec_c", dec_c[:, :], di["L0_attn_time_decay"], C)
    chan_load("fw_c", fw_c[:, :], di["L0_attn_time_first"], C)
    p.emit("scalar", lambda e: e.activation(out=dec_c[:, :], in_=dec_c[:, :], func=AF.Exp),
           reads=["dec_c"], writes=["dec_c"])
    p.emit("scalar", lambda e: e.activation(out=dec_c[:, :], in_=dec_c[:, :], func=AF.Exp, scale=-1.0),
           reads=["dec_c"], writes=["dec_c"])
    p.emit("scalar", lambda e: e.activation(out=fw_c[:, :], in_=fw_c[:, :], func=AF.Exp),
           reads=["fw_c"], writes=["fw_c"])

    WQ_CHUNKS = ([("L1_ffn_w_gate", 0, ch, C) for ch in range(I_FF // 128)]
                 + [("L1_ffn_w_up", 1, ch, C) for ch in range(I_FF // 128)]
                 + [("L1_ffn_w_down", 2, ch, I_FF) for ch in range(C // 128)])

    def make_wq_gen(width=2816):
        emit = make_wq_emitter(width, step=True)
        return emit

    def make_wq_emitter(width=2816, step=False):
        wrow = A.take([128, width], F32)
        qA = A.take([128, width], F32)
        qB = A.take([128, width], F32)
        qC = wrow
        bmq = A.take([128, width // 16], F32)
        scq = A.take([128, width // 16], F32)
        rsq = A.take([128, width // 16], F32)
        wqout = qB.bitcast(BF16)[:, 0:width]

        def emit_chunk(item):
            srcn, dsth, ch, ncols = item
            for _ in emit_gen(item):
                pass

        def emit_gen(item):
            srcn, dsth, ch, ncols = item
            dst_h = (wqg_h, wqu_h, wqd_h)[dsth]
            nblk = ncols // 16
            n = ncols
            p.emit("sync", lambda e: e.dma_start(
                out=wrow[:, 0:ncols], in_=di[srcn][128 * ch:128 * (ch + 1), :]),
                writes=["wrow"], dma="wqld")
            yield
            p.emit("vector", lambda e: e.tensor_reduce(
                out=bmq[:, 0:nblk], in_=wrow[:, 0:n].rearrange("p (b k) -> p b k", k=16),
                axis=AX.X, op=OP.max, apply_absolute_value=True),
                reads=["wrow"], writes=["bmq"])
            yield
            p.emit("vector", lambda e: e.tensor_scalar(
                out=scq[:, 0:nblk], in0=bmq[:, 0:nblk], scalar1=6e-12, scalar2=1.0 / 6.0,
                op0=OP.max, op1=OP.mult), reads=["bmq"], writes=["scq"])
            yield
            p.emit("vector", lambda e: e.reciprocal(out=rsq[:, 0:nblk], in_=scq[:, 0:nblk]),
                   reads=["scq"], writes=["rsq"])
            yield
            p.emit("vector", lambda e: e.tensor_tensor(
                out=qA[:, 0:n].rearrange("p (b k) -> p b k", k=16),
                in0=wrow[:, 0:n].rearrange("p (b k) -> p b k", k=16),
                in1=rsq[:, 0:nblk].unsqueeze(-1).broadcast_to([128, nblk, 16]), op=OP.mult),
                reads=["wrow", "rsq"], writes=["qA"])
            yield
            p.emit("scalar", lambda e: e.activation(out=qB[:, 0:n], in_=qA[:, 0:n], func=AF.Abs),
                   reads=["qA"], writes=["qB", "wqo"])
            yield
            p.emit("vector", lambda e: e.tensor_scalar(
                out=qC[:, 0:n], in0=qB[:, 0:n], scalar1=3.5, scalar2=0.5,
                op0=OP.is_gt, op1=OP.add), reads=["qB", "wrow"], writes=["qC", "wrow"])
            yield
            p.emit("vector", lambda e: e.tensor_scalar(
                out=qB[:, 0:n], in0=qB[:, 0:n], scalar1=1.75, scalar2=0.5,
                op0=OP.is_gt, op1=OP.mult), reads=["qB"], writes=["qB"])
            yield
            p.emit("vector", lambda e: e.tensor_tensor(
                out=qB[:, 0:n], in0=qB[:, 0:n], in1=qC[:, 0:n], op=OP.add),
                reads=["qB", "qC"], writes=["qB"])
            p.emit("vector", lambda e: e.reciprocal(out=qC[:, 0:n], in_=qB[:, 0:n]),
                   reads=["qB"], writes=["qC"])
            p.emit("vector", lambda e: e.tensor_tensor(
                out=qA[:, 0:n], in0=qA[:, 0:n], in1=qC[:, 0:n], op=OP.mult),
                reads=["qA", "qC"], writes=["qA"])
            yield
            p.emit("vector", lambda e: e.tensor_scalar(
                out=qA[:, 0:n], in0=qA[:, 0:n], scalar1=RND_C, scalar2=RND_C,
                op0=OP.add, op1=OP.subtract), reads=["qA"], writes=["qA"])
            yield
            p.emit("vector", lambda e: e.tensor_tensor(
                out=qA[:, 0:n], in0=qA[:, 0:n], in1=qB[:, 0:n], op=OP.mult),
                reads=["qA", "qB"], writes=["qA"])
            yield
            p.emit("vector", lambda e: e.tensor_tensor(
                out=wqout[:, 0:n].rearrange("p (b k) -> p b k", k=16),
                in0=qA[:, 0:n].rearrange("p (b k) -> p b k", k=16),
                in1=scq[:, 0:nblk].unsqueeze(-1).broadcast_to([128, nblk, 16]), op=OP.mult),
                reads=["qA", "scq"], writes=["wqo", "qB"])
            yield
            p.emit("sync", lambda e: e.dma_start(
                out=dst_h[128 * ch:128 * (ch + 1), :], in_=wqout[:, 0:ncols]),
                reads=["wqo"], dma="wqst")
            yield
        return emit_gen if step else emit_chunk

    NT_A = 256
    NA = T // NT_A
    if last_phase == "pre0":
        p.barrier()
        with nc.Block() as block:
            p.finalize(block)
        st.close()
        return nc

    wkT = A.take([128, KC, C], BF16)
    wvT = A.take([128, KC, C], BF16)
    wrT = A.take([128, KC, C], BF16)
    woT = A.take([128, KC, C], BF16)
    wtmp = A.take([128, 4, C], BF16)
    x_tm = A.take([128, 2, 2, C], F32)
    stat6 = A.take([128, 2, 2, 6], F32)
    mvp = A.take([128, 2, 2], F32)
    rstd0 = A.take([128, 2, 1], F32)
    nmr0 = A.take([128, 2, 1], F32)
    xn_tm = A.take([128, 2, C], BF16)
    xn_fm = A.take([128, KC, NT_A + 1], BF16)
    xm_fm = A.take([128, KC, NT_A], BF16)
    d_fm = A.take([128, KC, NT_A], BF16)
    k_sb = A.take([128, KC, NT_A], F32)
    kv_sb = A.take([128, KC, NT_A], F32)
    er_sb = A.take([128, KC, NT_A], F32)
    wkvS = A.take([128, KC, NT_A + 1], F32)
    wkS = A.take([128, KC, NT_A + 1], F32)
    wkv_c = A.take([128, KC, NT_A], F32)
    wk_c = A.take([128, KC, NT_A], F32)
    rwkv_b = A.take([128, KC, NT_A], BF16)
    s_sb = A.take([128, KC, NT_A], BF16)

    def load_transposed(dstname, dst, src_name):
        src = di[src_name]
        for half in range(2):
            p.emit("gpsimd",
                   lambda e, half=half: e.dma_start(
                       out=wtmp[:, :, :],
                       in_=src.rearrange("(jc jp) d -> jp jc d", jp=128)[:, 4 * half:4 * half + 4, :]),
                   writes=["wtmp"], dma="wtmp")
            tp_batch2([([wtmp[:, jc4, 128 * dc:128 * dc + 128] for dc in range(d0, d0 + 4)],
                         dst[:, d0:d0 + 4, 128 * (4 * half + jc4):128 * (4 * half + jc4) + 128],
                         ["wtmp"], [])
                        for jc4 in range(4) for d0 in (0, 4)])
        p.setw(dstname, p.tok("vector"))
    load_transposed("wkT", wkT, "L0_attn_Wk")
    load_transposed("wvT", wvT, "L0_attn_Wv")
    load_transposed("wrT", wrT, "L0_attn_Wr")
    load_transposed("woT", woT, "L0_attn_Wo")

    if last_phase == "pre":
        p.barrier()
        with nc.Block() as block:
            p.finalize(block)
        st.close()
        return nc

    for t in range(NA):
        db = t % 2
        n0 = NT_A * t
        xk = "x_tm%d" % db
        p.emit("sync",
               lambda e, t=t, db=db: e.dma_start(
                   out=x_tm[:, db, :, :],
                   in_=di["x"][NT_A * t:NT_A * (t + 1), :].rearrange("(s p) c -> p s c", p=128)),
               writes=[xk], dma="xload%d" % db)

        for s_i in range(2):
            sk = "st%d" % s_i
            for h_i in range(2):
                p.emit("vector", lambda e, db=db, s_i=s_i, h_i=h_i: e.bn_stats(
                    stat6[:, s_i, h_i, :], x_tm[:, db, s_i, 512 * h_i:512 * (h_i + 1)]),
                    reads=[xk], writes=[sk])
            p.emit("vector", lambda e, s_i=s_i: e.bn_aggr(mvp[:, s_i, :], stat6[:, s_i, :, :]),
                   reads=[sk], writes=["mv%d" % s_i])
            p.emit("vector", lambda e, s_i=s_i: e.tensor_scalar(
                out=rstd0[:, s_i, :], in0=mvp[:, s_i, 1:2], scalar1=1e-5, scalar2=None,
                op0=OP.add, op1=OP.bypass), reads=["mv%d" % s_i], writes=["rstd%d" % s_i])
            p.emit("scalar", lambda e, s_i=s_i: e.activation(
                out=rstd0[:, s_i, :], in_=rstd0[:, s_i, :], func=AF.Ln),
                reads=["rstd%d" % s_i], writes=["rstd%d" % s_i])
            p.emit("scalar", lambda e, s_i=s_i: e.activation(
                out=rstd0[:, s_i, :], in_=rstd0[:, s_i, :], func=AF.Exp, scale=-0.5),
                reads=["rstd%d" % s_i], writes=["rstd%d" % s_i])
            p.emit("vector", lambda e, s_i=s_i: e.scalar_tensor_tensor(
                out=nmr0[:, s_i, :], in0=mvp[:, s_i, 0:1], scalar=-1.0, in1=rstd0[:, s_i, :],
                op0=OP.mult, op1=OP.mult), reads=["mv%d" % s_i, "rstd%d" % s_i],
                writes=["nmr%d" % s_i])
            p.emit("scalar", lambda e, db=db, s_i=s_i: e.activation(
                out=xn_tm[:, s_i, :], in_=x_tm[:, db, s_i, :], func=AF.Identity,
                scale=rstd0[:, s_i, :], bias=nmr0[:, s_i, :]),
                reads=[xk, "rstd%d" % s_i, "nmr%d" % s_i], writes=["xn_tm%d" % s_i])

        if t > 0:
            p.emit("vector", lambda e: e.tensor_copy(xn_fm[:, :, 0:1], xn_fm[:, :, NT_A:NT_A + 1]),
                   reads=["xn_fm"], writes=["xn_carry"])
        else:
            p.emit("vector", lambda e: e.memset(xn_fm[:, :, 0:1], 0.0), writes=["xn_carry"])

        tp_batch2([([xn_tm[:, s_i, 128 * c:128 * (c + 1)] for c in range(c0, c0 + 4)],
                     xn_fm[:, c0:c0 + 4, 1 + 128 * s_i:1 + 128 * (s_i + 1)],
                     ["xn_tm%d" % s_i],
                     ["xn_fm"] if (s_i == 0 and c0 == 0) else [])
                    for s_i in range(2) for c0 in (0, 4)])
        p.setw("xn_fm", p.tok("vector"))

        p.emit("sync", lambda e, n0=n0: e.dma_start(
            out=ap_fm(xn_h, 0, KC, n0, n0 + NT_A), in_=xn_fm[:, :, 1:NT_A + 1]),
            reads=["xn_fm"], dma="xnst")

        p.emit("vector", lambda e: e.tensor_tensor(
            out=d_fm[:, :, :], in0=xn_fm[:, :, 1:NT_A + 1], in1=xn_fm[:, :, 0:NT_A],
            op=OP.subtract), reads=["xn_fm", "xn_carry"], writes=["d_fm"])
        for c in range(KC):
            p.emit("vector", lambda e, c=c: e.scalar_tensor_tensor(
                out=xm_fm[:, c, :], in0=d_fm[:, c, :], scalar=tm_c[:, c:c + 1],
                in1=xn_fm[:, c, 0:NT_A], op0=OP.mult, op1=OP.add),
                reads=["d_fm", "tm_c", "xn_fm", "xn_carry"], writes=["xm%d" % c])

        for jc in range(KC):
            bk = jc % 2
            for kc in range(KC):
                mm("B%d" % bk, ps[bk][:, 0:NT_A], wkT[:, kc, 128 * jc:128 * (jc + 1)],
                   xm_fm[:, kc, :], ["wkT", "xm%d" % kc], kc == 0, kc == KC - 1)
            p.emit("scalar", lambda e, jc=jc, bk=bk: e.activation(
                out=k_sb[:, jc, :], in_=ps[bk][:, 0:NT_A], func=AF.Exp),
                reads=["B%d" % bk], writes=["k%d" % jc])
            for kc in range(KC):
                mm("B%d" % (2 + bk), ps[2 + bk][:, 0:NT_A], wvT[:, kc, 128 * jc:128 * (jc + 1)],
                   xm_fm[:, kc, :], ["wvT", "xm%d" % kc], kc == 0, kc == KC - 1)
            p.emit("vector", lambda e, jc=jc: e.tensor_scalar(
                out=k_sb[:, jc, :], in0=k_sb[:, jc, :], scalar1=EXP60, scalar2=None,
                op0=OP.min, op1=OP.bypass), reads=["k%d" % jc], writes=["k%d" % jc])
            p.emit("vector", lambda e, jc=jc, bk=bk: e.tensor_tensor(
                out=kv_sb[:, jc, :], in0=k_sb[:, jc, :], in1=ps[2 + bk][:, 0:NT_A], op=OP.mult),
                reads=["k%d" % jc, "B%d" % (2 + bk)], writes=["kv%d" % jc])
            for kc in range(KC):
                mm("B%d" % (4 + bk), ps[4 + bk][:, 0:NT_A], wrT[:, kc, 128 * jc:128 * (jc + 1)],
                   xm_fm[:, kc, :], ["wrT", "xm%d" % kc], kc == 0, kc == KC - 1)
            p.emit("scalar", lambda e, jc=jc, bk=bk: e.activation(
                out=er_sb[:, jc, :], in_=ps[4 + bk][:, 0:NT_A], func=AF.Exp, scale=-1.0),
                reads=["B%d" % (4 + bk)], writes=["er%d" % jc])

        if t == 0:
            p.emit("vector", lambda e: e.memset(wkvS[:, :, 0:1], 0.0), writes=["wkvS0"])
            p.emit("vector", lambda e: e.memset(wkS[:, :, 0:1], 0.0), writes=["wkS0"])
        else:
            p.emit("vector", lambda e: e.tensor_copy(wkvS[:, :, 0:1], wkvS[:, :, NT_A:NT_A + 1]),
                   reads=["wkvS"], writes=["wkvS0"])
            p.emit("vector", lambda e: e.tensor_copy(wkS[:, :, 0:1], wkS[:, :, NT_A:NT_A + 1]),
                   reads=["wkS"], writes=["wkS0"])
        for c in range(KC):
            p.emit("vector", lambda e, c=c: e.tensor_tensor_scan(
                out=wkvS[:, c, 1:NT_A + 1], data0=dec_c[:, c:c + 1].broadcast_to([128, NT_A]),
                data1=kv_sb[:, c, :], initial=wkvS[:, c, 0:1], op0=OP.mult, op1=OP.add),
                reads=["dec_c", "kv%d" % c, "wkvS0"], writes=["wkvS"])
            p.emit("vector", lambda e, c=c: e.tensor_tensor_scan(
                out=wkS[:, c, 1:NT_A + 1], data0=dec_c[:, c:c + 1].broadcast_to([128, NT_A]),
                data1=k_sb[:, c, :], initial=wkS[:, c, 0:1], op0=OP.mult, op1=OP.add),
                reads=["dec_c", "k%d" % c, "wkS0"], writes=["wkS"])
            p.emit("vector", lambda e, c=c: e.scalar_tensor_tensor(
                out=wkv_c[:, c, :], in0=kv_sb[:, c, :], scalar=fw_c[:, c:c + 1],
                in1=wkvS[:, c, 0:NT_A], op0=OP.mult, op1=OP.add),
                reads=["kv%d" % c, "fw_c", "wkvS", "wkvS0"], writes=["wkvc"])
            p.emit("vector", lambda e, c=c: e.scalar_tensor_tensor(
                out=wk_c[:, c, :], in0=k_sb[:, c, :], scalar=fw_c[:, c:c + 1],
                in1=wkS[:, c, 0:NT_A], op0=OP.mult, op1=OP.add),
                reads=["k%d" % c, "fw_c", "wkS", "wkS0"], writes=["wkc"])
        p.emit("vector", lambda e: e.tensor_scalar(
            out=wk_c[:, :, :], in0=wk_c[:, :, :], scalar1=1e-16, scalar2=None,
            op0=OP.add, op1=OP.bypass), reads=["wkc"], writes=["wkc"])

        p.emit("scalar", lambda e: e.activation(out=wk_c[:, :, :], in_=wk_c[:, :, :], func=AF.Ln),
               reads=["wkc"], writes=["wkc"])
        p.emit("scalar", lambda e: e.activation(out=er_sb[:, :, :], in_=er_sb[:, :, :],
                                                func=AF.Ln, bias=1.0),
               reads=["er%d" % c2 for c2 in range(KC)], writes=["ers"])
        p.emit("vector", lambda e: e.tensor_tensor(
            out=wk_c[:, :, :], in0=wk_c[:, :, :], in1=er_sb[:, :, :], op=OP.add),
            reads=["wkc", "ers"], writes=["wkc"])
        p.emit("scalar", lambda e: e.activation(out=wk_c[:, :, :], in_=wk_c[:, :, :],
                                                func=AF.Exp, scale=-1.0),
               reads=["wkc"], writes=["wkc"])
        p.emit("vector", lambda e: e.tensor_tensor(
            out=rwkv_b[:, :, :], in0=wkv_c[:, :, :], in1=wk_c[:, :, :], op=OP.mult),
            reads=["wkvc", "wkc"], writes=["rwkv"])

        for jc in range(KC):
            sb_ = 4 + jc % 2
            for kc in range(KC):
                mm("B%d" % sb_, ps[sb_][:, 0:NT_A], woT[:, kc, 128 * jc:128 * (jc + 1)],
                   rwkv_b[:, kc, :], ["woT", "rwkv"], kc == 0, kc == KC - 1)
            p.emit("vector", lambda e, jc=jc, sb_=sb_: e.tensor_copy(s_sb[:, jc, :], ps[sb_][:, 0:NT_A]),
                   reads=["B%d" % sb_], writes=["ssb%d" % jc])
        p.emit("sync", lambda e, n0=n0: e.dma_start(
            out=ap_fm(s0_h, 0, KC, n0, n0 + NT_A), in_=s_sb[:, :, :]),
            reads=["ssb%d" % j for j in range(KC)], dma="sst")

    if last_phase == "p0a":
        p.barrier()
        with nc.Block() as block:
            p.finalize(block)
        st.close()
        return nc

    p.barrier()
    A.reset()
    NT_B = 1024
    NB = T // NT_B

    wrb = A.take([128, KC, 4], BF16)
    wcol = A.take([128, 32, E], F32)
    wTb = A.take([128, 128], F32)[0:4, :]
    b1L0 = A.take([128, E, 32], F32)
    gb1b = A.take([128, E, 32], BF16)
    b2s = A.take([128, E, KC], F32)
    c_sb = A.take([128, E, KC], F32)
    ccT = A.take([128, KC, 128], F32)[0:4, :, :]

    xnb = A.take([128, 1, KC, NT_B], BF16)
    y_acc = A.take([128, NT_B // 128, C], F32)
    h_sb = A.take([128, 16, NT_B], BF16)
    w1b = A.take([128, 2, KC, 512], BF16)
    w2b = A.take([128, 4, 4, C], BF16)
    x_in = A.take([128, 2, C], F32)
    lg_sb = x_in[0:4, 0, 0:512]
    msk_sb = x_in[0:4, 0, 512:1024]
    gx_sb = x_in[0:4, 1, 0:512]
    x1_tm = A.take([128, C], F32)
    st6b = A.take([128, 2, 6], F32)
    g_tm = A.take([128, C], BF16)
    s_tm = A.take([128, 2, C], BF16)

    p.emit("gpsimd", lambda e: e.dma_start(
        out=wrb[:, :, :], in_=di["L0_ffn_wr"].rearrange("(c p) e -> p c e", p=128),
        allow_slow_non_contiguous=True), writes=["wrb"], dma="chg")
    p.emit("gpsimd", lambda e: e.dma_start(
        out=br4[:, :], in_=di["L0_ffn_br"].unsqueeze(-1), allow_slow_non_contiguous=True),
        writes=["br4"], dma="chg")
    for nt in range(T // 512):
        n0 = nt * 512
        xq0 = "xnq%d" % (nt % 2)
        p.emit("sync", lambda e, n0=n0, nt=nt: e.dma_start(
            out=xnb[:, 0, :, 512 * (nt % 2):512 * (nt % 2) + 512], in_=ap_fm(xn_h, 0, KC, n0, n0 + 512)),
            writes=[xq0, "xnb0"], dma="xnld%d" % (nt % 2))
        for kc in range(KC):
            mm("B0", ps[0][0:4, 0:512], wrb[:, kc, :], xnb[:, 0, kc, 512 * (nt % 2):512 * (nt % 2) + 512],
               ["wrb", xq0], kc == 0, kc == KC - 1)
        p.emit("vector", lambda e: e.tensor_scalar(
            out=lg_sb[:, :], in0=ps[0][0:4, 0:512], scalar1=br4[:, :], scalar2=None,
            op0=OP.add, op1=OP.bypass), reads=["B0", "br4"], writes=["lg"])
        for r in range(3):
            mm("B%d" % (1 + r), ps[1 + r][0:4, 0:512], perm_f[:, r, :], lg_sb[:, :],
               ["perm", "lg"], True, True)
        p.emit("vector", lambda e: e.tensor_tensor(
            out=msk_sb[:, :], in0=ps[1][0:4, 0:512], in1=lg_sb[:, :], op=OP.is_gt),
            reads=["B1", "lg"], writes=["cnt"])
        for r in (1, 2):
            p.emit("vector", lambda e, r=r: e.tensor_tensor(
                out=gx_sb, in0=ps[1 + r][0:4, 0:512], in1=lg_sb[:, :], op=OP.is_gt),
                reads=["B%d" % (1 + r), "lg"], writes=["gtt"])
            p.emit("vector", lambda e: e.tensor_tensor(
                out=msk_sb[:, :], in0=msk_sb[:, :], in1=gx_sb, op=OP.add),
                reads=["gtt", "cnt"], writes=["cnt"])
        p.emit("vector", lambda e: e.tensor_scalar(
            out=msk_sb[:, :], in0=msk_sb[:, :], scalar1=1.5, scalar2=None,
            op0=OP.is_lt, op1=OP.bypass), reads=["cnt"], writes=["cnt"])
        p.emit("vector", lambda e: e.tensor_tensor(
            out=gx_sb, in0=ps[1][0:4, 0:512], in1=lg_sb[:, :], op=OP.max),
            reads=["B1", "lg"], writes=["mx"])
        for r in (1, 2):
            p.emit("vector", lambda e, r=r: e.tensor_tensor(
                out=gx_sb, in0=gx_sb, in1=ps[1 + r][0:4, 0:512], op=OP.max),
                reads=["mx", "B%d" % (1 + r)], writes=["mx"])
        p.emit("vector", lambda e: e.tensor_tensor(
            out=lg_sb[:, :], in0=lg_sb[:, :], in1=gx_sb, op=OP.subtract),
            reads=["lg", "mx"], writes=["lg"])
        p.emit("scalar", lambda e: e.activation(out=lg_sb[:, :], in_=lg_sb[:, :], func=AF.Exp),
               reads=["lg"], writes=["lg"])
        p.emit("vector", lambda e: e.tensor_tensor(
            out=lg_sb[:, :], in0=lg_sb[:, :], in1=msk_sb[:, :], op=OP.mult),
            reads=["lg", "cnt"], writes=["lg"])
        mm("B4", ps[4][0:4, 0:512], ones4f[:, :], lg_sb[:, :], ["ones4", "lg"], True, True)
        p.emit("scalar", lambda e: e.activation(out=msk_sb[:, :], in_=ps[4][0:4, 0:512], func=AF.Ln),
               reads=["B4"], writes=["rz"])
        p.emit("scalar", lambda e: e.activation(out=msk_sb[:, :], in_=msk_sb[:, :], func=AF.Exp,
                                                scale=-1.0), reads=["rz"], writes=["rz"])
        p.emit("vector", lambda e, n0=n0: e.tensor_tensor(
            out=lg_sb[:, :], in0=lg_sb[:, :], in1=msk_sb[:, :], op=OP.mult),
            reads=["lg", "rz"], writes=["lg"])
        tp_batch2([([lg_sb[:, 128 * b4:128 * (b4 + 1)] for b4 in range(4)],
                    wcol[:, (n0 // 128):(n0 // 128) + 4, :].rearrange("p a o -> p (a o)"),
                    ["lg"], ["wcol"])], f32=True)

    for e_i in range(E):
        chan_load("b1L0", b1L0[:, e_i, :], di["L0_ffn_b1"][e_i], H)
        chan_load("b2sE", b2s[:, e_i, :], di["L0_ffn_b2"][e_i], C)
    p.emit("scalar", lambda e: e.activation(out=gb1b[:, :, :], in_=b1L0[:, :, :], func=AF.Gelu),
           reads=["b1L0"], writes=["gb1"])
    for e_i in (1, 2, 3):
        p.emit("vector", lambda e, e_i=e_i: e.tensor_tensor(
            out=b2s[:, 0, :], in0=b2s[:, 0, :], in1=b2s[:, e_i, :], op=OP.add),
            reads=["b2sE"], writes=["b2sE"])

    wq_gen0 = make_wq_gen(width=1024)
    wq_queue0 = [it for it in WQ_CHUNKS if it[3] == C]
    wq_state0 = {"g": None}

    def wq_step(nops):
        for _ in range(nops):
            if wq_state0["g"] is None:
                if not wq_queue0:
                    return
                wq_state0["g"] = wq_gen0(wq_queue0.pop(0))
            try:
                next(wq_state0["g"])
            except StopIteration:
                wq_state0["g"] = None

    for hf in range(NB):
        hf0 = hf * NT_B
        xdb = 0
        xnk = "xnb0"
        p.emit("sync", lambda e, hf0=hf0, xdb=xdb: e.dma_start(
            out=xnb[:, xdb, :, :], in_=ap_fm(xn_h, 0, KC, hf0, hf0 + NT_B)),
            writes=["xnb0", "xnb1", "xnq0", "xnq1"], dma="xnbld")

        for e_i in range(E):
            for hg in range(2):
                for hc4 in range(4):
                    hc = 4 * hg + hc4
                    wdb = hc % 2
                    w1k = "w1b%d" % wdb
                    w2k = "w2g%d" % hc4
                    p.emit("gpsimd", lambda e, e_i=e_i, hc=hc, wdb=wdb: e.dma_start(
                        out=w1b[:, wdb, :, :],
                        in_=di["L0_ffn_w1"][e_i].rearrange("(kc p) h -> p kc h", p=128)[:, :, 512 * hc:512 * (hc + 1)]),
                        writes=[w1k], dma="w1ld%d" % wdb)
                    p.emit("gpsimd", lambda e, e_i=e_i, hc=hc, hc4=hc4: e.dma_start(
                        out=w2b[:, hc4, :, :],
                        in_=di["L0_ffn_w2"][e_i][512 * hc:512 * (hc + 1), :].rearrange("(hs p) d -> p hs d", p=128)),
                        writes=[w2k], dma="w2ld%d" % hc4)
                    for ts_ in range(NT_B // 512):
                        for hs in range(4):
                            hb = (4 * ts_ + hs) % 2
                            for kc in range(KC):
                                mm("B%d" % hb, ps[hb][:, 0:512],
                                   w1b[:, wdb, kc, 128 * hs:128 * (hs + 1)],
                                   xnb[:, xdb, kc, 512 * ts_:512 * (ts_ + 1)],
                                   [w1k, xnk], kc == 0, kc == KC - 1)
                            p.emit("scalar", lambda e, hs=hs, ts_=ts_, e_i=e_i, hc=hc, hb=hb, hc4=hc4: e.activation(
                                out=h_sb[:, 4 * hc4 + hs, 512 * ts_:512 * (ts_ + 1)], in_=ps[hb][:, 0:512],
                                func=AF.Gelu, bias=b1L0[:, e_i, 4 * hc + hs:4 * hc + hs + 1]),
                                reads=["B%d" % hb, "b1L0"], writes=["hsb%d%d" % (hc4, ts_)])
                    if hf == 0:
                        for hs in range(4):
                            for dc in range(KC):
                                mm("B4", ps[4][:, 32 + 8 * e_i + dc:33 + 8 * e_i + dc],
                                   w2b[:, hc4, hs, 128 * dc:128 * (dc + 1)],
                                   gb1b[:, e_i, 4 * hc + hs:4 * hc + hs + 1],
                                   [w2k, "gb1"], (e_i == 0 and hc == 0 and hs == 0 and dc == 0), True)
                for tk in range(NT_B // 128):
                    ts_ = tk // 4
                    blk = hf0 // 128 + tk
                    for dh in range(2):
                        yb = 2 + (2 * tk + dh) % 2
                        for hx in range(16):
                            mm("B%d" % yb, ps[yb][:, 0:512],
                               h_sb[:, hx, 128 * tk:128 * (tk + 1)],
                               w2b[:, hx // 4, hx % 4, 512 * dh:512 * (dh + 1)],
                               ["w2g%d" % (hx // 4), "hsb%d%d" % (hx // 4, ts_)],
                               hx == 0, hx == 15)
                        first = (e_i == 0 and hg == 0)
                        if first:
                            p.emit("vector", lambda e, tk=tk, dh=dh, yb=yb, e_i=e_i, blk=blk: e.tensor_scalar(
                                out=y_acc[:, tk, 512 * dh:512 * (dh + 1)], in0=ps[yb][:, 0:512],
                                scalar1=wcol[:, blk, e_i:e_i + 1], scalar2=None,
                                op0=OP.mult, op1=OP.bypass),
                                reads=["B%d" % yb, "wcol"], writes=["yac%d%d" % (tk, dh)])
                        else:
                            p.emit("vector", lambda e, tk=tk, dh=dh, yb=yb, e_i=e_i, blk=blk: e.scalar_tensor_tensor(
                                out=y_acc[:, tk, 512 * dh:512 * (dh + 1)], in0=ps[yb][:, 0:512],
                                scalar=wcol[:, blk, e_i:e_i + 1],
                                in1=y_acc[:, tk, 512 * dh:512 * (dh + 1)], op0=OP.mult, op1=OP.add),
                                reads=["B%d" % yb, "wcol", "yac%d%d" % (tk, dh)],
                                writes=["yac%d%d" % (tk, dh)])
                        if e_i + hg > 0 or hf > 0:
                            wq_step(2)
        if hf == 0:
            p.emit("vector", lambda e: e.tensor_copy(
                c_sb.rearrange("p e d -> p (e d)"), ps[4][:, 32:64]),
                reads=["B4"], writes=["c_sb"])
            for e_i in (1, 2, 3):
                p.emit("vector", lambda e, e_i=e_i: e.tensor_tensor(
                    out=b2s[:, 0, :], in0=b2s[:, 0, :], in1=c_sb[:, e_i, :], op=OP.add),
                    reads=["c_sb", "b2sE"], writes=["b2sE"])
            p.emit("vector", lambda e: e.tensor_tensor(
                out=b2s[:, 0, :], in0=b2s[:, 0, :], in1=c_sb[:, 0, :], op=OP.add),
                reads=["c_sb", "b2sE"], writes=["b2sE"])
            for e_i in range(E):
                p.emit("vector", lambda e, e_i=e_i: e.tensor_tensor(
                    out=c_sb[:, e_i, :], in0=b2s[:, 0, :], in1=c_sb[:, e_i, :], op=OP.subtract),
                    reads=["b2sE", "c_sb"], writes=["c_sb"])
            tp_batch([(c_sb[:, :, dc], ccT[:, dc, :], ["c_sb"], ["ccT"]) for dc in range(KC)],
                     f32=True)

        for tk in range(NT_B // 128):
            tok_n0 = hf0 + 128 * tk
            blk = tok_n0 // 128
            xku = "x_in%d" % (tk % 2)
            p.emit("sync", lambda e, tok_n0=tok_n0, tk=tk: e.dma_start(
                out=x_in[:, tk % 2, :],
                in_=di["x"][tok_n0:tok_n0 + 128, :]),
                writes=[xku, "lg", "cnt", "mx", "gtt"], dma="xld2%d" % (tk % 2))
            p.emit("sync", lambda e, tok_n0=tok_n0, tk=tk: e.dma_start(
                out=s_tm[:, tk % 2, :], in_=s0_h[:, tok_n0:tok_n0 + 128], transpose=True),
                writes=["s_tm%d" % (tk % 2)], dma="sldT%d" % (tk % 2))
            tp_batch([(wcol[:, blk, :], wTb[:, :], ["wcol"], ["wTb"])], f32=True, fbanks=(4, 5))
            for dh in range(2):
                ccb = 4 + dh
                mm("B%d" % ccb, ps[ccb][:, 0:512], wTb[:, :],
                   ccT.rearrange("e c o -> e (c o)")[:, 512 * dh:512 * (dh + 1)],
                   ["wTb", "ccT"], True, True)
                p.emit("vector", lambda e, tk=tk, dh=dh, ccb=ccb: e.tensor_tensor(
                    out=y_acc[:, tk, 512 * dh:512 * (dh + 1)],
                    in0=y_acc[:, tk, 512 * dh:512 * (dh + 1)], in1=ps[ccb][:, 0:512], op=OP.add),
                    reads=["B%d" % ccb, "yac%d%d" % (tk, dh)], writes=["yac%d%d" % (tk, dh)])
            p.emit("scalar", lambda e, tk=tk: e.activation(
                out=g_tm[:, :], in_=y_acc[:, tk, :], func=AF.Silu),
                reads=["yac%d0" % tk, "yac%d1" % tk], writes=["g_tm"])
            p.emit("vector", lambda e, tk=tk: e.tensor_tensor(
                out=g_tm[:, :], in0=g_tm[:, :], in1=s_tm[:, tk % 2, :], op=OP.mult),
                reads=["g_tm", "s_tm%d" % (tk % 2)], writes=["g_tm"])
            p.emit("vector", lambda e, tk=tk: e.tensor_tensor(
                out=x1_tm[:, :], in0=x_in[:, tk % 2, :], in1=g_tm[:, :], op=OP.add),
                reads=[xku, "g_tm"], writes=["x1tm"])
            ti = tok_n0 // 128
            for h_i in range(2):
                p.emit("vector", lambda e, h_i=h_i: e.bn_stats(
                    st6b[:, h_i, :], x1_tm[:, 512 * h_i:512 * (h_i + 1)]),
                    reads=["x1tm"], writes=["st6b"])
            p.emit("vector", lambda e, ti=ti: e.bn_aggr(mvar1[:, :, ti], st6b[:, :, :]),
                   reads=["st6b"], writes=["mvar1"])
            p.emit("sync", lambda e, tok_n0=tok_n0: e.dma_start(
                out=x1t_h[tok_n0:tok_n0 + 128, :], in_=x1_tm[:, :]),
                reads=["x1tm"], dma="x1st")

    while wq_queue0 or wq_state0["g"] is not None:
        wq_step(4)

    if last_phase == "p0b":
        p.barrier()
        with nc.Block() as block:
            p.finalize(block)
        st.close()
        return nc

    p.barrier()
    A.reset()
    NTILES = T // 512

    kvw = A.take([128, KC, 2048], BF16)
    qkvb_row = A.take([128, 2048], F32)[0:1, :]
    kvbrep = A.take([128, 4, 512], F32)
    x1l = A.take([128, 2, 4, C], F32)
    tln = A.take([128, 4, C], BF16)
    ln1f = A.take([128, KC, 512], BF16)
    kb_sb = A.take([128, 2, 512], F32)
    ex_sb = A.take([128, 2, 512], F32)
    kf_sb = A.take([128, C], BF16)
    v_sb = A.take([128, 16, 65], BF16)
    kv_acc = A.take([128, 8, 130], F32)

    p.emit("gpsimd", lambda e: e.dma_start(
        out=kvw[:, :, :],
        in_=di["L1_attn_qkv_w"].rearrange("(kc p) j -> p kc j", p=128)[:, :, 1024:3072]),
        writes=["kvw"], dma="kvwld")
    chan_load("g1c", g1c[:, :], di["L1_attn_g1"], C)
    chan_load("be1c", be1c[:, :], di["L1_attn_be1"], C)
    p.emit("sync", lambda e: e.dma_start(
        out=qkvb_row, in_=di["L1_attn_qkv_b"][1024:3072].rearrange("(o j) -> o j", o=1)),
        writes=["qkvbr"], dma="qkvbld")
    for jc in range(4):
        mm("B0", ps[0][:, 0:512], ones_f[0:1, 0:128], qkvb_row[:, 512 * jc:512 * (jc + 1)],
           ["onesf", "qkvbr"], True, True)
        p.emit("vector", lambda e, jc=jc: e.tensor_copy(kvbrep[:, jc, :], ps[0][:, 0:512]),
               reads=["B0"], writes=["kvbrep"])
    NTOK = T // 128
    p.emit("vector", lambda e: e.tensor_scalar(
        out=rstd1c[:, 0:NTOK], in0=mvar1[:, 1, 0:NTOK], scalar1=1e-5, scalar2=None,
        op0=OP.add, op1=OP.bypass), reads=["mvar1"], writes=["r1c"])
    p.emit("scalar", lambda e: e.activation(out=rstd1c[:, 0:NTOK], in_=rstd1c[:, 0:NTOK], func=AF.Ln),
           reads=["r1c"], writes=["r1c"])
    p.emit("scalar", lambda e: e.activation(out=rstd1c[:, 0:NTOK], in_=rstd1c[:, 0:NTOK],
                                            func=AF.Exp, scale=-0.5), reads=["r1c"], writes=["r1c"])
    p.emit("vector", lambda e: e.tensor_tensor(
        out=rr1[:, 0:NTOK], in0=rstd1c[:, 0:NTOK], in1=rstd1c[:, 0:NTOK], op=OP.mult),
        reads=["r1c"], writes=["rr1"])
    p.emit("vector", lambda e: e.tensor_tensor(
        out=rr1[:, 0:NTOK], in0=rr1[:, 0:NTOK], in1=mvar1[:, 1, 0:NTOK], op=OP.mult),
        reads=["rr1", "mvar1"], writes=["rr1"])
    p.emit("vector", lambda e: e.tensor_scalar(
        out=rr1[:, 0:NTOK], in0=rr1[:, 0:NTOK], scalar1=1e-5, scalar2=None,
        op0=OP.add, op1=OP.bypass), reads=["rr1"], writes=["rr1"])
    p.emit("scalar", lambda e: e.activation(out=rr1[:, 0:NTOK], in_=rr1[:, 0:NTOK], func=AF.Ln),
           reads=["rr1"], writes=["rr1"])
    p.emit("scalar", lambda e: e.activation(out=rr1[:, 0:NTOK], in_=rr1[:, 0:NTOK],
                                            func=AF.Exp, scale=-0.5), reads=["rr1"], writes=["rr1"])
    p.emit("vector", lambda e: e.tensor_tensor(
        out=rr1[:, 0:NTOK], in0=rr1[:, 0:NTOK], in1=rstd1c[:, 0:NTOK], op=OP.mult),
        reads=["rr1", "r1c"], writes=["rr1"])
    p.emit("vector", lambda e: e.scalar_tensor_tensor(
        out=nmrr1[:, 0:NTOK], in0=mvar1[:, 0, 0:NTOK], scalar=-1.0, in1=rr1[:, 0:NTOK],
        op0=OP.mult, op1=OP.mult), reads=["mvar1", "rr1"], writes=["nmrr1"])
    p.emit("vector", lambda e: e.scalar_tensor_tensor(
        out=nmr1c[:, 0:NTOK], in0=mvar1[:, 0, 0:NTOK], scalar=-1.0, in1=rstd1c[:, 0:NTOK],
        op0=OP.mult, op1=OP.mult), reads=["mvar1", "r1c"], writes=["nmr1c"])
    p.emit("vector", lambda e: e.memset(kv_acc[:, :, :], 0.0), writes=["kv_acc"])
    p.emit("vector", lambda e: e.memset(
        v_sb.rearrange("p (q t) o -> p q t o", t=2)[:, :, 0, 64:65], 1.0), writes=["vones_e"])
    p.emit("vector", lambda e: e.memset(
        v_sb.rearrange("p (q t) o -> p q t o", t=2)[:, :, 1, 0:1], 1.0), writes=["vones_o"])

    for tj in range(NTILES):
        db = tj % 2
        n0 = 512 * tj
        xk = "x1l%d" % db
        p.emit("sync", lambda e, n0=n0, db=db: e.dma_start(
            out=x1l[:, db, :, :],
            in_=x1t_h[n0:n0 + 512, :].rearrange("(s p) c -> p s c", p=128)),
            writes=[xk], dma="x1ld%d" % db)
        for u in range(4):
            ti = 4 * tj + u
            p.emit("scalar", lambda e, db=db, u=u, ti=ti: e.activation(
                out=tln[:, u, :], in_=x1l[:, db, u, :], func=AF.Identity,
                scale=rr1[:, ti:ti + 1], bias=nmrr1[:, ti:ti + 1]),
                reads=[xk, "rr1", "nmrr1"], writes=["tln%d" % u])
            tp_batch2([([tln[:, u, 128 * c:128 * (c + 1)] for c in range(c0, c0 + 4)],
                         ln1f[:, c0:c0 + 4, 128 * u:128 * (u + 1)],
                         ["tln%d" % u], ["ln1f"] if (u == 0 and c0 == 0) else [])
                        for c0 in (0, 4)])
        p.setw("ln1f", p.tok("vector"))
        for c in range(KC):
            p.emit("vector", lambda e, c=c: e.tensor_scalar(
                out=ln1f[:, c, :], in0=ln1f[:, c, :], scalar1=g1c[:, c:c + 1],
                scalar2=be1c[:, c:c + 1], op0=OP.mult, op1=OP.add),
                reads=["ln1f", "g1c", "be1c"], writes=["ln1f"])
        for u in range(4):
            for jc in range(4):
                bk = (u * 4 + jc) % 2
                for kc in range(KC):
                    mm("B%d" % bk, ps[bk][:, 0:512],
                       ln1f[:, kc, 128 * u:128 * (u + 1)], kvw[:, kc, 512 * jc:512 * (jc + 1)],
                       ["ln1f", "kvw"], kc == 0, kc == KC - 1)
                if jc < 2:
                    p.emit("vector", lambda e, jc=jc, bk=bk: e.tensor_tensor(
                        out=kb_sb[:, jc, :], in0=ps[bk][:, 0:512], in1=kvbrep[:, jc, :], op=OP.add),
                        reads=["B%d" % bk, "kvbrep"], writes=["kb%d" % jc])
                    p.emit("vector", lambda e, jc=jc: e.tensor_scalar(
                        out=ex_sb[:, jc, :], in0=kb_sb[:, jc, :], scalar1=0.0, scalar2=None,
                        op0=OP.min, op1=OP.bypass), reads=["kb%d" % jc], writes=["exm%d" % jc])
                    p.emit("scalar", lambda e, jc=jc: e.activation(
                        out=ex_sb[:, jc, :], in_=ex_sb[:, jc, :], func=AF.Exp),
                        reads=["exm%d" % jc], writes=["exm%d" % jc])
                    p.emit("vector", lambda e, jc=jc: e.scalar_tensor_tensor(
                        out=kf_sb[:, 512 * jc:512 * (jc + 1)], in0=kb_sb[:, jc, :], scalar=0.0,
                        in1=ex_sb[:, jc, :], op0=OP.max, op1=OP.add),
                        reads=["kb%d" % jc, "exm%d" % jc], writes=["kf%d" % jc])
                else:
                    vq = jc - 2
                    p.emit("vector", lambda e, vq=vq, bk=bk: e.tensor_tensor(
                        out=v_sb[:, 8 * vq:8 * vq + 8, :].rearrange("p (a t) o -> p a t o", t=2)[:, :, 0, 0:64],
                        in0=ps[bk][:, 0:512].rearrange("p (a t d) -> p a t d", a=4, t=2, d=64)[:, :, 0, :],
                        in1=kvbrep[:, 2 + vq, :].rearrange("p (a t d) -> p a t d", a=4, t=2, d=64)[:, :, 0, :],
                        op=OP.add),
                        reads=["B%d" % bk, "kvbrep"], writes=["ve%d" % vq])
                    p.emit("vector", lambda e, vq=vq, bk=bk: e.tensor_tensor(
                        out=v_sb[:, 8 * vq:8 * vq + 8, :].rearrange("p (a t) o -> p a t o", t=2)[:, :, 1, 1:65],
                        in0=ps[bk][:, 0:512].rearrange("p (a t d) -> p a t d", a=4, t=2, d=64)[:, :, 1, :],
                        in1=kvbrep[:, 2 + vq, :].rearrange("p (a t d) -> p a t d", a=4, t=2, d=64)[:, :, 1, :],
                        op=OP.add),
                        reads=["B%d" % bk, "kvbrep"], writes=["vo%d" % vq])
            for q in range(8):
                bkk = 2 + q % 2
                mm("B%d" % (2 + q % 2), ps[bkk][:, 0:130],
                   kf_sb[:, 128 * q:128 * (q + 1)],
                   v_sb[:, 2 * q:2 * q + 2, :].rearrange("p a o -> p (a o)"),
                   ["kf0", "kf1", "ve0", "ve1", "vo0", "vo1"], True, True)
                p.emit("vector", lambda e, q=q, bkk=bkk: e.tensor_tensor(
                    out=kv_acc[:, q, :], in0=kv_acc[:, q, :], in1=ps[bkk][:, 0:130], op=OP.add),
                    reads=["B%d" % (2 + q % 2)], writes=["kv_acc"])

    p.emit("sync", lambda e: e.dma_start(
        out=kvz_h.rearrange("p (q o) -> p q o", q=8), in_=kv_acc[:, :, :]),
        reads=["kv_acc"], dma="kvst")

    if last_phase == "p1":
        p.barrier()
        with nc.Block() as block:
            p.finalize(block)
        st.close()
        return nc

    p.barrier()
    A.reset()
    kvP2 = A.take([128, 8, 130], F32)
    p.emit("sync", lambda e: e.dma_start(
        out=kvP2[:, :, :], in_=kvz_h.rearrange("p (q o) -> p q o", q=8)),
        writes=["kvP2"], dma="kvld")
    kvz_b = A.take([128, 8, 130], BF16)
    zmat = A.take([128, KC, 16], BF16)
    sel16 = A.take([128, KC, 128], F32)[0:16, :, :]
    aB = A.take([128, 1024], F32)
    io16 = aB.bitcast(I32).rearrange("p (a b) -> p a b", b=128)[0:16, :, :]
    aqo = aB.bitcast(BF16)[:, 0:C]
    abb = A.take([128, 1024], BF16)
    ab1 = A.take([128, 1024], BF16)
    arr = A.take([128, 1024], BF16)
    abm = A.take([128, 64], F32)
    asc = A.take([128, 64], F32)
    ars = A.take([128, 64], F32)
    qw = A.take([128, KC, C], BF16)
    ow = A.take([128, KC, C], BF16)
    qbc = A.take([128, KC], F32)
    obc = A.take([128, KC], F32)
    g2cA = A.take([128, KC], F32)
    be2cA = A.take([128, KC], F32)
    x1l2 = A.take([128, 1, 4, C], F32)
    xn1_tm = A.take([128, 4, C], F32)
    tln2 = A.take([128, 4, C], BF16)
    ln1f2 = A.take([128, KC, 512], BF16)
    xn1f = A.take([128, KC, 512], F32)
    qf_fm = A.take([128, KC, 512], BF16)
    q12 = A.take([128, 1024], F32)
    q1 = q12[:, 0:512]
    q2 = q12[:, 512:1024]
    aA = q12
    den16 = A.take([128, 512], F32)[0:16, :]
    y_bf = A.take([128, KC, 512], BF16)
    x1a = A.take([128, KC, 512], F32)
    x1a_tmb = A.take([128, 4, C], F32)
    st6c = A.take([128, 2, 6], F32)

    p.emit("vector", lambda e: e.tensor_copy(kvz_b[:, :, :], kvP2[:, :, :]),
           reads=["kvP2"], writes=["kvz_b"])
    p.emit("vector", lambda e: e.memset(zmat[:, :, :], 0.0), writes=["zmat0"])
    for q in range(8):
        p.emit("vector", lambda e, q=q: e.tensor_copy(
            zmat[0:64, q, 2 * q:2 * q + 1], kvP2[0:64, q, 64:65]),
            reads=["kvP2", "zmat0"], writes=["zmat"])
        p.emit("vector", lambda e, q=q: e.tensor_copy(
            zmat[64:128, q, 2 * q + 1:2 * q + 2], kvP2[64:128, q, 65:66]),
            reads=["kvP2", "zmat0"], writes=["zmat"])
    p.emit("gpsimd", lambda e: e.iota(io16, pattern=[[-2, KC], [-1, 2], [0, 64]], base=0,
                                      channel_multiplier=1), writes=["io16"])
    p.emit("vector", lambda e: e.tensor_scalar(out=sel16, in0=io16, scalar1=0, scalar2=None,
                                               op0=OP.is_equal, op1=OP.bypass),
           reads=["io16"], writes=["sel16"])
    p.emit("gpsimd", lambda e: e.dma_start(
        out=qw[:, :, :], in_=di["L1_attn_qkv_w"].rearrange("(kc p) j -> p kc j", p=128)[:, :, 0:1024]),
        writes=["qw"], dma="qwld")
    p.emit("gpsimd", lambda e: e.dma_start(
        out=ow[:, :, :], in_=di["L1_attn_out_w"].rearrange("(kc p) j -> p kc j", p=128)),
        writes=["ow"], dma="owld")
    chan_load("qbc", qbc[:, :], di["L1_attn_qkv_b"][0:1024], C)
    chan_load("obc", obc[:, :], di["L1_attn_out_b"], C)
    chan_load("g2cA", g2cA[:, :], di["L1_attn_g2"], C)
    chan_load("be2cA", be2cA[:, :], di["L1_attn_be2"], C)

    def quant_xn1_gen(n0q, u):
        src_ap = xn1_tm[:, u, :]
        src_keys = ["xn1t%d" % u]
        n, nblk = C, C // 16
        p.emit("vector", lambda e: e.tensor_reduce(
            out=abm[:, 0:nblk], in_=src_ap.rearrange("p (b k) -> p b k", k=16),
            axis=AX.X, op=OP.max, apply_absolute_value=True),
            reads=src_keys, writes=["abm"])
        yield
        p.emit("vector", lambda e: e.tensor_scalar(
            out=asc[:, 0:nblk], in0=abm[:, 0:nblk], scalar1=6e-12, scalar2=1.0 / 6.0,
            op0=OP.max, op1=OP.mult), reads=["abm"], writes=["asc"])
        yield
        p.emit("vector", lambda e: e.reciprocal(out=ars[:, 0:nblk], in_=asc[:, 0:nblk]),
               reads=["asc"], writes=["ars"])
        yield
        p.emit("vector", lambda e: e.tensor_tensor(
            out=aA[:, 0:n].rearrange("p (b k) -> p b k", k=16),
            in0=src_ap.rearrange("p (b k) -> p b k", k=16),
            in1=ars[:, 0:nblk].unsqueeze(-1).broadcast_to([128, nblk, 16]), op=OP.mult),
            reads=src_keys + ["ars"], writes=["q1", "q2"])
        yield
        p.emit("scalar", lambda e: e.activation(out=aB[:, 0:n], in_=aA[:, 0:n], func=AF.Abs),
               reads=["q1", "q2"], writes=["io16"])
        yield
        p.emit("vector", lambda e: e.tensor_scalar(
            out=abb[:, 0:n], in0=aB[:, 0:n], scalar1=3.5, scalar2=0.5,
            op0=OP.is_gt, op1=OP.add), reads=["io16"], writes=["abb"])
        yield
        p.emit("vector", lambda e: e.tensor_scalar(
            out=ab1[:, 0:n], in0=aB[:, 0:n], scalar1=1.75, scalar2=0.5,
            op0=OP.is_gt, op1=OP.mult), reads=["io16"], writes=["ab1"])
        yield
        p.emit("vector", lambda e: e.tensor_scalar(
            out=aB[:, 0:n], in0=aB[:, 0:n], scalar1=1.75, scalar2=-1.0,
            op0=OP.is_gt, op1=OP.mult), reads=["io16"], writes=["io16"])
        yield
        p.emit("vector", lambda e: e.tensor_scalar(
            out=arr[:, 0:n], in0=abb[:, 0:n], scalar1=-0.5, scalar2=2.25,
            op0=OP.mult, op1=OP.add), reads=["abb"], writes=["arr"])
        yield
        p.emit("vector", lambda e: e.tensor_tensor(
            out=arr[:, 0:n], in0=arr[:, 0:n], in1=aB[:, 0:n], op=OP.add),
            reads=["arr", "io16"], writes=["arr"])
        yield
        p.emit("vector", lambda e: e.tensor_tensor(
            out=abb[:, 0:n], in0=ab1[:, 0:n], in1=abb[:, 0:n], op=OP.add),
            reads=["ab1", "abb"], writes=["abb"])
        yield
        p.emit("vector", lambda e: e.tensor_tensor(
            out=aA[:, 0:n], in0=aA[:, 0:n], in1=arr[:, 0:n], op=OP.mult),
            reads=["q1", "q2", "arr"], writes=["q1", "q2"])
        yield
        p.emit("vector", lambda e: e.tensor_scalar(
            out=aA[:, 0:n], in0=aA[:, 0:n], scalar1=RND_C, scalar2=RND_C,
            op0=OP.add, op1=OP.subtract), reads=["q1", "q2"], writes=["q1", "q2"])
        yield
        p.emit("vector", lambda e: e.tensor_tensor(
            out=aA[:, 0:n], in0=aA[:, 0:n], in1=abb[:, 0:n], op=OP.mult),
            reads=["q1", "q2", "abb"], writes=["q1", "q2"])
        yield
        p.emit("vector", lambda e: e.tensor_tensor(
            out=aqo.rearrange("p (b k) -> p b k", k=16),
            in0=aA[:, 0:n].rearrange("p (b k) -> p b k", k=16),
            in1=asc[:, 0:nblk].unsqueeze(-1).broadcast_to([128, nblk, 16]), op=OP.mult),
            reads=["q1", "q2", "asc"], writes=["io16"])
        yield
        p.emit("sync", lambda e, n0q=n0q, u=u: e.dma_start(
            out=xq_h[n0q + 128 * u:n0q + 128 * (u + 1), :], in_=aqo),
            reads=["io16"], dma="xqst")

    xnq_state = {"g": None, "q": []}

    def xnq_step(nops):
        for _ in range(nops):
            if xnq_state["g"] is None:
                if not xnq_state["q"]:
                    return
                n0q, uq = xnq_state["q"].pop(0)
                xnq_state["g"] = quant_xn1_gen(n0q, uq)
            try:
                next(xnq_state["g"])
            except StopIteration:
                xnq_state["g"] = None
                if xnq_state["q"]:
                    n0q, uq = xnq_state["q"].pop(0)
                    xnq_state["g"] = quant_xn1_gen(n0q, uq)
                    try:
                        next(xnq_state["g"])
                    except StopIteration:
                        xnq_state["g"] = None

    for tj in range(NTILES):
        db = tj % 2
        n0 = 512 * tj
        db = 0
        xk = "x2l%d" % db
        p.emit("sync", lambda e, n0=n0, db=db: e.dma_start(
            out=x1l2[:, db, :, :],
            in_=x1t_h[n0:n0 + 512, :].rearrange("(s p) c -> p s c", p=128)),
            writes=[xk], dma="x2ld%d" % db)
        for u in range(4):
            ti = 4 * tj + u
            p.emit("scalar", lambda e, db=db, u=u, ti=ti: e.activation(
                out=tln2[:, u, :], in_=x1l2[:, db, u, :], func=AF.Identity,
                scale=rr1[:, ti:ti + 1], bias=nmrr1[:, ti:ti + 1]),
                reads=[xk, "rr1", "nmrr1"], writes=["tl2%d" % u])
            p.emit("scalar", lambda e, db=db, u=u, ti=ti: e.activation(
                out=xn1_tm[:, u, :], in_=x1l2[:, db, u, :], func=AF.Identity,
                scale=rstd1c[:, ti:ti + 1], bias=nmr1c[:, ti:ti + 1]),
                reads=[xk, "r1c", "nmr1c"], writes=["xn1t%d" % u])
            tp_batch2([([tln2[:, u, 128 * c:128 * (c + 1)] for c in range(c0, c0 + 4)],
                         ln1f2[:, c0:c0 + 4, 128 * u:128 * (u + 1)],
                         ["tl2%d" % u], ["ln1f2"] if (u == 0 and c0 == 0) else [])
                        for c0 in (0, 4)])
            tp_batch2([([xn1_tm[:, u, 128 * c:128 * (c + 1)] for c in range(c0, c0 + 4)],
                         xn1f[:, c0:c0 + 4, 128 * u:128 * (u + 1)],
                         ["xn1t%d" % u], ["xn1f"] if (u == 0 and c0 == 0) else [])
                        for c0 in (0, 4)], f32=True)
        xnq_state["q"] = [(n0, uq) for uq in range(4)]
        p.setw("ln1f2", p.tok("vector"))
        p.setw("xn1f", p.tok("vector"))
        for c in range(KC):
            p.emit("vector", lambda e, c=c: e.tensor_scalar(
                out=ln1f2[:, c, :], in0=ln1f2[:, c, :], scalar1=g1c[:, c:c + 1],
                scalar2=be1c[:, c:c + 1], op0=OP.mult, op1=OP.add),
                reads=["ln1f2", "g1c", "be1c"], writes=["ln1f2"])
        for jc in range(KC):
            bk = jc % 2
            for kc in range(KC):
                mm("B%d" % bk, ps[bk][:, 0:512], qw[:, kc, 128 * jc:128 * (jc + 1)],
                   ln1f2[:, kc, :], ["qw", "ln1f2"], kc == 0, kc == KC - 1)
            p.emit("vector", lambda e, jc=jc, bk=bk: e.tensor_scalar(
                out=q1[:, :], in0=ps[bk][:, 0:512], scalar1=qbc[:, jc:jc + 1], scalar2=0.0,
                op0=OP.add, op1=OP.min), reads=["B%d" % bk, "qbc"], writes=["q1"])
            p.emit("scalar", lambda e: e.activation(out=q1[:, :], in_=q1[:, :], func=AF.Exp),
                   reads=["q1"], writes=["q1"])
            p.emit("vector", lambda e, jc=jc, bk=bk: e.tensor_scalar(
                out=q2[:, :], in0=ps[bk][:, 0:512], scalar1=qbc[:, jc:jc + 1], scalar2=0.0,
                op0=OP.add, op1=OP.max), reads=["B%d" % bk, "qbc"], writes=["q2"])
            p.emit("vector", lambda e, jc=jc: e.tensor_tensor(
                out=qf_fm[:, jc, :], in0=q1[:, :], in1=q2[:, :], op=OP.add),
                reads=["q1", "q2"], writes=["qf"])
        for q in range(8):
            ybk = q % 2
            mm("B%d" % (2 + ybk), ps[2 + ybk][0:64, 0:512], kvz_b[0:64, q, 0:64],
               qf_fm[0:64, q, :], ["kvz_b", "qf"], True, False)
            mm("B%d" % (2 + ybk), ps[2 + ybk][64:128, 0:512], kvz_b[64:128, q, 66:130],
               qf_fm[64:128, q, :], ["kvz_b", "qf"], False, True, isa_start=True)
            p.emit("vector", lambda e, q=q, ybk=ybk: e.tensor_copy(
                y_bf[:, q, :], ps[2 + ybk][:, 0:512]),
                reads=["B%d" % (2 + ybk)], writes=["ybf%d" % q])
            xnq_step(2)
        for kc in range(KC):
            mm("B4", ps[4][0:16, 0:512], zmat[:, kc, :], qf_fm[:, kc, :],
               ["zmat", "qf"], kc == 0, kc == KC - 1)
        p.emit("vector", lambda e: e.tensor_scalar(
            out=den16, in0=ps[4][0:16, 0:512], scalar1=1e-6, scalar2=None,
            op0=OP.max, op1=OP.bypass), reads=["B4"], writes=["den16"])
        p.emit("scalar", lambda e: e.activation(out=den16, in_=den16, func=AF.Ln),
               reads=["den16"], writes=["den16"])
        p.emit("scalar", lambda e: e.activation(out=den16, in_=den16, func=AF.Exp, scale=-1.0),
               reads=["den16"], writes=["den16"])
        for c in range(KC):
            mm("B5", ps[5][:, 0:512], sel16[:, c, :], den16, ["sel16", "den16"], True, True)
            p.emit("vector", lambda e, c=c: e.tensor_tensor(
                out=y_bf[:, c, :], in0=y_bf[:, c, :], in1=ps[5][:, 0:512], op=OP.mult),
                reads=["ybf%d" % c, "B5"], writes=["ybf%d" % c])
            xnq_step(2)
        for jc in range(KC):
            bk = jc % 2
            for kc in range(KC):
                mm("B%d" % bk, ps[bk][:, 0:512], ow[:, kc, 128 * jc:128 * (jc + 1)],
                   y_bf[:, kc, :], ["ow", "ybf%d" % kc], kc == 0, kc == KC - 1)
            p.emit("vector", lambda e, jc=jc, bk=bk: e.scalar_tensor_tensor(
                out=x1a[:, jc, :], in0=ps[bk][:, 0:512], scalar=obc[:, jc:jc + 1],
                in1=xn1f[:, jc, :], op0=OP.add, op1=OP.add),
                reads=["B%d" % bk, "obc", "xn1f"], writes=["x1a%d" % jc])
            xnq_step(4)
        p.emit("sync", lambda e, n0=n0: e.dma_start(
            out=ap_fm(x1a_f_h, 0, KC, n0, n0 + 512), in_=x1a[:, :, :]),
            reads=["x1a%d" % j for j in range(KC)], dma="xafst")
        for u in range(4):
            ti = 4 * tj + u
            tp_batch2([([x1a[:, c, 128 * u:128 * (u + 1)] for c in range(c0, c0 + 4)],
                         x1a_tmb[:, u, 128 * c0:128 * (c0 + 4)],
                         ["x1a%d" % c2 for c2 in range(c0, c0 + 4)],
                         ["xat%d" % u] if c0 == 0 else [])
                        for c0 in (0, 4)], f32=True)
            p.setw("xat%d" % u, p.tok("vector"))
            for h_i in range(2):
                p.emit("vector", lambda e, u=u, h_i=h_i: e.bn_stats(
                    st6c[:, h_i, :], x1a_tmb[:, u, 512 * h_i:512 * (h_i + 1)]),
                    reads=["xat%d" % u], writes=["st6c"])
            p.emit("vector", lambda e, ti=ti: e.bn_aggr(mvar2[:, :, ti], st6c[:, :, :]),
                   reads=["st6c"], writes=["mvar2"])
        p.emit("sync", lambda e, n0=n0: e.dma_start(
            out=x1a_t_h[n0:n0 + 512, :].rearrange("(s p) c -> p s c", p=128),
            in_=x1a_tmb[:, :, :]),
            reads=["xat0", "xat1", "xat2", "xat3"], dma="xatst")
        xnq_step(999)

    if last_phase == "p2a":
        p.barrier()
        with nc.Block() as block:
            p.finalize(block)
        st.close()
        return nc

    p.barrier()
    A.reset()
    NT2 = T // 256

    w1B = A.take([128, KC, 2048], BF16)
    w2B = A.take([128, 16, C], BF16)
    b1cB = A.take([128, 32], F32)
    b2cB = A.take([128, KC], F32)
    g2cB = A.take([128, KC], F32)
    be2cB = A.take([128, KC], F32)
    rstd2c = A.take([128, 32], F32)
    nmr2c = A.take([128, 32], F32)
    x1aB = A.take([128, 2, 2, C], F32)
    t2B = A.take([128, 2, C], BF16)
    ln2B = A.take([128, KC, 256], BF16)
    h2B = A.take([128, 16, 256], BF16)
    xafB = A.take([128, 2, KC, 256], F32)
    sevB = A.take([128, KC, 256], F32)

    chan_load("b1cB", b1cB[:, :], di["L1_attn_b1"], H)
    chan_load("b2cB", b2cB[:, :], di["L1_attn_b2"], C)
    chan_load("g2cB", g2cB[:, :], di["L1_attn_g2"], C)
    chan_load("be2cB", be2cB[:, :], di["L1_attn_be2"], C)
    NTOK2 = T // 128
    p.emit("vector", lambda e: e.tensor_scalar(
        out=rstd2c[:, 0:NTOK2], in0=mvar2[:, 1, 0:NTOK2], scalar1=1e-5, scalar2=None,
        op0=OP.add, op1=OP.bypass), reads=["mvar2"], writes=["r2c"])
    p.emit("scalar", lambda e: e.activation(out=rstd2c[:, 0:NTOK2], in_=rstd2c[:, 0:NTOK2],
                                            func=AF.Ln), reads=["r2c"], writes=["r2c"])
    p.emit("scalar", lambda e: e.activation(out=rstd2c[:, 0:NTOK2], in_=rstd2c[:, 0:NTOK2],
                                            func=AF.Exp, scale=-0.5), reads=["r2c"], writes=["r2c"])
    p.emit("vector", lambda e: e.scalar_tensor_tensor(
        out=nmr2c[:, 0:NTOK2], in0=mvar2[:, 0, 0:NTOK2], scalar=-1.0, in1=rstd2c[:, 0:NTOK2],
        op0=OP.mult, op1=OP.mult), reads=["mvar2", "r2c"], writes=["nm2c"])

    wq_emit2 = make_wq_emitter()
    wq_queue2 = [it for it in WQ_CHUNKS if it[3] != C]

    for hh in range(2):
        p.emit("gpsimd", lambda e, hh=hh: e.dma_start(
            out=w1B[:, :, :],
            in_=di["L1_attn_w1"].rearrange("(kc p) h -> p kc h", p=128)[:, :, 2048 * hh:2048 * (hh + 1)]),
            writes=["w1B"], dma="w1B")
        p.emit("gpsimd", lambda e, hh=hh: e.dma_start(
            out=w2B[:, :, :],
            in_=di["L1_attn_w2"][2048 * hh:2048 * (hh + 1), :].rearrange("(hs p) d -> p hs d", p=128)),
            writes=["w2B"], dma="w2B")
        for tj in range(NT2):
            db = tj % 2
            n0 = 256 * tj
            xk = "xaB%d" % db
            p.emit("sync", lambda e, n0=n0, db=db: e.dma_start(
                out=x1aB[:, db, :, :],
                in_=x1a_t_h[n0:n0 + 256, :].rearrange("(s p) c -> p s c", p=128)),
                writes=[xk], dma="xaB%d" % db)
            for u in range(2):
                ti = 2 * tj + u
                p.emit("scalar", lambda e, db=db, u=u, ti=ti: e.activation(
                    out=t2B[:, u, :], in_=x1aB[:, db, u, :], func=AF.Identity,
                    scale=rstd2c[:, ti:ti + 1], bias=nmr2c[:, ti:ti + 1]),
                    reads=[xk, "r2c", "nm2c"], writes=["t2B%d" % u])
                tp_batch2([([t2B[:, u, 128 * c:128 * (c + 1)] for c in range(c0, c0 + 4)],
                             ln2B[:, c0:c0 + 4, 128 * u:128 * (u + 1)],
                             ["t2B%d" % u], ["ln2B"] if (u == 0 and c0 == 0) else [])
                            for c0 in (0, 4)])
            p.setw("ln2B", p.tok("vector"))
            for c in range(KC):
                p.emit("vector", lambda e, c=c: e.tensor_scalar(
                    out=ln2B[:, c, :], in0=ln2B[:, c, :], scalar1=g2cB[:, c:c + 1],
                    scalar2=be2cB[:, c:c + 1], op0=OP.mult, op1=OP.add),
                    reads=["ln2B", "g2cB", "be2cB"], writes=["ln2B"])
            for hc in range(16):
                hbk = hc % 2
                for kc in range(KC):
                    mm("B%d" % hbk, ps[hbk][:, 0:256], w1B[:, kc, 128 * hc:128 * (hc + 1)],
                       ln2B[:, kc, :], ["w1B", "ln2B"], kc == 0, kc == KC - 1)
                p.emit("scalar", lambda e, hc=hc, hbk=hbk, hh=hh: e.activation(
                    out=h2B[:, hc, :], in_=ps[hbk][:, 0:256], func=AF.Gelu,
                    bias=b1cB[:, 16 * hh + hc:16 * hh + hc + 1]),
                    reads=["B%d" % hbk, "b1cB"], writes=["h2B%d" % hc])
            if hh == 0:
                p.emit("sync", lambda e, n0=n0, db=db: e.dma_start(
                    out=xafB[:, db, :, :], in_=ap_fm(x1a_f_h, 0, KC, n0, n0 + 256)),
                    writes=["xafB%d" % db], dma="xafB%d" % db)
            else:
                p.emit("sync", lambda e, n0=n0, db=db: e.dma_start(
                    out=xafB[:, db, :, :], in_=ap_fm(sp_h, 0, KC, n0, n0 + 256)),
                    writes=["xafB%d" % db], dma="xafB%d" % db)
            for dc in range(KC):
                sbk = 2 + dc % 4
                half = (dc // 4) * 256
                for hc in range(16):
                    mm("B%d" % (2 + dc % 4), ps[sbk][:, half:half + 256],
                       w2B[:, hc, 128 * dc:128 * (dc + 1)], h2B[:, hc, :],
                       ["w2B", "h2B%d" % hc], hc == 0, hc == 15)
                if hh == 0:
                    p.emit("vector", lambda e, dc=dc, sbk=sbk, half=half, db=db: e.tensor_tensor(
                        out=sevB[:, dc, :], in0=ps[sbk][:, half:half + 256],
                        in1=xafB[:, db, dc, :], op=OP.add),
                        reads=["B%d" % (2 + dc % 4), "xafB%d" % db], writes=["sevB%d" % dc])
                else:
                    p.emit("vector", lambda e, dc=dc, sbk=sbk, half=half, db=db: e.scalar_tensor_tensor(
                        out=sevB[:, dc, :], in0=ps[sbk][:, half:half + 256],
                        scalar=b2cB[:, dc:dc + 1], in1=xafB[:, db, dc, :],
                        op0=OP.add, op1=OP.add),
                        reads=["B%d" % (2 + dc % 4), "b2cB", "xafB%d" % db], writes=["sevB%d" % dc])
            p.emit("sync", lambda e, n0=n0, hh=hh: e.dma_start(
                out=ap_fm(sp_h if hh == 0 else s1_h, 0, KC, n0, n0 + 256), in_=sevB[:, :, :]),
                reads=["sevB%d" % d2 for d2 in range(KC)], dma="spst%d" % hh)
            if (hh > 0 or tj >= 2) and tj % 4 == 2 and wq_queue2:
                wq_emit2(wq_queue2.pop(0))

    while wq_queue2:
        wq_emit2(wq_queue2.pop(0))

    if last_phase == "p2b":
        p.barrier()
        with nc.Block() as block:
            p.finalize(block)
        st.close()
        return nc

    p.barrier()
    A.reset()
    wqg = A.take([128, KC, 2816], BF16)
    wqu = A.take([128, KC, 2816], BF16)
    wqd = A.take([128, IC, C], BF16)
    x1C = A.take([128, C], F32)
    hbuf = A.take([128, 2, I_FF], BF16)
    cA = A.take([128, 1024], F32)
    cB = A.take([128, 1024], F32)
    bb_b = A.take([128, 1024], BF16)
    b1_b = A.take([128, 1024], BF16)
    rr_b = A.take([128, 1024], BF16)
    cbm = A.take([128, 64], F32)
    csc = A.take([128, 64], F32)
    crs = A.take([128, 64], F32)
    qout = cB.bitcast(BF16)[:, 0:C]
    xqf = A.take([128, KC, 128], BF16)
    hqf = A.take([128, IC, 128], BF16)
    s1C = A.take([128, KC, 128], BF16)
    ubuf = A.take([128, 2, 512], BF16)
    sgsil = A.take([128, 2, 128], BF16)
    sgC = bb_b.rearrange("p (c o) -> p c o", o=128)
    sgtC = rr_b

    p.emit("sync", lambda e: e.dma_start(out=wqg[:, :, :], in_=wqg_h, transpose=True),
           writes=["wqg"], dma="wqtl")
    p.emit("sync", lambda e: e.dma_start(out=wqu[:, :, :], in_=wqu_h, transpose=True),
           writes=["wqu"], dma="wqtl")
    p.emit("sync", lambda e: e.dma_start(out=wqd[:, :, :], in_=wqd_h, transpose=True),
           writes=["wqd"], dma="wqtl")

    def quant_act(n, src_ap, src_keys, dst_ap, dst_keys):
        nblk = n // 16
        p.emit("vector", lambda e: e.tensor_reduce(
            out=cbm[:, 0:nblk], in_=src_ap.rearrange("p (b k) -> p b k", k=16),
            axis=AX.X, op=OP.max, apply_absolute_value=True),
            reads=src_keys, writes=["cbm"])
        yield
        p.emit("vector", lambda e: e.tensor_scalar(
            out=csc[:, 0:nblk], in0=cbm[:, 0:nblk], scalar1=6e-12, scalar2=1.0 / 6.0,
            op0=OP.max, op1=OP.mult), reads=["cbm"], writes=["csc"])
        yield
        p.emit("vector", lambda e: e.reciprocal(out=crs[:, 0:nblk], in_=csc[:, 0:nblk]),
               reads=["csc"], writes=["crs"])
        yield
        p.emit("vector", lambda e: e.tensor_tensor(
            out=cA[:, 0:n].rearrange("p (b k) -> p b k", k=16),
            in0=src_ap.rearrange("p (b k) -> p b k", k=16),
            in1=crs[:, 0:nblk].unsqueeze(-1).broadcast_to([128, nblk, 16]), op=OP.mult),
            reads=src_keys + ["crs"], writes=["cA"])
        yield
        p.emit("scalar", lambda e: e.activation(out=cB[:, 0:n], in_=cA[:, 0:n], func=AF.Abs),
               reads=["cA"], writes=["cBq"])
        yield
        p.emit("vector", lambda e: e.tensor_scalar(
            out=bb_b[:, 0:n], in0=cB[:, 0:n], scalar1=3.5, scalar2=0.5,
            op0=OP.is_gt, op1=OP.add), reads=["cBq"], writes=["bbq"])
        yield
        p.emit("vector", lambda e: e.tensor_scalar(
            out=b1_b[:, 0:n], in0=cB[:, 0:n], scalar1=1.75, scalar2=0.5,
            op0=OP.is_gt, op1=OP.mult), reads=["cBq"], writes=["b1q"])
        yield
        p.emit("vector", lambda e: e.tensor_scalar(
            out=cB[:, 0:n], in0=cB[:, 0:n], scalar1=1.75, scalar2=-1.0,
            op0=OP.is_gt, op1=OP.mult), reads=["cBq"], writes=["cBq"])
        yield

        p.emit("vector", lambda e: e.tensor_scalar(
            out=rr_b[:, 0:n], in0=bb_b[:, 0:n], scalar1=-0.5, scalar2=2.25,
            op0=OP.mult, op1=OP.add), reads=["bbq"], writes=["rrq", "sgtC"])
        yield
        p.emit("vector", lambda e: e.tensor_tensor(
            out=rr_b[:, 0:n], in0=rr_b[:, 0:n], in1=cB[:, 0:n], op=OP.add),
            reads=["rrq", "cBq"], writes=["rrq", "sgtC"])
        yield
        p.emit("vector", lambda e: e.tensor_tensor(
            out=bb_b[:, 0:n], in0=b1_b[:, 0:n], in1=bb_b[:, 0:n], op=OP.add),
            reads=["b1q", "bbq"], writes=["bbq"])
        yield
        p.emit("vector", lambda e: e.tensor_tensor(
            out=cA[:, 0:n], in0=cA[:, 0:n], in1=rr_b[:, 0:n], op=OP.mult),
            reads=["cA", "rrq"], writes=["cA"])
        yield
        p.emit("gpsimd", lambda e: e.tensor_scalar(
            out=cA[:, 0:n], in0=cA[:, 0:n], scalar1=RND_C, scalar2=RND_C,
            op0=OP.add, op1=OP.subtract), reads=["cA"], writes=["cA"])
        yield
        p.emit("gpsimd", lambda e: e.tensor_tensor(
            out=cA[:, 0:n], in0=cA[:, 0:n], in1=bb_b[:, 0:n], op=OP.mult),
            reads=["cA", "bbq"], writes=["cA"])
        yield
        p.emit("gpsimd", lambda e: e.tensor_tensor(
            out=dst_ap, in0=cA[:, 0:n].rearrange("p (b k) -> p b k", k=16),
            in1=csc[:, 0:nblk].unsqueeze(-1).broadcast_to([128, nblk, 16]), op=OP.mult),
            reads=["cA", "csc"], writes=list(dst_keys) + ["cBq"])
        yield


    def quant_h_gen(hch, db1):
        hw = min(1024, I_FF - 1024 * hch)
        for _ in quant_act(hw, hbuf[:, db1, 1024 * hch:1024 * hch + hw],
                           ["hb%d%d" % (db1, 2 * hch), "hb%d%d" % (db1, 2 * hch + 1)],
                           qout[:, 0:hw].rearrange("p (b k) -> p b k", k=16), ["qot"]):
            yield
        nb = hw // 128
        tp_batch2([([qout[:, 128 * c2:128 * (c2 + 1)] for c2 in range(c0, min(c0 + 4, nb))],
                    hqf[:, 8 * hch + c0:8 * hch + min(c0 + 4, nb), :].rearrange("p a o -> p (a o)"),
                    ["qot"], []) for c0 in range(0, nb, 4)])
        p.setw("hqf%d" % hch, p.tok("vector"))
        yield

    hq_state = {"g": None, "q": []}

    def hq_step(nops):
        for _ in range(nops):
            if hq_state["g"] is None:
                if not hq_state["q"]:
                    return
                hq_state["g"] = quant_h_gen(*hq_state["q"].pop(0))
            try:
                next(hq_state["g"])
            except StopIteration:
                hq_state["g"] = None

    NT3 = T // 128
    if DBG:
        dbg_h = nc.dram_tensor("dbgh", [T, I_FF], BF16, kind="Internal").ap()
        dbg_hq = nc.dram_tensor("dbghq", [IC * 128, T], BF16, kind="Internal").ap()
        dbg_sg = nc.dram_tensor("dbgsg", [T, C], BF16, kind="Internal").ap()
    p.emit("sync", lambda e: e.dma_start(
        out=xqf[:, :, :], in_=xq_h[0:128, :], transpose=True), writes=["xqf"], dma="xqtl")
    for t in range(NT3 + 1):
        tm1 = t - 1
        if 0 <= tm1 < NT3:
            hq_state["q"] = [(hch, tm1 % 2) for hch in range(3)]
        if t < NT3:
            db = t % 2
            for ci in range(6):
                hch, sc_i = divmod(ci, 2)
                i0 = 1024 * hch + 512 * sc_i
                iw = min(512, I_FF - i0)
                gbk = ci % 2
                for kc in range(KC):
                    mm("B%d" % gbk, ps[gbk][:, 0:iw], xqf[:, kc, :], wqg[:, kc, i0:i0 + iw],
                       ["xqf", "wqg"], kc == 0, kc == KC - 1)
                for kc in range(KC):
                    mm("B%d" % (2 + gbk), ps[2 + gbk][:, 0:iw], xqf[:, kc, :], wqu[:, kc, i0:i0 + iw],
                       ["xqf", "wqu"], kc == 0, kc == KC - 1)
                p.emit("scalar", lambda e, i0=i0, iw=iw, gbk=gbk, db=db, ci=ci: e.activation(
                    out=hbuf[:, db, i0:i0 + iw], in_=ps[gbk][:, 0:iw], func=AF.Silu),
                    reads=["B%d" % gbk], writes=["hb%d%d" % (db, ci)])
                p.emit("scalar", lambda e, iw=iw, gbk=gbk, ci=ci: e.activation(
                    out=ubuf[:, ci % 2, 0:iw], in_=ps[2 + gbk][:, 0:iw], func=AF.Identity),
                    reads=["B%d" % (2 + gbk)], writes=["ub%d" % (ci % 2)])
                p.emit("gpsimd", lambda e, i0=i0, iw=iw, db=db, ci=ci: e.tensor_tensor(
                    out=hbuf[:, db, i0:i0 + iw], in0=hbuf[:, db, i0:i0 + iw],
                    in1=ubuf[:, ci % 2, 0:iw], op=OP.mult),
                    reads=["hb%d%d" % (db, ci), "ub%d" % (ci % 2)],
                    writes=["hb%d%d" % (db, ci)])
                hq_step(8)
            if DBG:
                p.emit("sync", lambda e, t=t, db=db: e.dma_start(
                    out=dbg_h[128 * t:128 * (t + 1), :], in_=hbuf[:, db, :]),
                    writes=["hb%d%d" % (db, c9) for c9 in range(6)], dma="dbgh")
        if 0 <= tm1 < NT3:
            hq_step(999)
            if DBG:
                p.emit("sync", lambda e, tm1=tm1: e.dma_start(
                    out=dbg_hq.rearrange("(a p) n -> p a n", p=128)[:, :, 128 * tm1:128 * (tm1 + 1)],
                    in_=hqf[:, :, :]),
                    writes=["hqf0", "hqf1", "hqf2"], dma="dbghq")
            n0 = 128 * tm1
            for ic2 in range(IC):
                for dc in range(KC):
                    bk = 4 + dc % 2
                    q4 = dc // 2
                    mm("B%d" % bk, ps[bk][:, 128 * q4:128 * (q4 + 1)],
                       wqd[:, ic2, 128 * dc:128 * (dc + 1)], hqf[:, ic2, :],
                       ["wqd", "hqf%d" % min(ic2 // 8, 2)],
                       ic2 == 0 and q4 == 0, ic2 == IC - 1 and q4 == 3)
            for dc in range(KC):
                bk = 4 + dc % 2
                q4 = dc // 2
                p.emit("scalar", lambda e, bk=bk, q4=q4, dc=dc: e.activation(
                    out=sgsil[:, dc % 2, :], in_=ps[bk][:, 128 * q4:128 * (q4 + 1)],
                    func=AF.Silu),
                    reads=["B%d" % bk], writes=["sgs%d" % (dc % 2)])
                p.emit("vector", lambda e, dc=dc: e.tensor_tensor(
                    out=sgC[:, dc, :], in0=s1C[:, dc, :], in1=sgsil[:, dc % 2, :], op=OP.mult),
                    reads=["s1C0", "sgs%d" % (dc % 2)],
                    writes=["sgC%d" % dc, "bbq"])
            tp_batch2([([sgC[:, c, :] for c in range(c0, c0 + 4)],
                         sgtC[:, 128 * c0:128 * (c0 + 4)],
                         ["sgC%d" % c2 for c2 in range(c0, c0 + 4)],
                         ["sgtC"] if c0 == 0 else []) for c0 in (0, 4)])
            p.setw("sgtC", p.tok("vector"))
            if DBG:
                p.emit("sync", lambda e, n0=n0: e.dma_start(
                    out=dbg_sg[n0:n0 + 128, :], in_=sgtC[:, :]),
                    writes=["sgtC"], dma="dbgsg")
            p.emit("vector", lambda e: e.tensor_tensor(
                out=x1C[:, :], in0=x1C[:, :], in1=sgtC[:, :], op=OP.add),
                reads=["x1C0", "sgtC"], writes=["x1C0"])
            p.emit("sync", lambda e, n0=n0: e.dma_start(
                out=out_y[n0:n0 + 128, :], in_=x1C[:, :]), reads=["x1C0"], dma="yst")
        if t + 1 < NT3:
            p.emit("sync", lambda e, t=t: e.dma_start(
                out=xqf[:, :, :], in_=xq_h[128 * (t + 1):128 * (t + 2), :], transpose=True),
                writes=["xqf"], dma="xqtl")
        if t < NT3:
            n1 = 128 * t
            p.emit("gpsimd", lambda e, n1=n1: e.dma_start(
                out=s1C[:, :, :], in_=ap_fm(s1_h, 0, KC, n1, n1 + 128)),
                writes=["s1C0"], dma="s1C0")
            p.emit("sync", lambda e, n1=n1: e.dma_start(
                out=x1C[:, :], in_=x1t_h[n1:n1 + 128, :]), writes=["x1C0"], dma="x1C0")

    p.barrier()
    with nc.Block() as block:
        p.finalize(block)
    st.close()
    return nc




_NC_CACHE = {}


def _get_nc(T):
    if T not in _NC_CACHE:
        _NC_CACHE[T] = build(T)
    return _NC_CACHE[T]


def kernel(**inputs):
    x = np.ascontiguousarray(np.asarray(inputs["L0_attn_x"], dtype=np.float32))
    B, T, Cc = x.shape
    assert Cc == C
    nc = _get_nc(T)
    weights = {k: np.ascontiguousarray(np.asarray(v, dtype=np.float32))
               for k, v in inputs.items() if k != "L0_attn_x"}
    in_maps = []
    for b in range(B):
        m = dict(weights)
        m["x"] = x[b]
        in_maps.append(m)
    res = run_bass_kernel_spmd(nc, in_maps, core_ids=list(range(B)))
    return np.stack([np.asarray(r["y"]) for r in res.results], axis=0)


# version 15
# speedup vs baseline: 1.1515x; 1.1515x over previous
import numpy as np
from contextlib import ExitStack
import concourse.bass as bass
import concourse.mybir as mybir
from concourse.alu_op_type import AluOpType as OP
from concourse.bass_utils import run_bass_kernel_spmd

F32, BF16, I32, U8 = mybir.dt.float32, mybir.dt.bfloat16, mybir.dt.int32, mybir.dt.uint8
AF = mybir.ActivationFunctionType
AX = mybir.AxisListType
ENGS = ("tensor", "vector", "scalar", "gpsimd", "sync")
ESEM = {"tensor": "Tn", "vector": "V", "scalar": "S", "gpsimd": "G", "sync": "SY"}

C = 1024
KC = C // 128
H = 4096
E = 4
I_FF = 2816
IC = I_FF // 128
NHEAD, HD = 16, 64
EXP60 = float(np.exp(60.0))
RND_C = 12582912.0
DBG = False
AR_BYTES = 178176

INPUT_SHAPES = [
    ("L0_attn_time_decay", [C]), ("L0_attn_time_first", [C]), ("L0_attn_time_mix", [C]),
    ("L0_attn_Wk", [C, C]), ("L0_attn_Wv", [C, C]), ("L0_attn_Wr", [C, C]), ("L0_attn_Wo", [C, C]),
    ("L0_ffn_w1", [E, C, H]), ("L0_ffn_b1", [E, H]), ("L0_ffn_w2", [E, H, C]), ("L0_ffn_b2", [E, C]),
    ("L0_ffn_wr", [C, E]), ("L0_ffn_br", [E]),
    ("L1_attn_qkv_w", [C, 3 * C]), ("L1_attn_qkv_b", [3 * C]),
    ("L1_attn_out_w", [C, C]), ("L1_attn_out_b", [C]),
    ("L1_attn_w1", [C, H]), ("L1_attn_b1", [H]), ("L1_attn_w2", [H, C]), ("L1_attn_b2", [C]),
    ("L1_attn_g1", [C]), ("L1_attn_be1", [C]), ("L1_attn_g2", [C]), ("L1_attn_be2", [C]),
    ("L1_ffn_w_gate", [I_FF, C]), ("L1_ffn_w_up", [I_FF, C]), ("L1_ffn_w_down", [C, I_FF]),
]


class Prog:

    def __init__(self, nc, stack):
        self.nc = nc
        self.stack = stack
        self.q = {e: [] for e in ENGS}
        self.sems = {}
        self.cnt = {}
        self.last_wait = {}
        self.w = {}
        self.r = {}

    def sem(self, name):
        h = self.stack.enter_context(self.nc.semaphore(name))
        self.sems[name] = h
        self.cnt[name] = 0

    def _wait(self, eng, tok):
        if tok is None:
            return
        s, v = tok
        if s.startswith("@"):
            s = s[1:]
            v = self.cnt[s]
        key = (eng, s)
        if self.last_wait.get(key, 0) >= v:
            return
        self.last_wait[key] = v
        h = self.sems[s]
        self.q[eng].append(lambda e, h=h, v=v: e.wait_ge(h, v))

    def emit(self, eng, build, reads=(), writes=(), dma=None,
             accum_key=None, accum_first=True, accum_last=True):
        for k in reads:
            self._wait(eng, self.w.get(k))
        wkeys = list(writes)
        if accum_key is not None and accum_first:
            wkeys.append(accum_key)
        for k in wkeys:
            self._wait(eng, self.w.get(k))
            for t in self.r.get(k, ()):
                self._wait(eng, t)
        if dma:
            sname = "D_" + dma
            if sname not in self.sems:
                self.sem(sname)
            n = 16
        else:
            sname = ESEM[eng]
            n = 1
        self.cnt[sname] += n
        tok = (("@" + sname) if dma else sname, self.cnt[sname])
        h = self.sems[sname]

        def f(e, build=build, h=h, n=n):
            build(e).then_inc(h, n)

        self.q[eng].append(f)
        for k in reads:
            self.r.setdefault(k, []).append(tok)
        reg = list(writes)
        if accum_key is not None and accum_last:
            reg.append(accum_key)
        for k in reg:
            self.w[k] = tok
            self.r[k] = []
        return tok

    def tok(self, eng):
        s = ESEM[eng]
        return (s, self.cnt[s])

    def setw(self, key, tok):
        self.w[key] = tok
        self.r[key] = []

    def barrier(self):
        toks = [((("@" + s) if s.startswith("D_") else s), self.cnt[s])
                for s in self.sems if self.cnt[s] > 0]
        for eng in ENGS:
            for t in toks:
                self._wait(eng, t)

    def finalize(self, block):
        for name in ENGS:
            fns = self.q[name]
            getattr(block, name)(lambda e, fns=fns: [f(e) for f in fns])


class Arena:

    def __init__(self, t):
        self.t = t
        self.off = 0

    def reset(self):
        self.off = 0

    def take(self, shape, dt):
        assert shape[0] == 128
        elems = int(np.prod(shape[1:]))
        size = {F32: 4, BF16: 2, I32: 4, U8: 1}[dt]
        nbytes = elems * size
        off = (self.off + 63) & ~63
        self.off = off + nbytes
        assert self.off <= AR_BYTES, ("arena overflow", self.off)
        v = self.t.ap()[:, off:off + nbytes].bitcast(dt)
        if len(shape) > 2:
            dims = " ".join("d%d" % i for i in range(1, len(shape)))
            kw = {("d%d" % i): shape[i] for i in range(1, len(shape))}
            v = v.rearrange("p (%s) -> p %s" % (dims, dims), **kw)
        return v


def ap_fm(dram, c_lo, c_hi, n_lo, n_hi):
    return dram.rearrange("(c p) n -> p c n", p=128)[:, c_lo:c_hi, n_lo:n_hi]


def build(T=4096, last_phase="p2c"):
    nc = bass.Bass("TRN2", target_bir_lowering=False)
    st = ExitStack()
    p = Prog(nc, st)

    di = {"x": nc.dram_tensor("x", [T, C], F32, kind="ExternalInput").ap()}
    for n, sh in INPUT_SHAPES:
        di[n] = nc.dram_tensor(n, sh, F32, kind="ExternalInput").ap()

    out_y = nc.dram_tensor("y", [T, C], F32, kind="ExternalOutput").ap()
    s0_h = nc.dram_tensor("s0h", [C, T], BF16, kind="Internal").ap()
    xn_h = nc.dram_tensor("xnh", [C, T], BF16, kind="Internal").ap()
    x1t_h = nc.dram_tensor("x1th", [T, C], F32, kind="Internal").ap()
    x1a_t_h = nc.dram_tensor("x1ath", [T, C], F32, kind="Internal").ap()
    x1a_f_h = nc.dram_tensor("x1afh", [C, T], F32, kind="Internal").ap()
    kvz_h = nc.dram_tensor("kvzh", [128, 8 * 130], F32, kind="Internal").ap()
    xq_h = nc.dram_tensor("xqh", [T, C], BF16, kind="Internal").ap()
    sp_h = nc.dram_tensor("sph", [C, T], F32, kind="Internal").ap()
    s1_h = nc.dram_tensor("s1h", [C, T], F32, kind="Internal").ap()
    wqg_h = nc.dram_tensor("wqgh", [I_FF, C], BF16, kind="Internal").ap()
    wqu_h = nc.dram_tensor("wquh", [I_FF, C], BF16, kind="Internal").ap()
    wqd_h = nc.dram_tensor("wqdh", [C, I_FF], BF16, kind="Internal").ap()

    for s in ("V", "S", "Tn", "G", "SY"):
        p.sem(s)

    sb = {}
    def sbuf(name, shape, dt):
        sb[name] = st.enter_context(nc.sbuf_tensor(name, shape, dt))
        return sb[name]

    ident_b = sbuf("ident_b", [128, 128], BF16)
    ident_f = sbuf("ident_f", [128, 128], F32)
    stg = sbuf("stg", [128, 128], F32)
    io32 = sbuf("io32", [128, 128], I32)
    ones_b = sbuf("ones_b", [128, 128], BF16)
    ones_f = sbuf("ones_f", [128, 128], F32)
    tm_c = sbuf("tm_c", [128, KC], F32)
    dec_c = sbuf("dec_c", [128, KC], F32)
    fw_c = sbuf("fw_c", [128, KC], F32)
    mvar1 = sbuf("mvar1", [128, 2, 32], F32)
    mvar2 = sbuf("mvar2", [128, 2, 32], F32)
    rr1 = sbuf("rr1", [128, 32], F32)
    nmrr1 = sbuf("nmrr1", [128, 32], F32)
    rstd1c = sbuf("rstd1c", [128, 32], F32)
    nmr1c = sbuf("nmr1c", [128, 32], F32)
    g1c = sbuf("g1c", [128, KC], F32)
    be1c = sbuf("be1c", [128, KC], F32)
    perm_f = sbuf("perm_f", [4, 3, 4], F32)
    prow = sbuf("prow", [4, 8], F32)
    ones4f = sbuf("ones4f", [4, 4], F32)
    br4 = sbuf("br4", [4, 1], F32)

    ar_t = sbuf("AR", [128, AR_BYTES], U8)
    A = Arena(ar_t)

    ps = [st.enter_context(nc.psum_tensor("ps%d" % i, [128, 512], F32)) for i in range(6)]
    ptA = st.enter_context(nc.psum_tensor("ptA", [128, 4, 128], BF16))
    ptB = st.enter_context(nc.psum_tensor("ptB", [128, 4, 128], BF16))

    tp_state = {"i": 0}

    def tp_batch2(batches, f32=False, fbanks=(4, 5)):
        for in_aps, dst_ap, rk, wk in batches:
            n = len(in_aps)
            assert n <= 4
            wid = in_aps[0].shape[0]
            if f32:
                b = fbanks[tp_state["i"] % 2]
                bank = ps[b]
                bank3 = bank.rearrange("p (a o) -> p a o", o=128)
                run = bank3[:, 0:n, 0:wid] if wid != 128 else bank[:, 0:128 * n]
                slots = [bank[:, 128 * j:128 * (j + 1)] for j in range(n)]
                bkey = "B%d" % b
                ikey, iap = "identf", ident_f[:, :]
            else:
                bno = tp_state["i"] % 2
                bankt = (ptA, ptB)[bno]
                run = (bankt[:, 0:n, 0:wid] if wid != 128
                       else bankt.rearrange("p a o -> p (a o)")[:, 0:128 * n])
                slots = [bankt[:, j, :] for j in range(n)]
                bkey = ("BTA", "BTB")[bno]
                ikey, iap = "ident", ident_b[:, :]
            tp_state["i"] += 1
            for j, in_ap in enumerate(in_aps):
                pp = in_ap.shape[0]
                ff = int(np.prod(in_ap.shape[1:]))
                s_ = slots[j][0:ff, 0:pp]
                iap2 = iap[0:pp, 0:pp]
                p.emit("tensor",
                       lambda e, in_ap=in_ap, s_=s_, iap2=iap2: e.transpose(s_, in_ap, iap2),
                       reads=list(rk) + [ikey], accum_key=bkey,
                       accum_first=(j == 0), accum_last=(j == n - 1))
            p.emit("vector",
                   lambda e, dst_ap=dst_ap, run=run: e.tensor_copy(dst_ap, run),
                   reads=[bkey], writes=wk)

    def tp_batch(items, f32=False, fbanks=(4, 5)):
        idx = 0
        while idx < len(items):
            grp = items[idx:idx + 4]
            if f32:
                b = fbanks[tp_state["i"] % 2]
                slots = [ps[b][:, 128 * j:128 * (j + 1)] for j in range(4)]
                bkey = "B%d" % b
                ikey, iap = "identf", ident_f[:, :]
            else:
                bno = tp_state["i"] % 2
                bank = (ptA, ptB)[bno]
                slots = [bank[:, j, :] for j in range(4)]
                bkey = ("BTA", "BTB")[bno]
                ikey, iap = "ident", ident_b[:, :]
            tp_state["i"] += 1
            for j, (in_ap, out_ap, rk, wk) in enumerate(grp):
                pp, ff = in_ap.shape[0], int(np.prod(in_ap.shape[1:]))
                s_ = slots[j][0:ff, 0:pp]
                iap2 = iap[0:pp, 0:pp]
                p.emit("tensor",
                       lambda e, in_ap=in_ap, s_=s_, iap2=iap2: e.transpose(s_, in_ap, iap2),
                       reads=list(rk) + [ikey], accum_key=bkey,
                       accum_first=(j == 0), accum_last=(j == len(grp) - 1))
            for j, (in_ap, out_ap, rk, wk) in enumerate(grp):
                pp, ff = in_ap.shape[0], int(np.prod(in_ap.shape[1:]))
                s_ = slots[j][0:ff, 0:pp]
                p.emit("vector",
                       lambda e, out_ap=out_ap, s_=s_: e.tensor_copy(out_ap, s_),
                       reads=[bkey], writes=wk)
            idx += 4

    def mm(eng_key, out_ap, lhsT_ap, rhs_ap, rkeys, first, last, isa_start=None):
        s_ = first if isa_start is None else isa_start
        p.emit("tensor", lambda e: e.matmul(out_ap, lhsT_ap, rhs_ap, start=s_, stop=last,
                                            skip_group_check=True),
               reads=rkeys, accum_key=eng_key, accum_first=first, accum_last=last)

    p.emit("gpsimd", lambda e: e.memset(ones_b[:, :], 1.0), writes=["ones"])
    p.emit("gpsimd", lambda e: e.memset(ones_f[:, :], 1.0), writes=["onesf"])
    p.emit("gpsimd", lambda e: e.memset(ones4f[:, :], 1.0), writes=["ones4"])
    for r in range(3):
        p.emit("gpsimd", lambda e, r=r: e.iota(io32[:, 0:4], pattern=[[-1, 4]], base=0,
                                               channel_multiplier=1), writes=["io32"])
        p.emit("vector", lambda e, r=r: e.tensor_scalar(
            out=prow[:, 0:4], in0=io32[0:4, 0:4], scalar1=float(r + 1), scalar2=None,
            op0=OP.subtract, op1=OP.bypass), reads=["io32"], writes=["ptmp"])
        p.emit("vector", lambda e, r=r: e.tensor_scalar(
            out=prow[:, 4:8], in0=prow[:, 0:4], scalar1=-4.0, scalar2=None,
            op0=OP.is_equal, op1=OP.bypass), reads=["ptmp"], writes=["p4"])
        p.emit("vector", lambda e, r=r: e.tensor_scalar(
            out=prow[:, 0:4], in0=prow[:, 0:4], scalar1=0.0, scalar2=None,
            op0=OP.is_equal, op1=OP.bypass), reads=["ptmp"], writes=["ptmp"])
        p.emit("vector", lambda e, r=r: e.tensor_tensor(
            out=perm_f[:, r, :], in0=prow[:, 0:4], in1=prow[:, 4:8], op=OP.add),
            reads=["ptmp", "p4"], writes=["perm"])
    p.emit("gpsimd", lambda e: e.iota(io32[:, :], pattern=[[-1, 128]], base=0,
                                      channel_multiplier=1), writes=["io32"])
    p.emit("vector", lambda e: e.tensor_scalar(out=ident_b[:, :], in0=io32[:, :],
                                               scalar1=0, scalar2=None,
                                               op0=OP.is_equal, op1=OP.bypass),
           reads=["io32"], writes=["ident"])
    p.emit("vector", lambda e: e.tensor_scalar(out=ident_f[:, :], in0=io32[:, :],
                                               scalar1=0, scalar2=None,
                                               op0=OP.is_equal, op1=OP.bypass),
           reads=["io32"], writes=["identf"])

    def chan_load(dstname, dst_ap, src_ap, n):
        q = n // 128
        p.emit("sync",
               lambda e: e.dma_start(out=stg[0:q, :], in_=src_ap.rearrange("(q j) -> q j", j=128)),
               writes=["stg"], dma="chan")
        p.emit("tensor", lambda e: e.transpose(ps[5][:, 0:q], stg[0:q, :], ident_f[0:q, 0:q]),
               reads=["stg", "identf"], accum_key="B5")
        p.emit("vector", lambda e: e.tensor_copy(dst_ap, ps[5][:, 0:q]),
               reads=["B5"], writes=[dstname])
    chan_load("tm_c", tm_c[:, :], di["L0_attn_time_mix"], C)
    chan_load("dec_c", dec_c[:, :], di["L0_attn_time_decay"], C)
    chan_load("fw_c", fw_c[:, :], di["L0_attn_time_first"], C)
    p.emit("scalar", lambda e: e.activation(out=dec_c[:, :], in_=dec_c[:, :], func=AF.Exp),
           reads=["dec_c"], writes=["dec_c"])
    p.emit("scalar", lambda e: e.activation(out=dec_c[:, :], in_=dec_c[:, :], func=AF.Exp, scale=-1.0),
           reads=["dec_c"], writes=["dec_c"])
    p.emit("scalar", lambda e: e.activation(out=fw_c[:, :], in_=fw_c[:, :], func=AF.Exp),
           reads=["fw_c"], writes=["fw_c"])

    WQ_CHUNKS = ([("L1_ffn_w_gate", 0, ch, C) for ch in range(I_FF // 128)]
                 + [("L1_ffn_w_up", 1, ch, C) for ch in range(I_FF // 128)]
                 + [("L1_ffn_w_down", 2, ch, I_FF) for ch in range(C // 128)])

    def make_wq_gen(width=2816):
        emit = make_wq_emitter(width, step=True)
        return emit

    def make_wq_emitter(width=2816, step=False):
        wrow = A.take([128, width], F32)
        qA = A.take([128, width], F32)
        qB = A.take([128, width], F32)
        qC = wrow
        bmq = A.take([128, width // 16], F32)
        scq = A.take([128, width // 16], F32)
        rsq = A.take([128, width // 16], F32)
        wqout = qB.bitcast(BF16)[:, 0:width]

        def emit_chunk(item):
            srcn, dsth, ch, ncols = item
            for _ in emit_gen(item):
                pass

        def emit_gen(item):
            srcn, dsth, ch, ncols = item
            dst_h = (wqg_h, wqu_h, wqd_h)[dsth]
            nblk = ncols // 16
            n = ncols
            p.emit("sync", lambda e: e.dma_start(
                out=wrow[:, 0:ncols], in_=di[srcn][128 * ch:128 * (ch + 1), :]),
                writes=["wrow"], dma="wqld")
            yield
            p.emit("vector", lambda e: e.tensor_reduce(
                out=bmq[:, 0:nblk], in_=wrow[:, 0:n].rearrange("p (b k) -> p b k", k=16),
                axis=AX.X, op=OP.max, apply_absolute_value=True),
                reads=["wrow"], writes=["bmq"])
            yield
            p.emit("vector", lambda e: e.tensor_scalar(
                out=scq[:, 0:nblk], in0=bmq[:, 0:nblk], scalar1=6e-12, scalar2=1.0 / 6.0,
                op0=OP.max, op1=OP.mult), reads=["bmq"], writes=["scq"])
            yield
            p.emit("vector", lambda e: e.reciprocal(out=rsq[:, 0:nblk], in_=scq[:, 0:nblk]),
                   reads=["scq"], writes=["rsq"])
            yield
            p.emit("vector", lambda e: e.tensor_tensor(
                out=qA[:, 0:n].rearrange("p (b k) -> p b k", k=16),
                in0=wrow[:, 0:n].rearrange("p (b k) -> p b k", k=16),
                in1=rsq[:, 0:nblk].unsqueeze(-1).broadcast_to([128, nblk, 16]), op=OP.mult),
                reads=["wrow", "rsq"], writes=["qA"])
            yield
            p.emit("scalar", lambda e: e.activation(out=qB[:, 0:n], in_=qA[:, 0:n], func=AF.Abs),
                   reads=["qA"], writes=["qB", "wqo"])
            yield
            p.emit("vector", lambda e: e.tensor_scalar(
                out=qC[:, 0:n], in0=qB[:, 0:n], scalar1=3.5, scalar2=0.5,
                op0=OP.is_gt, op1=OP.add), reads=["qB", "wrow"], writes=["qC", "wrow"])
            yield
            p.emit("vector", lambda e: e.tensor_scalar(
                out=qB[:, 0:n], in0=qB[:, 0:n], scalar1=1.75, scalar2=0.5,
                op0=OP.is_gt, op1=OP.mult), reads=["qB"], writes=["qB"])
            yield
            p.emit("vector", lambda e: e.tensor_tensor(
                out=qB[:, 0:n], in0=qB[:, 0:n], in1=qC[:, 0:n], op=OP.add),
                reads=["qB", "qC"], writes=["qB"])
            p.emit("vector", lambda e: e.reciprocal(out=qC[:, 0:n], in_=qB[:, 0:n]),
                   reads=["qB"], writes=["qC"])
            p.emit("vector", lambda e: e.tensor_tensor(
                out=qA[:, 0:n], in0=qA[:, 0:n], in1=qC[:, 0:n], op=OP.mult),
                reads=["qA", "qC"], writes=["qA"])
            yield
            p.emit("vector", lambda e: e.tensor_scalar(
                out=qA[:, 0:n], in0=qA[:, 0:n], scalar1=RND_C, scalar2=RND_C,
                op0=OP.add, op1=OP.subtract), reads=["qA"], writes=["qA"])
            yield
            p.emit("vector", lambda e: e.tensor_tensor(
                out=qA[:, 0:n], in0=qA[:, 0:n], in1=qB[:, 0:n], op=OP.mult),
                reads=["qA", "qB"], writes=["qA"])
            yield
            p.emit("vector", lambda e: e.tensor_tensor(
                out=wqout[:, 0:n].rearrange("p (b k) -> p b k", k=16),
                in0=qA[:, 0:n].rearrange("p (b k) -> p b k", k=16),
                in1=scq[:, 0:nblk].unsqueeze(-1).broadcast_to([128, nblk, 16]), op=OP.mult),
                reads=["qA", "scq"], writes=["wqo", "qB"])
            yield
            p.emit("sync", lambda e: e.dma_start(
                out=dst_h[128 * ch:128 * (ch + 1), :], in_=wqout[:, 0:ncols]),
                reads=["wqo"], dma="wqst")
            yield
        return emit_gen if step else emit_chunk

    NT_A = 256
    NA = T // NT_A
    if last_phase == "pre0":
        p.barrier()
        with nc.Block() as block:
            p.finalize(block)
        st.close()
        return nc

    wkT = A.take([128, KC, C], BF16)
    wvT = A.take([128, KC, C], BF16)
    wrT = A.take([128, KC, C], BF16)
    woT = A.take([128, KC, C], BF16)
    wtmp = A.take([128, 4, C], BF16)
    x_tm = A.take([128, 2, 2, C], F32)
    stat6 = A.take([128, 2, 2, 6], F32)
    mvp = A.take([128, 2, 2], F32)
    rstd0 = A.take([128, 2, 1], F32)
    nmr0 = A.take([128, 2, 1], F32)
    xn_tm = A.take([128, 2, C], BF16)
    xn_fm = A.take([128, KC, NT_A + 1], BF16)
    xm_fm = A.take([128, KC, NT_A], BF16)
    d_fm = A.take([128, KC, NT_A], BF16)
    k_sb = A.take([128, KC, NT_A], F32)
    kv_sb = A.take([128, KC, NT_A], F32)
    er_sb = A.take([128, KC, NT_A], F32)
    wkvS = A.take([128, KC, NT_A + 1], F32)
    wkS = A.take([128, KC, NT_A + 1], F32)
    wkv_c = A.take([128, KC, NT_A], F32)
    wk_c = A.take([128, KC, NT_A], F32)
    rwkv_b = A.take([128, KC, NT_A], BF16)
    s_sb = A.take([128, KC, NT_A], BF16)

    def load_transposed(dstname, dst, src_name):
        src = di[src_name]
        for half in range(2):
            p.emit("gpsimd",
                   lambda e, half=half: e.dma_start(
                       out=wtmp[:, :, :],
                       in_=src.rearrange("(jc jp) d -> jp jc d", jp=128)[:, 4 * half:4 * half + 4, :]),
                   writes=["wtmp"], dma="wtmp")
            tp_batch2([([wtmp[:, jc4, 128 * dc:128 * dc + 128] for dc in range(d0, d0 + 4)],
                         dst[:, d0:d0 + 4, 128 * (4 * half + jc4):128 * (4 * half + jc4) + 128],
                         ["wtmp"], [])
                        for jc4 in range(4) for d0 in (0, 4)])
        p.setw(dstname, p.tok("vector"))
    load_transposed("wkT", wkT, "L0_attn_Wk")
    load_transposed("wvT", wvT, "L0_attn_Wv")
    load_transposed("wrT", wrT, "L0_attn_Wr")
    load_transposed("woT", woT, "L0_attn_Wo")

    if last_phase == "pre":
        p.barrier()
        with nc.Block() as block:
            p.finalize(block)
        st.close()
        return nc

    for t in range(NA):
        db = t % 2
        n0 = NT_A * t
        xk = "x_tm%d" % db
        p.emit("sync",
               lambda e, t=t, db=db: e.dma_start(
                   out=x_tm[:, db, :, :],
                   in_=di["x"][NT_A * t:NT_A * (t + 1), :].rearrange("(s p) c -> p s c", p=128)),
               writes=[xk], dma="xload%d" % db)

        for s_i in range(2):
            sk = "st%d" % s_i
            for h_i in range(2):
                p.emit("vector", lambda e, db=db, s_i=s_i, h_i=h_i: e.bn_stats(
                    stat6[:, s_i, h_i, :], x_tm[:, db, s_i, 512 * h_i:512 * (h_i + 1)]),
                    reads=[xk], writes=[sk])
            p.emit("vector", lambda e, s_i=s_i: e.bn_aggr(mvp[:, s_i, :], stat6[:, s_i, :, :]),
                   reads=[sk], writes=["mv%d" % s_i])
            p.emit("vector", lambda e, s_i=s_i: e.tensor_scalar(
                out=rstd0[:, s_i, :], in0=mvp[:, s_i, 1:2], scalar1=1e-5, scalar2=None,
                op0=OP.add, op1=OP.bypass), reads=["mv%d" % s_i], writes=["rstd%d" % s_i])
            p.emit("scalar", lambda e, s_i=s_i: e.activation(
                out=rstd0[:, s_i, :], in_=rstd0[:, s_i, :], func=AF.Ln),
                reads=["rstd%d" % s_i], writes=["rstd%d" % s_i])
            p.emit("scalar", lambda e, s_i=s_i: e.activation(
                out=rstd0[:, s_i, :], in_=rstd0[:, s_i, :], func=AF.Exp, scale=-0.5),
                reads=["rstd%d" % s_i], writes=["rstd%d" % s_i])
            p.emit("vector", lambda e, s_i=s_i: e.scalar_tensor_tensor(
                out=nmr0[:, s_i, :], in0=mvp[:, s_i, 0:1], scalar=-1.0, in1=rstd0[:, s_i, :],
                op0=OP.mult, op1=OP.mult), reads=["mv%d" % s_i, "rstd%d" % s_i],
                writes=["nmr%d" % s_i])
            p.emit("scalar", lambda e, db=db, s_i=s_i: e.activation(
                out=xn_tm[:, s_i, :], in_=x_tm[:, db, s_i, :], func=AF.Identity,
                scale=rstd0[:, s_i, :], bias=nmr0[:, s_i, :]),
                reads=[xk, "rstd%d" % s_i, "nmr%d" % s_i], writes=["xn_tm%d" % s_i])

        if t > 0:
            p.emit("vector", lambda e: e.tensor_copy(xn_fm[:, :, 0:1], xn_fm[:, :, NT_A:NT_A + 1]),
                   reads=["xn_fm"], writes=["xn_carry"])
        else:
            p.emit("vector", lambda e: e.memset(xn_fm[:, :, 0:1], 0.0), writes=["xn_carry"])

        tp_batch2([([xn_tm[:, s_i, 128 * c:128 * (c + 1)] for c in range(c0, c0 + 4)],
                     xn_fm[:, c0:c0 + 4, 1 + 128 * s_i:1 + 128 * (s_i + 1)],
                     ["xn_tm%d" % s_i],
                     ["xn_fm"] if (s_i == 0 and c0 == 0) else [])
                    for s_i in range(2) for c0 in (0, 4)])
        p.setw("xn_fm", p.tok("vector"))

        p.emit("sync", lambda e, n0=n0: e.dma_start(
            out=ap_fm(xn_h, 0, KC, n0, n0 + NT_A), in_=xn_fm[:, :, 1:NT_A + 1]),
            reads=["xn_fm"], dma="xnst")

        p.emit("vector", lambda e: e.tensor_tensor(
            out=d_fm[:, :, :], in0=xn_fm[:, :, 1:NT_A + 1], in1=xn_fm[:, :, 0:NT_A],
            op=OP.subtract), reads=["xn_fm", "xn_carry"], writes=["d_fm"])
        for c in range(KC):
            p.emit("vector", lambda e, c=c: e.scalar_tensor_tensor(
                out=xm_fm[:, c, :], in0=d_fm[:, c, :], scalar=tm_c[:, c:c + 1],
                in1=xn_fm[:, c, 0:NT_A], op0=OP.mult, op1=OP.add),
                reads=["d_fm", "tm_c", "xn_fm", "xn_carry"], writes=["xm%d" % c])

        for jc in range(KC):
            bk = jc % 2
            for kc in range(KC):
                mm("B%d" % bk, ps[bk][:, 0:NT_A], wkT[:, kc, 128 * jc:128 * (jc + 1)],
                   xm_fm[:, kc, :], ["wkT", "xm%d" % kc], kc == 0, kc == KC - 1)
            p.emit("scalar", lambda e, jc=jc, bk=bk: e.activation(
                out=k_sb[:, jc, :], in_=ps[bk][:, 0:NT_A], func=AF.Exp),
                reads=["B%d" % bk], writes=["k%d" % jc])
            for kc in range(KC):
                mm("B%d" % (2 + bk), ps[2 + bk][:, 0:NT_A], wvT[:, kc, 128 * jc:128 * (jc + 1)],
                   xm_fm[:, kc, :], ["wvT", "xm%d" % kc], kc == 0, kc == KC - 1)
            p.emit("vector", lambda e, jc=jc: e.tensor_scalar(
                out=k_sb[:, jc, :], in0=k_sb[:, jc, :], scalar1=EXP60, scalar2=None,
                op0=OP.min, op1=OP.bypass), reads=["k%d" % jc], writes=["k%d" % jc])
            p.emit("vector", lambda e, jc=jc, bk=bk: e.tensor_tensor(
                out=kv_sb[:, jc, :], in0=k_sb[:, jc, :], in1=ps[2 + bk][:, 0:NT_A], op=OP.mult),
                reads=["k%d" % jc, "B%d" % (2 + bk)], writes=["kv%d" % jc])
            for kc in range(KC):
                mm("B%d" % (4 + bk), ps[4 + bk][:, 0:NT_A], wrT[:, kc, 128 * jc:128 * (jc + 1)],
                   xm_fm[:, kc, :], ["wrT", "xm%d" % kc], kc == 0, kc == KC - 1)
            p.emit("scalar", lambda e, jc=jc, bk=bk: e.activation(
                out=er_sb[:, jc, :], in_=ps[4 + bk][:, 0:NT_A], func=AF.Exp, scale=-1.0),
                reads=["B%d" % (4 + bk)], writes=["er%d" % jc])

        if t == 0:
            p.emit("vector", lambda e: e.memset(wkvS[:, :, 0:1], 0.0), writes=["wkvS0"])
            p.emit("vector", lambda e: e.memset(wkS[:, :, 0:1], 0.0), writes=["wkS0"])
        else:
            p.emit("vector", lambda e: e.tensor_copy(wkvS[:, :, 0:1], wkvS[:, :, NT_A:NT_A + 1]),
                   reads=["wkvS"], writes=["wkvS0"])
            p.emit("vector", lambda e: e.tensor_copy(wkS[:, :, 0:1], wkS[:, :, NT_A:NT_A + 1]),
                   reads=["wkS"], writes=["wkS0"])
        for c in range(KC):
            p.emit("vector", lambda e, c=c: e.tensor_tensor_scan(
                out=wkvS[:, c, 1:NT_A + 1], data0=dec_c[:, c:c + 1].broadcast_to([128, NT_A]),
                data1=kv_sb[:, c, :], initial=wkvS[:, c, 0:1], op0=OP.mult, op1=OP.add),
                reads=["dec_c", "kv%d" % c, "wkvS0"], writes=["wkvS"])
            p.emit("vector", lambda e, c=c: e.tensor_tensor_scan(
                out=wkS[:, c, 1:NT_A + 1], data0=dec_c[:, c:c + 1].broadcast_to([128, NT_A]),
                data1=k_sb[:, c, :], initial=wkS[:, c, 0:1], op0=OP.mult, op1=OP.add),
                reads=["dec_c", "k%d" % c, "wkS0"], writes=["wkS"])
            p.emit("vector", lambda e, c=c: e.scalar_tensor_tensor(
                out=wkv_c[:, c, :], in0=kv_sb[:, c, :], scalar=fw_c[:, c:c + 1],
                in1=wkvS[:, c, 0:NT_A], op0=OP.mult, op1=OP.add),
                reads=["kv%d" % c, "fw_c", "wkvS", "wkvS0"], writes=["wkvc"])
            p.emit("vector", lambda e, c=c: e.scalar_tensor_tensor(
                out=wk_c[:, c, :], in0=k_sb[:, c, :], scalar=fw_c[:, c:c + 1],
                in1=wkS[:, c, 0:NT_A], op0=OP.mult, op1=OP.add),
                reads=["k%d" % c, "fw_c", "wkS", "wkS0"], writes=["wkc"])
        p.emit("vector", lambda e: e.tensor_scalar(
            out=wk_c[:, :, :], in0=wk_c[:, :, :], scalar1=1e-16, scalar2=None,
            op0=OP.add, op1=OP.bypass), reads=["wkc"], writes=["wkc"])

        p.emit("scalar", lambda e: e.activation(out=wk_c[:, :, :], in_=wk_c[:, :, :], func=AF.Ln),
               reads=["wkc"], writes=["wkc"])
        p.emit("scalar", lambda e: e.activation(out=er_sb[:, :, :], in_=er_sb[:, :, :],
                                                func=AF.Ln, bias=1.0),
               reads=["er%d" % c2 for c2 in range(KC)], writes=["ers"])
        p.emit("vector", lambda e: e.tensor_tensor(
            out=wk_c[:, :, :], in0=wk_c[:, :, :], in1=er_sb[:, :, :], op=OP.add),
            reads=["wkc", "ers"], writes=["wkc"])
        p.emit("scalar", lambda e: e.activation(out=wk_c[:, :, :], in_=wk_c[:, :, :],
                                                func=AF.Exp, scale=-1.0),
               reads=["wkc"], writes=["wkc"])
        p.emit("vector", lambda e: e.tensor_tensor(
            out=rwkv_b[:, :, :], in0=wkv_c[:, :, :], in1=wk_c[:, :, :], op=OP.mult),
            reads=["wkvc", "wkc"], writes=["rwkv"])

        for jc in range(KC):
            sb_ = 4 + jc % 2
            for kc in range(KC):
                mm("B%d" % sb_, ps[sb_][:, 0:NT_A], woT[:, kc, 128 * jc:128 * (jc + 1)],
                   rwkv_b[:, kc, :], ["woT", "rwkv"], kc == 0, kc == KC - 1)
            p.emit("vector", lambda e, jc=jc, sb_=sb_: e.tensor_copy(s_sb[:, jc, :], ps[sb_][:, 0:NT_A]),
                   reads=["B%d" % sb_], writes=["ssb%d" % jc])
        p.emit("sync", lambda e, n0=n0: e.dma_start(
            out=ap_fm(s0_h, 0, KC, n0, n0 + NT_A), in_=s_sb[:, :, :]),
            reads=["ssb%d" % j for j in range(KC)], dma="sst")

    if last_phase == "p0a":
        p.barrier()
        with nc.Block() as block:
            p.finalize(block)
        st.close()
        return nc

    p.barrier()
    A.reset()
    NT_B = 1024
    NB = T // NT_B

    wrb = A.take([128, KC, 4], BF16)
    wcol = A.take([128, 32, E], F32)
    wTb = A.take([128, 128], F32)[0:4, :]
    b1L0 = A.take([128, E, 32], F32)
    gb1b = A.take([128, E, 32], BF16)
    b2s = A.take([128, E, KC], F32)
    c_sb = A.take([128, E, KC], F32)
    ccT = A.take([128, KC, 128], F32)[0:4, :, :]

    xnb = A.take([128, 1, KC, NT_B], BF16)
    y_acc = A.take([128, NT_B // 128, C], F32)
    h_sb = A.take([128, 16, NT_B], BF16)
    w1b = A.take([128, 2, KC, 512], BF16)
    w2b = A.take([128, 4, 4, C], BF16)
    x_in = A.take([128, 2, C], F32)
    lg_sb = x_in[0:4, 0, 0:512]
    msk_sb = x_in[0:4, 0, 512:1024]
    gx_sb = x_in[0:4, 1, 0:512]
    x1_tm = A.take([128, C], F32)
    st6b = A.take([128, 2, 6], F32)
    g_tm = A.take([128, C], BF16)
    s_tm = A.take([128, 2, C], BF16)

    p.emit("gpsimd", lambda e: e.dma_start(
        out=wrb[:, :, :], in_=di["L0_ffn_wr"].rearrange("(c p) e -> p c e", p=128),
        allow_slow_non_contiguous=True), writes=["wrb"], dma="chg")
    p.emit("gpsimd", lambda e: e.dma_start(
        out=br4[:, :], in_=di["L0_ffn_br"].unsqueeze(-1), allow_slow_non_contiguous=True),
        writes=["br4"], dma="chg")
    for nt in range(T // 512):
        n0 = nt * 512
        xq0 = "xnq%d" % (nt % 2)
        p.emit("sync", lambda e, n0=n0, nt=nt: e.dma_start(
            out=xnb[:, 0, :, 512 * (nt % 2):512 * (nt % 2) + 512], in_=ap_fm(xn_h, 0, KC, n0, n0 + 512)),
            writes=[xq0, "xnb0"], dma="xnld%d" % (nt % 2))
        for kc in range(KC):
            mm("B0", ps[0][0:4, 0:512], wrb[:, kc, :], xnb[:, 0, kc, 512 * (nt % 2):512 * (nt % 2) + 512],
               ["wrb", xq0], kc == 0, kc == KC - 1)
        p.emit("vector", lambda e: e.tensor_scalar(
            out=lg_sb[:, :], in0=ps[0][0:4, 0:512], scalar1=br4[:, :], scalar2=None,
            op0=OP.add, op1=OP.bypass), reads=["B0", "br4"], writes=["lg"])
        for r in range(3):
            mm("B%d" % (1 + r), ps[1 + r][0:4, 0:512], perm_f[:, r, :], lg_sb[:, :],
               ["perm", "lg"], True, True)
        p.emit("vector", lambda e: e.tensor_tensor(
            out=msk_sb[:, :], in0=ps[1][0:4, 0:512], in1=lg_sb[:, :], op=OP.is_gt),
            reads=["B1", "lg"], writes=["cnt"])
        for r in (1, 2):
            p.emit("vector", lambda e, r=r: e.tensor_tensor(
                out=gx_sb, in0=ps[1 + r][0:4, 0:512], in1=lg_sb[:, :], op=OP.is_gt),
                reads=["B%d" % (1 + r), "lg"], writes=["gtt"])
            p.emit("vector", lambda e: e.tensor_tensor(
                out=msk_sb[:, :], in0=msk_sb[:, :], in1=gx_sb, op=OP.add),
                reads=["gtt", "cnt"], writes=["cnt"])
        p.emit("vector", lambda e: e.tensor_scalar(
            out=msk_sb[:, :], in0=msk_sb[:, :], scalar1=1.5, scalar2=None,
            op0=OP.is_lt, op1=OP.bypass), reads=["cnt"], writes=["cnt"])
        p.emit("vector", lambda e: e.tensor_tensor(
            out=gx_sb, in0=ps[1][0:4, 0:512], in1=lg_sb[:, :], op=OP.max),
            reads=["B1", "lg"], writes=["mx"])
        for r in (1, 2):
            p.emit("vector", lambda e, r=r: e.tensor_tensor(
                out=gx_sb, in0=gx_sb, in1=ps[1 + r][0:4, 0:512], op=OP.max),
                reads=["mx", "B%d" % (1 + r)], writes=["mx"])
        p.emit("vector", lambda e: e.tensor_tensor(
            out=lg_sb[:, :], in0=lg_sb[:, :], in1=gx_sb, op=OP.subtract),
            reads=["lg", "mx"], writes=["lg"])
        p.emit("scalar", lambda e: e.activation(out=lg_sb[:, :], in_=lg_sb[:, :], func=AF.Exp),
               reads=["lg"], writes=["lg"])
        p.emit("vector", lambda e: e.tensor_tensor(
            out=lg_sb[:, :], in0=lg_sb[:, :], in1=msk_sb[:, :], op=OP.mult),
            reads=["lg", "cnt"], writes=["lg"])
        mm("B4", ps[4][0:4, 0:512], ones4f[:, :], lg_sb[:, :], ["ones4", "lg"], True, True)
        p.emit("scalar", lambda e: e.activation(out=msk_sb[:, :], in_=ps[4][0:4, 0:512], func=AF.Ln),
               reads=["B4"], writes=["rz"])
        p.emit("scalar", lambda e: e.activation(out=msk_sb[:, :], in_=msk_sb[:, :], func=AF.Exp,
                                                scale=-1.0), reads=["rz"], writes=["rz"])
        p.emit("vector", lambda e, n0=n0: e.tensor_tensor(
            out=lg_sb[:, :], in0=lg_sb[:, :], in1=msk_sb[:, :], op=OP.mult),
            reads=["lg", "rz"], writes=["lg"])
        tp_batch2([([lg_sb[:, 128 * b4:128 * (b4 + 1)] for b4 in range(4)],
                    wcol[:, (n0 // 128):(n0 // 128) + 4, :].rearrange("p a o -> p (a o)"),
                    ["lg"], ["wcol"])], f32=True)

    for e_i in range(E):
        chan_load("b1L0", b1L0[:, e_i, :], di["L0_ffn_b1"][e_i], H)
        chan_load("b2sE", b2s[:, e_i, :], di["L0_ffn_b2"][e_i], C)
    p.emit("scalar", lambda e: e.activation(out=gb1b[:, :, :], in_=b1L0[:, :, :], func=AF.Gelu),
           reads=["b1L0"], writes=["gb1"])
    for e_i in (1, 2, 3):
        p.emit("vector", lambda e, e_i=e_i: e.tensor_tensor(
            out=b2s[:, 0, :], in0=b2s[:, 0, :], in1=b2s[:, e_i, :], op=OP.add),
            reads=["b2sE"], writes=["b2sE"])

    wq_gen0 = make_wq_gen(width=1024)
    wq_queue0 = [it for it in WQ_CHUNKS if it[3] == C]
    wq_state0 = {"g": None}

    def wq_step(nops):
        for _ in range(nops):
            if wq_state0["g"] is None:
                if not wq_queue0:
                    return
                wq_state0["g"] = wq_gen0(wq_queue0.pop(0))
            try:
                next(wq_state0["g"])
            except StopIteration:
                wq_state0["g"] = None

    for hf in range(NB):
        hf0 = hf * NT_B
        xdb = 0
        xnk = "xnb0"
        p.emit("sync", lambda e, hf0=hf0, xdb=xdb: e.dma_start(
            out=xnb[:, xdb, :, :], in_=ap_fm(xn_h, 0, KC, hf0, hf0 + NT_B)),
            writes=["xnb0", "xnb1", "xnq0", "xnq1"], dma="xnbld")

        for e_i in range(E):
            for hg in range(2):
                for hc4 in range(4):
                    hc = 4 * hg + hc4
                    wdb = hc % 2
                    w1k = "w1b%d" % wdb
                    w2k = "w2g%d" % hc4
                    p.emit("gpsimd", lambda e, e_i=e_i, hc=hc, wdb=wdb: e.dma_start(
                        out=w1b[:, wdb, :, :],
                        in_=di["L0_ffn_w1"][e_i].rearrange("(kc p) h -> p kc h", p=128)[:, :, 512 * hc:512 * (hc + 1)]),
                        writes=[w1k], dma="w1ld%d" % wdb)
                    p.emit("gpsimd", lambda e, e_i=e_i, hc=hc, hc4=hc4: e.dma_start(
                        out=w2b[:, hc4, :, :],
                        in_=di["L0_ffn_w2"][e_i][512 * hc:512 * (hc + 1), :].rearrange("(hs p) d -> p hs d", p=128)),
                        writes=[w2k], dma="w2ld%d" % hc4)
                    for ts_ in range(NT_B // 512):
                        for hs in range(4):
                            hb = (4 * ts_ + hs) % 2
                            for kc in range(KC):
                                mm("B%d" % hb, ps[hb][:, 0:512],
                                   w1b[:, wdb, kc, 128 * hs:128 * (hs + 1)],
                                   xnb[:, xdb, kc, 512 * ts_:512 * (ts_ + 1)],
                                   [w1k, xnk], kc == 0, kc == KC - 1)
                            p.emit("scalar", lambda e, hs=hs, ts_=ts_, e_i=e_i, hc=hc, hb=hb, hc4=hc4: e.activation(
                                out=h_sb[:, 4 * hc4 + hs, 512 * ts_:512 * (ts_ + 1)], in_=ps[hb][:, 0:512],
                                func=AF.Gelu, bias=b1L0[:, e_i, 4 * hc + hs:4 * hc + hs + 1]),
                                reads=["B%d" % hb, "b1L0"], writes=["hsb%d%d" % (hc4, ts_)])
                    if hf == 0:
                        for hs in range(4):
                            for dc in range(KC):
                                mm("B4", ps[4][:, 32 + 8 * e_i + dc:33 + 8 * e_i + dc],
                                   w2b[:, hc4, hs, 128 * dc:128 * (dc + 1)],
                                   gb1b[:, e_i, 4 * hc + hs:4 * hc + hs + 1],
                                   [w2k, "gb1"], (e_i == 0 and hc == 0 and hs == 0 and dc == 0), True)
                for tk in range(NT_B // 128):
                    ts_ = tk // 4
                    blk = hf0 // 128 + tk
                    for dh in range(2):
                        yb = 2 + (2 * tk + dh) % 2
                        for hx in range(16):
                            mm("B%d" % yb, ps[yb][:, 0:512],
                               h_sb[:, hx, 128 * tk:128 * (tk + 1)],
                               w2b[:, hx // 4, hx % 4, 512 * dh:512 * (dh + 1)],
                               ["w2g%d" % (hx // 4), "hsb%d%d" % (hx // 4, ts_)],
                               hx == 0, hx == 15)
                        first = (e_i == 0 and hg == 0)
                        if first:
                            p.emit("vector", lambda e, tk=tk, dh=dh, yb=yb, e_i=e_i, blk=blk: e.tensor_scalar(
                                out=y_acc[:, tk, 512 * dh:512 * (dh + 1)], in0=ps[yb][:, 0:512],
                                scalar1=wcol[:, blk, e_i:e_i + 1], scalar2=None,
                                op0=OP.mult, op1=OP.bypass),
                                reads=["B%d" % yb, "wcol"], writes=["yac%d%d" % (tk, dh)])
                        else:
                            p.emit("vector", lambda e, tk=tk, dh=dh, yb=yb, e_i=e_i, blk=blk: e.scalar_tensor_tensor(
                                out=y_acc[:, tk, 512 * dh:512 * (dh + 1)], in0=ps[yb][:, 0:512],
                                scalar=wcol[:, blk, e_i:e_i + 1],
                                in1=y_acc[:, tk, 512 * dh:512 * (dh + 1)], op0=OP.mult, op1=OP.add),
                                reads=["B%d" % yb, "wcol", "yac%d%d" % (tk, dh)],
                                writes=["yac%d%d" % (tk, dh)])
                        if e_i + hg > 0 or hf > 0:
                            wq_step(2)
        if hf == 0:
            p.emit("vector", lambda e: e.tensor_copy(
                c_sb.rearrange("p e d -> p (e d)"), ps[4][:, 32:64]),
                reads=["B4"], writes=["c_sb"])
            for e_i in (1, 2, 3):
                p.emit("vector", lambda e, e_i=e_i: e.tensor_tensor(
                    out=b2s[:, 0, :], in0=b2s[:, 0, :], in1=c_sb[:, e_i, :], op=OP.add),
                    reads=["c_sb", "b2sE"], writes=["b2sE"])
            p.emit("vector", lambda e: e.tensor_tensor(
                out=b2s[:, 0, :], in0=b2s[:, 0, :], in1=c_sb[:, 0, :], op=OP.add),
                reads=["c_sb", "b2sE"], writes=["b2sE"])
            for e_i in range(E):
                p.emit("vector", lambda e, e_i=e_i: e.tensor_tensor(
                    out=c_sb[:, e_i, :], in0=b2s[:, 0, :], in1=c_sb[:, e_i, :], op=OP.subtract),
                    reads=["b2sE", "c_sb"], writes=["c_sb"])
            tp_batch([(c_sb[:, :, dc], ccT[:, dc, :], ["c_sb"], ["ccT"]) for dc in range(KC)],
                     f32=True)

        for tk in range(NT_B // 128):
            tok_n0 = hf0 + 128 * tk
            blk = tok_n0 // 128
            xku = "x_in%d" % (tk % 2)
            p.emit("sync", lambda e, tok_n0=tok_n0, tk=tk: e.dma_start(
                out=x_in[:, tk % 2, :],
                in_=di["x"][tok_n0:tok_n0 + 128, :]),
                writes=[xku, "lg", "cnt", "mx", "gtt"], dma="xld2%d" % (tk % 2))
            p.emit("sync", lambda e, tok_n0=tok_n0, tk=tk: e.dma_start(
                out=s_tm[:, tk % 2, :], in_=s0_h[:, tok_n0:tok_n0 + 128], transpose=True),
                writes=["s_tm%d" % (tk % 2)], dma="sldT%d" % (tk % 2))
            tp_batch([(wcol[:, blk, :], wTb[:, :], ["wcol"], ["wTb"])], f32=True, fbanks=(4, 5))
            for dh in range(2):
                ccb = 4 + dh
                mm("B%d" % ccb, ps[ccb][:, 0:512], wTb[:, :],
                   ccT.rearrange("e c o -> e (c o)")[:, 512 * dh:512 * (dh + 1)],
                   ["wTb", "ccT"], True, True)
                p.emit("vector", lambda e, tk=tk, dh=dh, ccb=ccb: e.tensor_tensor(
                    out=y_acc[:, tk, 512 * dh:512 * (dh + 1)],
                    in0=y_acc[:, tk, 512 * dh:512 * (dh + 1)], in1=ps[ccb][:, 0:512], op=OP.add),
                    reads=["B%d" % ccb, "yac%d%d" % (tk, dh)], writes=["yac%d%d" % (tk, dh)])
            p.emit("scalar", lambda e, tk=tk: e.activation(
                out=g_tm[:, :], in_=y_acc[:, tk, :], func=AF.Silu),
                reads=["yac%d0" % tk, "yac%d1" % tk], writes=["g_tm"])
            p.emit("vector", lambda e, tk=tk: e.tensor_tensor(
                out=g_tm[:, :], in0=g_tm[:, :], in1=s_tm[:, tk % 2, :], op=OP.mult),
                reads=["g_tm", "s_tm%d" % (tk % 2)], writes=["g_tm"])
            p.emit("vector", lambda e, tk=tk: e.tensor_tensor(
                out=x1_tm[:, :], in0=x_in[:, tk % 2, :], in1=g_tm[:, :], op=OP.add),
                reads=[xku, "g_tm"], writes=["x1tm"])
            ti = tok_n0 // 128
            for h_i in range(2):
                p.emit("vector", lambda e, h_i=h_i: e.bn_stats(
                    st6b[:, h_i, :], x1_tm[:, 512 * h_i:512 * (h_i + 1)]),
                    reads=["x1tm"], writes=["st6b"])
            p.emit("vector", lambda e, ti=ti: e.bn_aggr(mvar1[:, :, ti], st6b[:, :, :]),
                   reads=["st6b"], writes=["mvar1"])
            p.emit("sync", lambda e, tok_n0=tok_n0: e.dma_start(
                out=x1t_h[tok_n0:tok_n0 + 128, :], in_=x1_tm[:, :]),
                reads=["x1tm"], dma="x1st")

    while wq_queue0 or wq_state0["g"] is not None:
        wq_step(4)

    if last_phase == "p0b":
        p.barrier()
        with nc.Block() as block:
            p.finalize(block)
        st.close()
        return nc

    p.barrier()
    A.reset()
    NTILES = T // 512

    kvw = A.take([128, KC, 2048], BF16)
    qkvb_row = A.take([128, 2048], F32)[0:1, :]
    kvbrep = A.take([128, 4, 512], F32)
    x1l = A.take([128, 2, 4, C], F32)
    tln = A.take([128, 4, C], BF16)
    ln1f = A.take([128, KC, 512], BF16)
    kb_sb = A.take([128, 2, 512], F32)
    ex_sb = A.take([128, 2, 512], F32)
    kf_sb = A.take([128, C], BF16)
    v_sb = A.take([128, 16, 65], BF16)
    kv_acc = A.take([128, 8, 130], F32)

    p.emit("gpsimd", lambda e: e.dma_start(
        out=kvw[:, :, :],
        in_=di["L1_attn_qkv_w"].rearrange("(kc p) j -> p kc j", p=128)[:, :, 1024:3072]),
        writes=["kvw"], dma="kvwld")
    chan_load("g1c", g1c[:, :], di["L1_attn_g1"], C)
    chan_load("be1c", be1c[:, :], di["L1_attn_be1"], C)
    p.emit("sync", lambda e: e.dma_start(
        out=qkvb_row, in_=di["L1_attn_qkv_b"][1024:3072].rearrange("(o j) -> o j", o=1)),
        writes=["qkvbr"], dma="qkvbld")
    for jc in range(4):
        mm("B0", ps[0][:, 0:512], ones_f[0:1, 0:128], qkvb_row[:, 512 * jc:512 * (jc + 1)],
           ["onesf", "qkvbr"], True, True)
        p.emit("vector", lambda e, jc=jc: e.tensor_copy(kvbrep[:, jc, :], ps[0][:, 0:512]),
               reads=["B0"], writes=["kvbrep"])
    NTOK = T // 128
    p.emit("vector", lambda e: e.tensor_scalar(
        out=rstd1c[:, 0:NTOK], in0=mvar1[:, 1, 0:NTOK], scalar1=1e-5, scalar2=None,
        op0=OP.add, op1=OP.bypass), reads=["mvar1"], writes=["r1c"])
    p.emit("scalar", lambda e: e.activation(out=rstd1c[:, 0:NTOK], in_=rstd1c[:, 0:NTOK], func=AF.Ln),
           reads=["r1c"], writes=["r1c"])
    p.emit("scalar", lambda e: e.activation(out=rstd1c[:, 0:NTOK], in_=rstd1c[:, 0:NTOK],
                                            func=AF.Exp, scale=-0.5), reads=["r1c"], writes=["r1c"])
    p.emit("vector", lambda e: e.tensor_tensor(
        out=rr1[:, 0:NTOK], in0=rstd1c[:, 0:NTOK], in1=rstd1c[:, 0:NTOK], op=OP.mult),
        reads=["r1c"], writes=["rr1"])
    p.emit("vector", lambda e: e.tensor_tensor(
        out=rr1[:, 0:NTOK], in0=rr1[:, 0:NTOK], in1=mvar1[:, 1, 0:NTOK], op=OP.mult),
        reads=["rr1", "mvar1"], writes=["rr1"])
    p.emit("vector", lambda e: e.tensor_scalar(
        out=rr1[:, 0:NTOK], in0=rr1[:, 0:NTOK], scalar1=1e-5, scalar2=None,
        op0=OP.add, op1=OP.bypass), reads=["rr1"], writes=["rr1"])
    p.emit("scalar", lambda e: e.activation(out=rr1[:, 0:NTOK], in_=rr1[:, 0:NTOK], func=AF.Ln),
           reads=["rr1"], writes=["rr1"])
    p.emit("scalar", lambda e: e.activation(out=rr1[:, 0:NTOK], in_=rr1[:, 0:NTOK],
                                            func=AF.Exp, scale=-0.5), reads=["rr1"], writes=["rr1"])
    p.emit("vector", lambda e: e.tensor_tensor(
        out=rr1[:, 0:NTOK], in0=rr1[:, 0:NTOK], in1=rstd1c[:, 0:NTOK], op=OP.mult),
        reads=["rr1", "r1c"], writes=["rr1"])
    p.emit("vector", lambda e: e.scalar_tensor_tensor(
        out=nmrr1[:, 0:NTOK], in0=mvar1[:, 0, 0:NTOK], scalar=-1.0, in1=rr1[:, 0:NTOK],
        op0=OP.mult, op1=OP.mult), reads=["mvar1", "rr1"], writes=["nmrr1"])
    p.emit("vector", lambda e: e.scalar_tensor_tensor(
        out=nmr1c[:, 0:NTOK], in0=mvar1[:, 0, 0:NTOK], scalar=-1.0, in1=rstd1c[:, 0:NTOK],
        op0=OP.mult, op1=OP.mult), reads=["mvar1", "r1c"], writes=["nmr1c"])
    p.emit("vector", lambda e: e.memset(kv_acc[:, :, :], 0.0), writes=["kv_acc"])
    p.emit("vector", lambda e: e.memset(
        v_sb.rearrange("p (q t) o -> p q t o", t=2)[:, :, 0, 64:65], 1.0), writes=["vones_e"])
    p.emit("vector", lambda e: e.memset(
        v_sb.rearrange("p (q t) o -> p q t o", t=2)[:, :, 1, 0:1], 1.0), writes=["vones_o"])

    for tj in range(NTILES):
        db = tj % 2
        n0 = 512 * tj
        xk = "x1l%d" % db
        p.emit("sync", lambda e, n0=n0, db=db: e.dma_start(
            out=x1l[:, db, :, :],
            in_=x1t_h[n0:n0 + 512, :].rearrange("(s p) c -> p s c", p=128)),
            writes=[xk], dma="x1ld%d" % db)
        for u in range(4):
            ti = 4 * tj + u
            p.emit("scalar", lambda e, db=db, u=u, ti=ti: e.activation(
                out=tln[:, u, :], in_=x1l[:, db, u, :], func=AF.Identity,
                scale=rr1[:, ti:ti + 1], bias=nmrr1[:, ti:ti + 1]),
                reads=[xk, "rr1", "nmrr1"], writes=["tln%d" % u])
            tp_batch2([([tln[:, u, 128 * c:128 * (c + 1)] for c in range(c0, c0 + 4)],
                         ln1f[:, c0:c0 + 4, 128 * u:128 * (u + 1)],
                         ["tln%d" % u], ["ln1f"] if (u == 0 and c0 == 0) else [])
                        for c0 in (0, 4)])
        p.setw("ln1f", p.tok("vector"))
        for c in range(KC):
            p.emit("vector", lambda e, c=c: e.tensor_scalar(
                out=ln1f[:, c, :], in0=ln1f[:, c, :], scalar1=g1c[:, c:c + 1],
                scalar2=be1c[:, c:c + 1], op0=OP.mult, op1=OP.add),
                reads=["ln1f", "g1c", "be1c"], writes=["ln1f"])
        for u in range(4):
            for jc in range(4):
                bk = (u * 4 + jc) % 2
                for kc in range(KC):
                    mm("B%d" % bk, ps[bk][:, 0:512],
                       ln1f[:, kc, 128 * u:128 * (u + 1)], kvw[:, kc, 512 * jc:512 * (jc + 1)],
                       ["ln1f", "kvw"], kc == 0, kc == KC - 1)
                if jc < 2:
                    p.emit("vector", lambda e, jc=jc, bk=bk: e.tensor_tensor(
                        out=kb_sb[:, jc, :], in0=ps[bk][:, 0:512], in1=kvbrep[:, jc, :], op=OP.add),
                        reads=["B%d" % bk, "kvbrep"], writes=["kb%d" % jc])
                    p.emit("vector", lambda e, jc=jc: e.tensor_scalar(
                        out=ex_sb[:, jc, :], in0=kb_sb[:, jc, :], scalar1=0.0, scalar2=None,
                        op0=OP.min, op1=OP.bypass), reads=["kb%d" % jc], writes=["exm%d" % jc])
                    p.emit("scalar", lambda e, jc=jc: e.activation(
                        out=ex_sb[:, jc, :], in_=ex_sb[:, jc, :], func=AF.Exp),
                        reads=["exm%d" % jc], writes=["exm%d" % jc])
                    p.emit("vector", lambda e, jc=jc: e.scalar_tensor_tensor(
                        out=kf_sb[:, 512 * jc:512 * (jc + 1)], in0=kb_sb[:, jc, :], scalar=0.0,
                        in1=ex_sb[:, jc, :], op0=OP.max, op1=OP.add),
                        reads=["kb%d" % jc, "exm%d" % jc], writes=["kf%d" % jc])
                else:
                    vq = jc - 2
                    p.emit("vector", lambda e, vq=vq, bk=bk: e.tensor_tensor(
                        out=v_sb[:, 8 * vq:8 * vq + 8, :].rearrange("p (a t) o -> p a t o", t=2)[:, :, 0, 0:64],
                        in0=ps[bk][:, 0:512].rearrange("p (a t d) -> p a t d", a=4, t=2, d=64)[:, :, 0, :],
                        in1=kvbrep[:, 2 + vq, :].rearrange("p (a t d) -> p a t d", a=4, t=2, d=64)[:, :, 0, :],
                        op=OP.add),
                        reads=["B%d" % bk, "kvbrep"], writes=["ve%d" % vq])
                    p.emit("vector", lambda e, vq=vq, bk=bk: e.tensor_tensor(
                        out=v_sb[:, 8 * vq:8 * vq + 8, :].rearrange("p (a t) o -> p a t o", t=2)[:, :, 1, 1:65],
                        in0=ps[bk][:, 0:512].rearrange("p (a t d) -> p a t d", a=4, t=2, d=64)[:, :, 1, :],
                        in1=kvbrep[:, 2 + vq, :].rearrange("p (a t d) -> p a t d", a=4, t=2, d=64)[:, :, 1, :],
                        op=OP.add),
                        reads=["B%d" % bk, "kvbrep"], writes=["vo%d" % vq])
            for q in range(8):
                bkk = 2 + q % 2
                mm("B%d" % (2 + q % 2), ps[bkk][:, 0:130],
                   kf_sb[:, 128 * q:128 * (q + 1)],
                   v_sb[:, 2 * q:2 * q + 2, :].rearrange("p a o -> p (a o)"),
                   ["kf0", "kf1", "ve0", "ve1", "vo0", "vo1"], True, True)
                p.emit("vector", lambda e, q=q, bkk=bkk: e.tensor_tensor(
                    out=kv_acc[:, q, :], in0=kv_acc[:, q, :], in1=ps[bkk][:, 0:130], op=OP.add),
                    reads=["B%d" % (2 + q % 2)], writes=["kv_acc"])

    p.emit("sync", lambda e: e.dma_start(
        out=kvz_h.rearrange("p (q o) -> p q o", q=8), in_=kv_acc[:, :, :]),
        reads=["kv_acc"], dma="kvst")

    if last_phase == "p1":
        p.barrier()
        with nc.Block() as block:
            p.finalize(block)
        st.close()
        return nc

    p.barrier()
    A.reset()
    kvP2 = A.take([128, 8, 130], F32)
    p.emit("sync", lambda e: e.dma_start(
        out=kvP2[:, :, :], in_=kvz_h.rearrange("p (q o) -> p q o", q=8)),
        writes=["kvP2"], dma="kvld")
    kvz_b = A.take([128, 8, 130], BF16)
    zmat = A.take([128, KC, 16], BF16)
    sel16 = A.take([128, KC, 128], F32)[0:16, :, :]
    aB = A.take([128, 1024], F32)
    io16 = aB.bitcast(I32).rearrange("p (a b) -> p a b", b=128)[0:16, :, :]
    aqo = aB.bitcast(BF16)[:, 0:C]
    abb = A.take([128, 1024], BF16)
    ab1 = A.take([128, 1024], BF16)
    arr = A.take([128, 1024], BF16)
    abm = A.take([128, 64], F32)
    asc = A.take([128, 64], F32)
    ars = A.take([128, 64], F32)
    qw = A.take([128, KC, C], BF16)
    ow = A.take([128, KC, C], BF16)
    qbc = A.take([128, KC], F32)
    obc = A.take([128, KC], F32)
    g2cA = A.take([128, KC], F32)
    be2cA = A.take([128, KC], F32)
    x1l2 = A.take([128, 1, 4, C], F32)
    xn1_tm = A.take([128, 4, C], F32)
    tln2 = A.take([128, 4, C], BF16)
    ln1f2 = A.take([128, KC, 512], BF16)
    xn1f = A.take([128, KC, 512], F32)
    qf_fm = A.take([128, KC, 512], BF16)
    q12 = A.take([128, 1024], F32)
    q1 = q12[:, 0:512]
    q2 = q12[:, 512:1024]
    aA = q12
    den16 = A.take([128, 512], F32)[0:16, :]
    y_bf = A.take([128, KC, 512], BF16)
    x1a = A.take([128, KC, 512], F32)
    x1a_tmb = A.take([128, 4, C], F32)
    st6c = A.take([128, 2, 6], F32)

    p.emit("vector", lambda e: e.tensor_copy(kvz_b[:, :, :], kvP2[:, :, :]),
           reads=["kvP2"], writes=["kvz_b"])
    p.emit("vector", lambda e: e.memset(zmat[:, :, :], 0.0), writes=["zmat0"])
    for q in range(8):
        p.emit("vector", lambda e, q=q: e.tensor_copy(
            zmat[0:64, q, 2 * q:2 * q + 1], kvP2[0:64, q, 64:65]),
            reads=["kvP2", "zmat0"], writes=["zmat"])
        p.emit("vector", lambda e, q=q: e.tensor_copy(
            zmat[64:128, q, 2 * q + 1:2 * q + 2], kvP2[64:128, q, 65:66]),
            reads=["kvP2", "zmat0"], writes=["zmat"])
    p.emit("gpsimd", lambda e: e.iota(io16, pattern=[[-2, KC], [-1, 2], [0, 64]], base=0,
                                      channel_multiplier=1), writes=["io16"])
    p.emit("vector", lambda e: e.tensor_scalar(out=sel16, in0=io16, scalar1=0, scalar2=None,
                                               op0=OP.is_equal, op1=OP.bypass),
           reads=["io16"], writes=["sel16"])
    p.emit("gpsimd", lambda e: e.dma_start(
        out=qw[:, :, :], in_=di["L1_attn_qkv_w"].rearrange("(kc p) j -> p kc j", p=128)[:, :, 0:1024]),
        writes=["qw"], dma="qwld")
    p.emit("gpsimd", lambda e: e.dma_start(
        out=ow[:, :, :], in_=di["L1_attn_out_w"].rearrange("(kc p) j -> p kc j", p=128)),
        writes=["ow"], dma="owld")
    chan_load("qbc", qbc[:, :], di["L1_attn_qkv_b"][0:1024], C)
    chan_load("obc", obc[:, :], di["L1_attn_out_b"], C)
    chan_load("g2cA", g2cA[:, :], di["L1_attn_g2"], C)
    chan_load("be2cA", be2cA[:, :], di["L1_attn_be2"], C)

    def quant_xn1_gen(n0q, u):
        src_ap = xn1_tm[:, u, :]
        src_keys = ["xn1t%d" % u]
        n, nblk = C, C // 16
        p.emit("vector", lambda e: e.tensor_reduce(
            out=abm[:, 0:nblk], in_=src_ap.rearrange("p (b k) -> p b k", k=16),
            axis=AX.X, op=OP.max, apply_absolute_value=True),
            reads=src_keys, writes=["abm"])
        yield
        p.emit("vector", lambda e: e.tensor_scalar(
            out=asc[:, 0:nblk], in0=abm[:, 0:nblk], scalar1=6e-12, scalar2=1.0 / 6.0,
            op0=OP.max, op1=OP.mult), reads=["abm"], writes=["asc"])
        yield
        p.emit("vector", lambda e: e.reciprocal(out=ars[:, 0:nblk], in_=asc[:, 0:nblk]),
               reads=["asc"], writes=["ars"])
        yield
        p.emit("vector", lambda e: e.tensor_tensor(
            out=aA[:, 0:n].rearrange("p (b k) -> p b k", k=16),
            in0=src_ap.rearrange("p (b k) -> p b k", k=16),
            in1=ars[:, 0:nblk].unsqueeze(-1).broadcast_to([128, nblk, 16]), op=OP.mult),
            reads=src_keys + ["ars"], writes=["q1", "q2"])
        yield
        p.emit("scalar", lambda e: e.activation(out=aB[:, 0:n], in_=aA[:, 0:n], func=AF.Abs),
               reads=["q1", "q2"], writes=["io16"])
        yield
        p.emit("vector", lambda e: e.tensor_scalar(
            out=abb[:, 0:n], in0=aB[:, 0:n], scalar1=3.5, scalar2=0.5,
            op0=OP.is_gt, op1=OP.add), reads=["io16"], writes=["abb"])
        yield
        p.emit("vector", lambda e: e.tensor_scalar(
            out=ab1[:, 0:n], in0=aB[:, 0:n], scalar1=1.75, scalar2=0.5,
            op0=OP.is_gt, op1=OP.mult), reads=["io16"], writes=["ab1"])
        yield
        p.emit("vector", lambda e: e.tensor_scalar(
            out=aB[:, 0:n], in0=aB[:, 0:n], scalar1=1.75, scalar2=-1.0,
            op0=OP.is_gt, op1=OP.mult), reads=["io16"], writes=["io16"])
        yield
        p.emit("vector", lambda e: e.tensor_scalar(
            out=arr[:, 0:n], in0=abb[:, 0:n], scalar1=-0.5, scalar2=2.25,
            op0=OP.mult, op1=OP.add), reads=["abb"], writes=["arr"])
        yield
        p.emit("vector", lambda e: e.tensor_tensor(
            out=arr[:, 0:n], in0=arr[:, 0:n], in1=aB[:, 0:n], op=OP.add),
            reads=["arr", "io16"], writes=["arr"])
        yield
        p.emit("vector", lambda e: e.tensor_tensor(
            out=abb[:, 0:n], in0=ab1[:, 0:n], in1=abb[:, 0:n], op=OP.add),
            reads=["ab1", "abb"], writes=["abb"])
        yield
        p.emit("vector", lambda e: e.tensor_tensor(
            out=aA[:, 0:n], in0=aA[:, 0:n], in1=arr[:, 0:n], op=OP.mult),
            reads=["q1", "q2", "arr"], writes=["q1", "q2"])
        yield
        p.emit("vector", lambda e: e.tensor_scalar(
            out=aA[:, 0:n], in0=aA[:, 0:n], scalar1=RND_C, scalar2=RND_C,
            op0=OP.add, op1=OP.subtract), reads=["q1", "q2"], writes=["q1", "q2"])
        yield
        p.emit("vector", lambda e: e.tensor_tensor(
            out=aA[:, 0:n], in0=aA[:, 0:n], in1=abb[:, 0:n], op=OP.mult),
            reads=["q1", "q2", "abb"], writes=["q1", "q2"])
        yield
        p.emit("vector", lambda e: e.tensor_tensor(
            out=aqo.rearrange("p (b k) -> p b k", k=16),
            in0=aA[:, 0:n].rearrange("p (b k) -> p b k", k=16),
            in1=asc[:, 0:nblk].unsqueeze(-1).broadcast_to([128, nblk, 16]), op=OP.mult),
            reads=["q1", "q2", "asc"], writes=["io16"])
        yield
        p.emit("sync", lambda e, n0q=n0q, u=u: e.dma_start(
            out=xq_h[n0q + 128 * u:n0q + 128 * (u + 1), :], in_=aqo),
            reads=["io16"], dma="xqst")

    xnq_state = {"g": None, "q": []}

    def xnq_step(nops):
        for _ in range(nops):
            if xnq_state["g"] is None:
                if not xnq_state["q"]:
                    return
                n0q, uq = xnq_state["q"].pop(0)
                xnq_state["g"] = quant_xn1_gen(n0q, uq)
            try:
                next(xnq_state["g"])
            except StopIteration:
                xnq_state["g"] = None
                if xnq_state["q"]:
                    n0q, uq = xnq_state["q"].pop(0)
                    xnq_state["g"] = quant_xn1_gen(n0q, uq)
                    try:
                        next(xnq_state["g"])
                    except StopIteration:
                        xnq_state["g"] = None

    for tj in range(NTILES):
        db = tj % 2
        n0 = 512 * tj
        db = 0
        xk = "x2l%d" % db
        p.emit("sync", lambda e, n0=n0, db=db: e.dma_start(
            out=x1l2[:, db, :, :],
            in_=x1t_h[n0:n0 + 512, :].rearrange("(s p) c -> p s c", p=128)),
            writes=[xk], dma="x2ld%d" % db)
        for u in range(4):
            ti = 4 * tj + u
            p.emit("scalar", lambda e, db=db, u=u, ti=ti: e.activation(
                out=tln2[:, u, :], in_=x1l2[:, db, u, :], func=AF.Identity,
                scale=rr1[:, ti:ti + 1], bias=nmrr1[:, ti:ti + 1]),
                reads=[xk, "rr1", "nmrr1"], writes=["tl2%d" % u])
            p.emit("scalar", lambda e, db=db, u=u, ti=ti: e.activation(
                out=xn1_tm[:, u, :], in_=x1l2[:, db, u, :], func=AF.Identity,
                scale=rstd1c[:, ti:ti + 1], bias=nmr1c[:, ti:ti + 1]),
                reads=[xk, "r1c", "nmr1c"], writes=["xn1t%d" % u])
            tp_batch2([([tln2[:, u, 128 * c:128 * (c + 1)] for c in range(c0, c0 + 4)],
                         ln1f2[:, c0:c0 + 4, 128 * u:128 * (u + 1)],
                         ["tl2%d" % u], ["ln1f2"] if (u == 0 and c0 == 0) else [])
                        for c0 in (0, 4)])
            tp_batch2([([xn1_tm[:, u, 128 * c:128 * (c + 1)] for c in range(c0, c0 + 4)],
                         xn1f[:, c0:c0 + 4, 128 * u:128 * (u + 1)],
                         ["xn1t%d" % u], ["xn1f"] if (u == 0 and c0 == 0) else [])
                        for c0 in (0, 4)], f32=True)
        xnq_state["q"] = [(n0, uq) for uq in range(4)]
        p.setw("ln1f2", p.tok("vector"))
        p.setw("xn1f", p.tok("vector"))
        for c in range(KC):
            p.emit("vector", lambda e, c=c: e.tensor_scalar(
                out=ln1f2[:, c, :], in0=ln1f2[:, c, :], scalar1=g1c[:, c:c + 1],
                scalar2=be1c[:, c:c + 1], op0=OP.mult, op1=OP.add),
                reads=["ln1f2", "g1c", "be1c"], writes=["ln1f2"])
        for jc in range(KC):
            bk = jc % 2
            for kc in range(KC):
                mm("B%d" % bk, ps[bk][:, 0:512], qw[:, kc, 128 * jc:128 * (jc + 1)],
                   ln1f2[:, kc, :], ["qw", "ln1f2"], kc == 0, kc == KC - 1)
            p.emit("vector", lambda e, jc=jc, bk=bk: e.tensor_scalar(
                out=q1[:, :], in0=ps[bk][:, 0:512], scalar1=qbc[:, jc:jc + 1], scalar2=0.0,
                op0=OP.add, op1=OP.min), reads=["B%d" % bk, "qbc"], writes=["q1"])
            p.emit("scalar", lambda e: e.activation(out=q1[:, :], in_=q1[:, :], func=AF.Exp),
                   reads=["q1"], writes=["q1"])
            p.emit("vector", lambda e, jc=jc, bk=bk: e.tensor_scalar(
                out=q2[:, :], in0=ps[bk][:, 0:512], scalar1=qbc[:, jc:jc + 1], scalar2=0.0,
                op0=OP.add, op1=OP.max), reads=["B%d" % bk, "qbc"], writes=["q2"])
            p.emit("vector", lambda e, jc=jc: e.tensor_tensor(
                out=qf_fm[:, jc, :], in0=q1[:, :], in1=q2[:, :], op=OP.add),
                reads=["q1", "q2"], writes=["qf"])
        for q in range(8):
            ybk = q % 2
            mm("B%d" % (2 + ybk), ps[2 + ybk][0:64, 0:512], kvz_b[0:64, q, 0:64],
               qf_fm[0:64, q, :], ["kvz_b", "qf"], True, False)
            mm("B%d" % (2 + ybk), ps[2 + ybk][64:128, 0:512], kvz_b[64:128, q, 66:130],
               qf_fm[64:128, q, :], ["kvz_b", "qf"], False, True, isa_start=True)
            p.emit("vector", lambda e, q=q, ybk=ybk: e.tensor_copy(
                y_bf[:, q, :], ps[2 + ybk][:, 0:512]),
                reads=["B%d" % (2 + ybk)], writes=["ybf%d" % q])
            xnq_step(2)
        for kc in range(KC):
            mm("B4", ps[4][0:16, 0:512], zmat[:, kc, :], qf_fm[:, kc, :],
               ["zmat", "qf"], kc == 0, kc == KC - 1)
        p.emit("vector", lambda e: e.tensor_scalar(
            out=den16, in0=ps[4][0:16, 0:512], scalar1=1e-6, scalar2=None,
            op0=OP.max, op1=OP.bypass), reads=["B4"], writes=["den16"])
        p.emit("scalar", lambda e: e.activation(out=den16, in_=den16, func=AF.Ln),
               reads=["den16"], writes=["den16"])
        p.emit("scalar", lambda e: e.activation(out=den16, in_=den16, func=AF.Exp, scale=-1.0),
               reads=["den16"], writes=["den16"])
        for c in range(KC):
            mm("B5", ps[5][:, 0:512], sel16[:, c, :], den16, ["sel16", "den16"], True, True)
            p.emit("vector", lambda e, c=c: e.tensor_tensor(
                out=y_bf[:, c, :], in0=y_bf[:, c, :], in1=ps[5][:, 0:512], op=OP.mult),
                reads=["ybf%d" % c, "B5"], writes=["ybf%d" % c])
            xnq_step(2)
        for jc in range(KC):
            bk = jc % 2
            for kc in range(KC):
                mm("B%d" % bk, ps[bk][:, 0:512], ow[:, kc, 128 * jc:128 * (jc + 1)],
                   y_bf[:, kc, :], ["ow", "ybf%d" % kc], kc == 0, kc == KC - 1)
            p.emit("vector", lambda e, jc=jc, bk=bk: e.scalar_tensor_tensor(
                out=x1a[:, jc, :], in0=ps[bk][:, 0:512], scalar=obc[:, jc:jc + 1],
                in1=xn1f[:, jc, :], op0=OP.add, op1=OP.add),
                reads=["B%d" % bk, "obc", "xn1f"], writes=["x1a%d" % jc])
            xnq_step(4)
        p.emit("sync", lambda e, n0=n0: e.dma_start(
            out=ap_fm(x1a_f_h, 0, KC, n0, n0 + 512), in_=x1a[:, :, :]),
            reads=["x1a%d" % j for j in range(KC)], dma="xafst")
        for u in range(4):
            ti = 4 * tj + u
            tp_batch2([([x1a[:, c, 128 * u:128 * (u + 1)] for c in range(c0, c0 + 4)],
                         x1a_tmb[:, u, 128 * c0:128 * (c0 + 4)],
                         ["x1a%d" % c2 for c2 in range(c0, c0 + 4)],
                         ["xat%d" % u] if c0 == 0 else [])
                        for c0 in (0, 4)], f32=True)
            p.setw("xat%d" % u, p.tok("vector"))
            for h_i in range(2):
                p.emit("vector", lambda e, u=u, h_i=h_i: e.bn_stats(
                    st6c[:, h_i, :], x1a_tmb[:, u, 512 * h_i:512 * (h_i + 1)]),
                    reads=["xat%d" % u], writes=["st6c"])
            p.emit("vector", lambda e, ti=ti: e.bn_aggr(mvar2[:, :, ti], st6c[:, :, :]),
                   reads=["st6c"], writes=["mvar2"])
        p.emit("sync", lambda e, n0=n0: e.dma_start(
            out=x1a_t_h[n0:n0 + 512, :].rearrange("(s p) c -> p s c", p=128),
            in_=x1a_tmb[:, :, :]),
            reads=["xat0", "xat1", "xat2", "xat3"], dma="xatst")
        xnq_step(999)

    if last_phase == "p2a":
        p.barrier()
        with nc.Block() as block:
            p.finalize(block)
        st.close()
        return nc

    p.barrier()
    A.reset()
    NT2 = T // 256

    w1B = A.take([128, KC, 2048], BF16)
    w2B = A.take([128, 16, C], BF16)
    b1cB = A.take([128, 32], F32)
    b2cB = A.take([128, KC], F32)
    g2cB = A.take([128, KC], F32)
    be2cB = A.take([128, KC], F32)
    rstd2c = A.take([128, 32], F32)
    nmr2c = A.take([128, 32], F32)
    x1aB = A.take([128, 2, 2, C], F32)
    t2B = A.take([128, 2, C], BF16)
    ln2B = A.take([128, KC, 256], BF16)
    h2B = A.take([128, 16, 256], BF16)
    xafB = A.take([128, 2, KC, 256], F32)
    sevB = A.take([128, KC, 256], F32)

    chan_load("b1cB", b1cB[:, :], di["L1_attn_b1"], H)
    chan_load("b2cB", b2cB[:, :], di["L1_attn_b2"], C)
    chan_load("g2cB", g2cB[:, :], di["L1_attn_g2"], C)
    chan_load("be2cB", be2cB[:, :], di["L1_attn_be2"], C)
    NTOK2 = T // 128
    p.emit("vector", lambda e: e.tensor_scalar(
        out=rstd2c[:, 0:NTOK2], in0=mvar2[:, 1, 0:NTOK2], scalar1=1e-5, scalar2=None,
        op0=OP.add, op1=OP.bypass), reads=["mvar2"], writes=["r2c"])
    p.emit("scalar", lambda e: e.activation(out=rstd2c[:, 0:NTOK2], in_=rstd2c[:, 0:NTOK2],
                                            func=AF.Ln), reads=["r2c"], writes=["r2c"])
    p.emit("scalar", lambda e: e.activation(out=rstd2c[:, 0:NTOK2], in_=rstd2c[:, 0:NTOK2],
                                            func=AF.Exp, scale=-0.5), reads=["r2c"], writes=["r2c"])
    p.emit("vector", lambda e: e.scalar_tensor_tensor(
        out=nmr2c[:, 0:NTOK2], in0=mvar2[:, 0, 0:NTOK2], scalar=-1.0, in1=rstd2c[:, 0:NTOK2],
        op0=OP.mult, op1=OP.mult), reads=["mvar2", "r2c"], writes=["nm2c"])

    wq_emit2 = make_wq_emitter()
    wq_queue2 = [it for it in WQ_CHUNKS if it[3] != C]

    for hh in range(2):
        p.emit("gpsimd", lambda e, hh=hh: e.dma_start(
            out=w1B[:, :, :],
            in_=di["L1_attn_w1"].rearrange("(kc p) h -> p kc h", p=128)[:, :, 2048 * hh:2048 * (hh + 1)]),
            writes=["w1B"], dma="w1B")
        p.emit("gpsimd", lambda e, hh=hh: e.dma_start(
            out=w2B[:, :, :],
            in_=di["L1_attn_w2"][2048 * hh:2048 * (hh + 1), :].rearrange("(hs p) d -> p hs d", p=128)),
            writes=["w2B"], dma="w2B")
        for tj in range(NT2):
            db = tj % 2
            n0 = 256 * tj
            xk = "xaB%d" % db
            p.emit("sync", lambda e, n0=n0, db=db: e.dma_start(
                out=x1aB[:, db, :, :],
                in_=x1a_t_h[n0:n0 + 256, :].rearrange("(s p) c -> p s c", p=128)),
                writes=[xk], dma="xaB%d" % db)
            for u in range(2):
                ti = 2 * tj + u
                p.emit("scalar", lambda e, db=db, u=u, ti=ti: e.activation(
                    out=t2B[:, u, :], in_=x1aB[:, db, u, :], func=AF.Identity,
                    scale=rstd2c[:, ti:ti + 1], bias=nmr2c[:, ti:ti + 1]),
                    reads=[xk, "r2c", "nm2c"], writes=["t2B%d" % u])
                tp_batch2([([t2B[:, u, 128 * c:128 * (c + 1)] for c in range(c0, c0 + 4)],
                             ln2B[:, c0:c0 + 4, 128 * u:128 * (u + 1)],
                             ["t2B%d" % u], ["ln2B"] if (u == 0 and c0 == 0) else [])
                            for c0 in (0, 4)])
            p.setw("ln2B", p.tok("vector"))
            for c in range(KC):
                p.emit("vector", lambda e, c=c: e.tensor_scalar(
                    out=ln2B[:, c, :], in0=ln2B[:, c, :], scalar1=g2cB[:, c:c + 1],
                    scalar2=be2cB[:, c:c + 1], op0=OP.mult, op1=OP.add),
                    reads=["ln2B", "g2cB", "be2cB"], writes=["ln2B"])
            for hc in range(16):
                hbk = hc % 2
                for kc in range(KC):
                    mm("B%d" % hbk, ps[hbk][:, 0:256], w1B[:, kc, 128 * hc:128 * (hc + 1)],
                       ln2B[:, kc, :], ["w1B", "ln2B"], kc == 0, kc == KC - 1)
                p.emit("scalar", lambda e, hc=hc, hbk=hbk, hh=hh: e.activation(
                    out=h2B[:, hc, :], in_=ps[hbk][:, 0:256], func=AF.Gelu,
                    bias=b1cB[:, 16 * hh + hc:16 * hh + hc + 1]),
                    reads=["B%d" % hbk, "b1cB"], writes=["h2B%d" % hc])
            if hh == 0:
                p.emit("sync", lambda e, n0=n0, db=db: e.dma_start(
                    out=xafB[:, db, :, :], in_=ap_fm(x1a_f_h, 0, KC, n0, n0 + 256)),
                    writes=["xafB%d" % db], dma="xafB%d" % db)
            else:
                p.emit("sync", lambda e, n0=n0, db=db: e.dma_start(
                    out=xafB[:, db, :, :], in_=ap_fm(sp_h, 0, KC, n0, n0 + 256)),
                    writes=["xafB%d" % db], dma="xafB%d" % db)
            for dc in range(KC):
                sbk = 2 + dc % 4
                half = (dc // 4) * 256
                for hc in range(16):
                    mm("B%d" % (2 + dc % 4), ps[sbk][:, half:half + 256],
                       w2B[:, hc, 128 * dc:128 * (dc + 1)], h2B[:, hc, :],
                       ["w2B", "h2B%d" % hc], hc == 0, hc == 15)
                if hh == 0:
                    p.emit("vector", lambda e, dc=dc, sbk=sbk, half=half, db=db: e.tensor_tensor(
                        out=sevB[:, dc, :], in0=ps[sbk][:, half:half + 256],
                        in1=xafB[:, db, dc, :], op=OP.add),
                        reads=["B%d" % (2 + dc % 4), "xafB%d" % db], writes=["sevB%d" % dc])
                else:
                    p.emit("vector", lambda e, dc=dc, sbk=sbk, half=half, db=db: e.scalar_tensor_tensor(
                        out=sevB[:, dc, :], in0=ps[sbk][:, half:half + 256],
                        scalar=b2cB[:, dc:dc + 1], in1=xafB[:, db, dc, :],
                        op0=OP.add, op1=OP.add),
                        reads=["B%d" % (2 + dc % 4), "b2cB", "xafB%d" % db], writes=["sevB%d" % dc])
            p.emit("sync", lambda e, n0=n0, hh=hh: e.dma_start(
                out=ap_fm(sp_h if hh == 0 else s1_h, 0, KC, n0, n0 + 256), in_=sevB[:, :, :]),
                reads=["sevB%d" % d2 for d2 in range(KC)], dma="spst%d" % hh)
            if (hh > 0 or tj >= 2) and tj % 4 == 2 and wq_queue2:
                wq_emit2(wq_queue2.pop(0))

    while wq_queue2:
        wq_emit2(wq_queue2.pop(0))

    if last_phase == "p2b":
        p.barrier()
        with nc.Block() as block:
            p.finalize(block)
        st.close()
        return nc

    p.barrier()
    A.reset()
    wqg = A.take([128, KC, 2816], BF16)
    wqu = A.take([128, KC, 2816], BF16)
    wqd = A.take([128, IC, C], BF16)
    x1C = A.take([128, C], F32)
    hbuf = A.take([128, 2, I_FF], BF16)
    cA = A.take([128, 1024], F32)
    cB = A.take([128, 1024], F32)
    bb_b = A.take([128, 1024], BF16)
    b1_b = A.take([128, 1024], BF16)
    rr_b = A.take([128, 1024], BF16)
    cbm = A.take([128, 64], F32)
    csc = A.take([128, 64], F32)
    crs = A.take([128, 64], F32)
    qout = cB.bitcast(BF16)[:, 0:C]
    xqf = A.take([128, KC, 128], BF16)
    hqf = A.take([128, IC, 128], BF16)
    s1C = A.take([128, KC, 128], BF16)
    ubuf = A.take([128, 2, 512], BF16)
    sgsil = A.take([128, 2, 128], BF16)
    sgC = bb_b.rearrange("p (c o) -> p c o", o=128)
    sgtC = rr_b

    p.emit("sync", lambda e: e.dma_start(out=wqg[:, :, :], in_=wqg_h, transpose=True),
           writes=["wqg"], dma="wqtl")
    p.emit("sync", lambda e: e.dma_start(out=wqu[:, :, :], in_=wqu_h, transpose=True),
           writes=["wqu"], dma="wqtl")
    p.emit("sync", lambda e: e.dma_start(out=wqd[:, :, :], in_=wqd_h, transpose=True),
           writes=["wqd"], dma="wqtl")

    def quant_act(n, src_ap, src_keys, dst_ap, dst_keys):
        nblk = n // 16
        p.emit("vector", lambda e: e.tensor_reduce(
            out=cbm[:, 0:nblk], in_=src_ap.rearrange("p (b k) -> p b k", k=16),
            axis=AX.X, op=OP.max, apply_absolute_value=True),
            reads=src_keys, writes=["cbm"])
        yield
        p.emit("vector", lambda e: e.tensor_scalar(
            out=csc[:, 0:nblk], in0=cbm[:, 0:nblk], scalar1=6e-12, scalar2=1.0 / 6.0,
            op0=OP.max, op1=OP.mult), reads=["cbm"], writes=["csc"])
        yield
        p.emit("vector", lambda e: e.reciprocal(out=crs[:, 0:nblk], in_=csc[:, 0:nblk]),
               reads=["csc"], writes=["crs"])
        yield
        p.emit("vector", lambda e: e.tensor_tensor(
            out=cA[:, 0:n].rearrange("p (b k) -> p b k", k=16),
            in0=src_ap.rearrange("p (b k) -> p b k", k=16),
            in1=crs[:, 0:nblk].unsqueeze(-1).broadcast_to([128, nblk, 16]), op=OP.mult),
            reads=src_keys + ["crs"], writes=["cA"])
        yield
        p.emit("scalar", lambda e: e.activation(out=cB[:, 0:n], in_=cA[:, 0:n], func=AF.Abs),
               reads=["cA"], writes=["cBq"])
        yield
        p.emit("vector", lambda e: e.tensor_scalar(
            out=bb_b[:, 0:n], in0=cB[:, 0:n], scalar1=3.5, scalar2=0.5,
            op0=OP.is_gt, op1=OP.add), reads=["cBq"], writes=["bbq"])
        yield
        p.emit("vector", lambda e: e.tensor_scalar(
            out=b1_b[:, 0:n], in0=cB[:, 0:n], scalar1=1.75, scalar2=0.5,
            op0=OP.is_gt, op1=OP.mult), reads=["cBq"], writes=["b1q"])
        yield
        p.emit("vector", lambda e: e.tensor_scalar(
            out=cB[:, 0:n], in0=cB[:, 0:n], scalar1=1.75, scalar2=-1.0,
            op0=OP.is_gt, op1=OP.mult), reads=["cBq"], writes=["cBq"])
        yield

        p.emit("vector", lambda e: e.tensor_scalar(
            out=rr_b[:, 0:n], in0=bb_b[:, 0:n], scalar1=-0.5, scalar2=2.25,
            op0=OP.mult, op1=OP.add), reads=["bbq"], writes=["rrq", "sgtC"])
        yield
        p.emit("vector", lambda e: e.tensor_tensor(
            out=rr_b[:, 0:n], in0=rr_b[:, 0:n], in1=cB[:, 0:n], op=OP.add),
            reads=["rrq", "cBq"], writes=["rrq", "sgtC"])
        yield
        p.emit("vector", lambda e: e.tensor_tensor(
            out=bb_b[:, 0:n], in0=b1_b[:, 0:n], in1=bb_b[:, 0:n], op=OP.add),
            reads=["b1q", "bbq"], writes=["bbq"])
        yield
        p.emit("vector", lambda e: e.tensor_tensor(
            out=cA[:, 0:n], in0=cA[:, 0:n], in1=rr_b[:, 0:n], op=OP.mult),
            reads=["cA", "rrq"], writes=["cA"])
        yield
        p.emit("vector", lambda e: e.tensor_scalar(
            out=cA[:, 0:n], in0=cA[:, 0:n], scalar1=RND_C, scalar2=RND_C,
            op0=OP.add, op1=OP.subtract), reads=["cA"], writes=["cA"])
        yield
        p.emit("vector", lambda e: e.tensor_tensor(
            out=cA[:, 0:n], in0=cA[:, 0:n], in1=bb_b[:, 0:n], op=OP.mult),
            reads=["cA", "bbq"], writes=["cA"])
        yield
        p.emit("vector", lambda e: e.tensor_tensor(
            out=dst_ap, in0=cA[:, 0:n].rearrange("p (b k) -> p b k", k=16),
            in1=csc[:, 0:nblk].unsqueeze(-1).broadcast_to([128, nblk, 16]), op=OP.mult),
            reads=["cA", "csc"], writes=list(dst_keys) + ["cBq"])
        yield


    def quant_h_gen(hch, db1):
        hw = min(1024, I_FF - 1024 * hch)
        for _ in quant_act(hw, hbuf[:, db1, 1024 * hch:1024 * hch + hw],
                           ["hb%d%d" % (db1, 2 * hch), "hb%d%d" % (db1, 2 * hch + 1)],
                           qout[:, 0:hw].rearrange("p (b k) -> p b k", k=16), ["qot"]):
            yield
        nb = hw // 128
        tp_batch2([([qout[:, 128 * c2:128 * (c2 + 1)] for c2 in range(c0, min(c0 + 4, nb))],
                    hqf[:, 8 * hch + c0:8 * hch + min(c0 + 4, nb), :].rearrange("p a o -> p (a o)"),
                    ["qot"], []) for c0 in range(0, nb, 4)])
        p.setw("hqf%d" % hch, p.tok("vector"))
        yield

    hq_state = {"g": None, "q": [], "done": 0}

    def hq_step(nops):
        for _ in range(nops):
            if hq_state["g"] is None:
                if not hq_state["q"]:
                    return
                hq_state["g"] = quant_h_gen(*hq_state["q"].pop(0))
            try:
                next(hq_state["g"])
            except StopIteration:
                hq_state["g"] = None
                hq_state["done"] += 1

    def hq_flush_until(k):
        while hq_state["done"] < k and (hq_state["g"] is not None or hq_state["q"]):
            hq_step(1)

    NT3 = T // 128
    if DBG:
        dbg_h = nc.dram_tensor("dbgh", [T, I_FF], BF16, kind="Internal").ap()
        dbg_hq = nc.dram_tensor("dbghq", [IC * 128, T], BF16, kind="Internal").ap()
        dbg_sg = nc.dram_tensor("dbgsg", [T, C], BF16, kind="Internal").ap()
    p.emit("sync", lambda e: e.dma_start(
        out=xqf[:, :, :], in_=xq_h[0:128, :], transpose=True), writes=["xqf"], dma="xqtl")
    for t in range(NT3 + 1):
        tm1 = t - 1
        if 0 <= tm1 < NT3:
            hq_state["q"] = [(hch, tm1 % 2) for hch in range(3)]
            hq_state["done"] = 0
        if t < NT3:
            db = t % 2
            for ci in range(6):
                hch, sc_i = divmod(ci, 2)
                i0 = 1024 * hch + 512 * sc_i
                iw = min(512, I_FF - i0)
                gbk = ci % 2
                for kc in range(KC):
                    mm("B%d" % gbk, ps[gbk][:, 0:iw], xqf[:, kc, :], wqg[:, kc, i0:i0 + iw],
                       ["xqf", "wqg"], kc == 0, kc == KC - 1)
                for kc in range(KC):
                    mm("B%d" % (2 + gbk), ps[2 + gbk][:, 0:iw], xqf[:, kc, :], wqu[:, kc, i0:i0 + iw],
                       ["xqf", "wqu"], kc == 0, kc == KC - 1)
                p.emit("scalar", lambda e, i0=i0, iw=iw, gbk=gbk, db=db, ci=ci: e.activation(
                    out=hbuf[:, db, i0:i0 + iw], in_=ps[gbk][:, 0:iw], func=AF.Silu),
                    reads=["B%d" % gbk], writes=["hb%d%d" % (db, ci)])
                p.emit("scalar", lambda e, iw=iw, gbk=gbk, ci=ci: e.activation(
                    out=ubuf[:, ci % 2, 0:iw], in_=ps[2 + gbk][:, 0:iw], func=AF.Identity),
                    reads=["B%d" % (2 + gbk)], writes=["ub%d" % (ci % 2)])
                p.emit("gpsimd", lambda e, i0=i0, iw=iw, db=db, ci=ci: e.tensor_tensor(
                    out=hbuf[:, db, i0:i0 + iw], in0=hbuf[:, db, i0:i0 + iw],
                    in1=ubuf[:, ci % 2, 0:iw], op=OP.mult),
                    reads=["hb%d%d" % (db, ci), "ub%d" % (ci % 2)],
                    writes=["hb%d%d" % (db, ci)])
                hq_step(4)
            if DBG:
                p.emit("sync", lambda e, t=t, db=db: e.dma_start(
                    out=dbg_h[128 * t:128 * (t + 1), :], in_=hbuf[:, db, :]),
                    writes=["hb%d%d" % (db, c9) for c9 in range(6)], dma="dbgh")
        if 0 <= tm1 < NT3:
            n0 = 128 * tm1

            def down_block(lo, hi):
                for ic2 in range(lo, hi):
                    for dc in range(KC):
                        bk = 4 + dc % 2
                        q4 = dc // 2
                        mm("B%d" % bk, ps[bk][:, 128 * q4:128 * (q4 + 1)],
                           wqd[:, ic2, 128 * dc:128 * (dc + 1)], hqf[:, ic2, :],
                           ["wqd", "hqf%d" % min(ic2 // 8, 2)],
                           ic2 == 0 and q4 == 0, ic2 == IC - 1 and q4 == 3)

            hq_flush_until(1)
            down_block(0, 8)
            hq_flush_until(2)
            down_block(8, 16)
            hq_flush_until(3)
            if DBG:
                p.emit("sync", lambda e, tm1=tm1: e.dma_start(
                    out=dbg_hq.rearrange("(a p) n -> p a n", p=128)[:, :, 128 * tm1:128 * (tm1 + 1)],
                    in_=hqf[:, :, :]),
                    writes=["hqf0", "hqf1", "hqf2"], dma="dbghq")
            down_block(16, IC)
            for dc in range(KC):
                bk = 4 + dc % 2
                q4 = dc // 2
                p.emit("scalar", lambda e, bk=bk, q4=q4, dc=dc: e.activation(
                    out=sgsil[:, dc % 2, :], in_=ps[bk][:, 128 * q4:128 * (q4 + 1)],
                    func=AF.Silu),
                    reads=["B%d" % bk], writes=["sgs%d" % (dc % 2)])
                p.emit("vector", lambda e, dc=dc: e.tensor_tensor(
                    out=sgC[:, dc, :], in0=s1C[:, dc, :], in1=sgsil[:, dc % 2, :], op=OP.mult),
                    reads=["s1C0", "sgs%d" % (dc % 2)],
                    writes=["sgC%d" % dc, "bbq"])
            tp_batch2([([sgC[:, c, :] for c in range(c0, c0 + 4)],
                         sgtC[:, 128 * c0:128 * (c0 + 4)],
                         ["sgC%d" % c2 for c2 in range(c0, c0 + 4)],
                         ["sgtC"] if c0 == 0 else []) for c0 in (0, 4)])
            p.setw("sgtC", p.tok("vector"))
            if DBG:
                p.emit("sync", lambda e, n0=n0: e.dma_start(
                    out=dbg_sg[n0:n0 + 128, :], in_=sgtC[:, :]),
                    writes=["sgtC"], dma="dbgsg")
            p.emit("vector", lambda e: e.tensor_tensor(
                out=x1C[:, :], in0=x1C[:, :], in1=sgtC[:, :], op=OP.add),
                reads=["x1C0", "sgtC"], writes=["x1C0"])
            p.emit("sync", lambda e, n0=n0: e.dma_start(
                out=out_y[n0:n0 + 128, :], in_=x1C[:, :]), reads=["x1C0"], dma="yst")
        if t + 1 < NT3:
            p.emit("sync", lambda e, t=t: e.dma_start(
                out=xqf[:, :, :], in_=xq_h[128 * (t + 1):128 * (t + 2), :], transpose=True),
                writes=["xqf"], dma="xqtl")
        if t < NT3:
            n1 = 128 * t
            p.emit("gpsimd", lambda e, n1=n1: e.dma_start(
                out=s1C[:, :, :], in_=ap_fm(s1_h, 0, KC, n1, n1 + 128)),
                writes=["s1C0"], dma="s1C0")
            p.emit("sync", lambda e, n1=n1: e.dma_start(
                out=x1C[:, :], in_=x1t_h[n1:n1 + 128, :]), writes=["x1C0"], dma="x1C0")

    p.barrier()
    with nc.Block() as block:
        p.finalize(block)
    st.close()
    return nc




_NC_CACHE = {}


def _get_nc(T):
    if T not in _NC_CACHE:
        _NC_CACHE[T] = build(T)
    return _NC_CACHE[T]


def kernel(**inputs):
    x = np.ascontiguousarray(np.asarray(inputs["L0_attn_x"], dtype=np.float32))
    B, T, Cc = x.shape
    assert Cc == C
    nc = _get_nc(T)
    weights = {k: np.ascontiguousarray(np.asarray(v, dtype=np.float32))
               for k, v in inputs.items() if k != "L0_attn_x"}
    in_maps = []
    for b in range(B):
        m = dict(weights)
        m["x"] = x[b]
        in_maps.append(m)
    res = run_bass_kernel_spmd(nc, in_maps, core_ids=list(range(B)))
    return np.stack([np.asarray(r["y"]) for r in res.results], axis=0)
